# version 02
# baseline (speedup 1.0000x reference)
import numpy as np
import ml_dtypes
import concourse.bass as bass
import concourse.mybir as mybir
from concourse.bass_utils import run_bass_kernel_spmd
from concourse.alu_op_type import AluOpType as OP

F32 = mybir.dt.float32
BF16 = mybir.dt.bfloat16
AF = mybir.ActivationFunctionType
BFNP = ml_dtypes.bfloat16

T = 4096
D = 1024
NCORES = 8
MAGIC = float(1.5 * 2 ** 23)
PI = float(np.pi)

DEBUG_DUMP = set()


class Buf:
    __slots__ = ("t", "wr", "rd", "name")

    def __init__(self, ap, name=""):
        self.t = ap
        self.wr = None
        self.rd = []
        self.name = name

    def __getitem__(self, key):
        return self.t[key]


class Prog:
    ENG = ("tensor", "vector", "scalar", "gpsimd", "sync")

    def __init__(self, nc, stack):
        self.nc = nc
        self.q = {e: [] for e in self.ENG}
        self.sems = {}
        for name in ("T", "V", "A", "G", "DS", "DG", "DA",
                     "FS", "FG", "FA"):
            self.sems[name] = stack.enter_context(nc.semaphore("sem" + name))
        self.cnt = {k: 0 for k in self.sems}
        self.opidx = {e: 0 for e in self.ENG}
        self.val2idx = {k: {} for k in self.sems}
        self.waited = {}
        self.csem = {"tensor": "T", "vector": "V", "scalar": "A",
                     "gpsimd": "G", "sync": None}
        self.dsem = {"sync": "DS", "gpsimd": "DG", "scalar": "DA"}
        self.fsem = {"sync": "FS", "gpsimd": "FG", "scalar": "FA"}
        self.sem_eng = {"T": "tensor", "V": "vector", "A": "scalar",
                        "G": "gpsimd", "DS": "sync", "DG": "gpsimd",
                        "DA": "scalar", "FS": "sync", "FG": "gpsimd",
                        "FA": "scalar"}
        self.dma_issue = {"sync": 0, "gpsimd": 0, "scalar": 0}
        self.flushed_issue = {"sync": 0, "gpsimd": 0, "scalar": 0}

    def _flag_for(self, queue, idx):
        fname = self.fsem[queue]
        if self.flushed_issue[queue] < idx:
            dname = self.dsem[queue]
            need = 16 * self.dma_issue[queue]
            self.cnt[fname] += 1
            fval = self.cnt[fname]
            sems = self.sems

            def runner(eng, dn=dname, nv=need, fs=fname, fv=fval):
                eng.wait_ge(sems[dn], nv)
                eng.nop().then_inc(sems[fs], 1)

            self.q[queue].append(runner)
            self.flushed_issue[queue] = self.dma_issue[queue]
            self.opidx[queue] += 1
            self.val2idx[fname][fval] = self.opidx[queue] - 1
        return (fname, self.cnt[fname])

    def _emit(self, engine, fn, deps, dma):
        sem_name = self.dsem[engine] if dma else self.csem[engine]
        assert sem_name is not None, f"{engine} cannot emit dma={dma}"
        step = 16 if dma else 1
        deps2 = []
        for (dsname, dval) in deps:
            if dval is None:
                continue
            if dsname.startswith("Q"):
                queue = dsname[1:]
                if engine == queue:
                    deps2.append(self._flag_for(queue, dval))
                else:
                    deps2.append(self._flag_for(queue, dval))
            else:
                deps2.append((dsname, dval))
        deps = deps2
        myidx = self.opidx[engine]
        waits = []
        need_drain = False
        for (dsname, dval) in deps:
            if dval is None:
                continue
            if dsname == sem_name and not dma and self.sem_eng[dsname] == engine:
                pidx = self.val2idx[dsname].get(dval, -10)
                if pidx >= myidx - 1:
                    need_drain = True
                continue
            if self.sem_eng[dsname] == engine and not (dsname in ("DS", "DG", "DA")):
                pidx = self.val2idx[dsname].get(dval, -10)
                if pidx >= myidx - 1:
                    need_drain = True
                continue
            key = (engine, dsname)
            if self.waited.get(key, -1) >= dval:
                continue
            self.waited[key] = dval
            waits.append((dsname, dval))
        self.cnt[sem_name] += step
        if dma:
            self.dma_issue[engine] += 1
            token = ("Q" + engine, self.dma_issue[engine])
        else:
            token = (sem_name, self.cnt[sem_name])
        self.val2idx[sem_name][self.cnt[sem_name]] = myidx
        self.opidx[engine] += 1
        sems = self.sems

        def runner(eng, waits=waits, need_drain=need_drain, fn=fn,
                   sem=sems[sem_name], step=step):
            for (wn, wv) in waits:
                eng.wait_ge(sems[wn], wv)
            if need_drain and hasattr(eng, "drain"):
                eng.drain()
            ins = fn(eng)
            ins.then_inc(sem, step)

        self.q[engine].append(runner)
        return token

    def op(self, engine, fn, reads=(), writes=(), deps=(), dma=False):
        dl = list(deps)
        for b in reads:
            if b.wr is not None:
                dl.append(b.wr)
        for b in writes:
            if b.wr is not None:
                dl.append(b.wr)
            dl.extend(b.rd)
        tok = self._emit(engine, fn, dl, dma)
        for b in reads:
            b.rd.append(tok)
        for b in writes:
            b.wr = tok
            b.rd = []
        return tok

    def barrier(self):
        snap = dict(self.cnt)
        for e in self.ENG:
            waits = []
            for sn, sv in snap.items():
                if sv == 0:
                    continue
                if self.sem_eng[sn] == e and sn not in ("DS", "DG", "DA"):
                    continue
                key = (e, sn)
                if self.waited.get(key, -1) >= sv:
                    continue
                self.waited[key] = sv
                waits.append((sn, sv))
            if waits:
                sems = self.sems

                def runner(eng, waits=waits):
                    for (wn, wv) in waits:
                        eng.wait_ge(sems[wn], wv)

                self.q[e].append(runner)

    def finalize(self, block):
        for e in self.ENG:
            ops = self.q[e]

            def body(eng, ops=ops):
                for r in ops:
                    r(eng)

            getattr(block, e)(body)

    def mm(self, out_ap, lhsT, rhs, start, stop, reads=(), writes=(), deps=()):
        return self.op("tensor",
                       lambda t: t.matmul(out_ap, lhsT, rhs, start=start, stop=stop),
                       reads=reads, writes=writes, deps=deps)

    def transpose(self, out_ap, in_ap, ident_ap, reads=(), writes=(), deps=()):
        return self.op("tensor",
                       lambda t: t.transpose(out_ap, in_ap, ident_ap),
                       reads=reads, writes=writes, deps=deps)

    def cp(self, out_ap, in_ap, reads=(), writes=(), deps=(), eng="vector"):
        if eng == "scalar":
            return self.op(eng, lambda s: s.copy(out_ap, in_ap),
                           reads=reads, writes=writes, deps=deps)
        return self.op(eng, lambda v: v.tensor_copy(out_ap, in_ap),
                       reads=reads, writes=writes, deps=deps)

    def tt(self, out_ap, a, b, opk, reads=(), writes=(), deps=(), eng="vector"):
        return self.op(eng, lambda v: v.tensor_tensor(out_ap, a, b, opk),
                       reads=reads, writes=writes, deps=deps)

    def ts(self, out_ap, in_ap, s1, s2, op0, op1=None, reads=(), writes=(),
           deps=(), eng="vector"):
        if op1 is None:
            return self.op(eng, lambda v: v.tensor_scalar(out_ap, in_ap, s1, None, op0),
                           reads=reads, writes=writes, deps=deps)
        return self.op(eng,
                       lambda v: v.tensor_scalar(out_ap, in_ap, s1, s2, op0, op1),
                       reads=reads, writes=writes, deps=deps)

    def stt(self, out_ap, in0, scalar, in1, op0, op1, reads=(), writes=(),
            deps=(), eng="vector"):
        return self.op(eng,
                       lambda v: v.scalar_tensor_tensor(out_ap, in0, scalar, in1, op0, op1),
                       reads=reads, writes=writes, deps=deps)

    def act(self, out_ap, in_ap, func, bias=None, scale=None, reads=(),
            writes=(), deps=()):
        kw = {}
        if bias is not None:
            kw["bias"] = bias
        if scale is not None:
            kw["scale"] = scale
        return self.op("scalar",
                       lambda s: s.activation(out_ap, in_ap, func, **kw),
                       reads=reads, writes=writes, deps=deps)

    def scan(self, out_ap, d0, d1, init, reads=(), writes=(), deps=()):
        return self.op("vector",
                       lambda v: v.tensor_tensor_scan(out_ap, d0, d1, init,
                                                      OP.mult, OP.add),
                       reads=reads, writes=writes, deps=deps)

    def red_max(self, out_ap, in_ap, reads=(), writes=(), deps=()):
        return self.op("vector",
                       lambda v: v.reduce_max(out_ap, in_ap,
                                              axis=mybir.AxisListType.X),
                       reads=reads, writes=writes, deps=deps)

    def recip(self, out_ap, in_ap, reads=(), writes=(), deps=()):
        return self.op("vector", lambda v: v.reciprocal(out_ap, in_ap),
                       reads=reads, writes=writes, deps=deps)

    def memset(self, ap, val, reads=(), writes=(), deps=(), eng="vector"):
        return self.op(eng, lambda v: v.memset(ap, val),
                       reads=reads, writes=writes, deps=deps)

    def dma(self, queue, out_ap, in_ap, reads=(), writes=(), deps=()):
        return self.op(queue, lambda q: q.dma_start(out=out_ap, in_=in_ap),
                       reads=reads, writes=writes, deps=deps, dma=True)


class Arena:

    def __init__(self, ap, words):
        self.ap = ap
        self.words = words
        self.off = 0
        self.marks = []

    def alloc(self, words, dtype=F32, parts=128, name=""):
        w32 = words if dtype == F32 else (words + 1) // 2
        w32 = (w32 + 1) & ~1
        assert self.off + w32 <= self.words, \
            f"arena overflow: {self.off}+{w32} > {self.words} ({name})"
        sl = self.ap[0:parts, self.off:self.off + w32]
        self.off += w32
        if dtype != F32:
            sl = sl.bitcast(dtype)
        return Buf(sl, name)

    def push(self):
        self.marks.append(self.off)

    def pop(self):
        self.off = self.marks.pop()


def prep_inputs(ws):
    f32 = lambda a: np.ascontiguousarray(a, dtype=np.float32)
    sh = {}
    sh["c_eye128"] = np.eye(128, dtype=np.float32)
    sh["c_eye64"] = np.eye(64, dtype=np.float32)
    sh["c_eye16"] = np.eye(16, dtype=np.float32)
    sh["c_eye32"] = np.eye(32, dtype=np.float32)
    sh["c_eye8"] = np.eye(8, dtype=np.float32)
    sh["c_ones_col"] = np.ones((128, 1), dtype=np.float32)
    sh["c_ones_row"] = np.ones((1, 128), dtype=np.float32)
    q = np.arange(128)
    sh["c_utri"] = np.where(q[:, None] <= q[None, :], 0.0, -1e5).astype(np.float32)
    sh["c_strict"] = (q[:, None] < q[None, :]).astype(np.float32)
    sh["w_l0_inproj"] = f32(ws["L0_attn_in_proj_w"])
    sh["w_l0_outproj"] = f32(ws["L0_attn_out_proj_w"])
    sh["w_l0_theta"] = f32(ws["L0_attn_theta_w"])
    sh["v_l0_ln_s"] = f32(ws["L0_attn_norm_scale"]).reshape(8, 128).T.copy()
    sh["v_l0_ln_b"] = f32(ws["L0_attn_norm_bias"]).reshape(8, 128).T.copy()
    sh["v_l0_alog"] = f32(ws["L0_attn_A_log"]).reshape(16, 1)
    sh["v_l0_dtb"] = f32(ws["L0_attn_dt_bias"]).reshape(16, 1)
    sh["v_l0_bcB"] = f32(ws["L0_attn_bcB_scale"]).reshape(64, 1)
    sh["v_l0_bcC"] = f32(ws["L0_attn_bcC_scale"]).reshape(64, 1)
    sh["v_l0_Bb"] = f32(ws["L0_attn_B_bias"]).reshape(64, 1)
    sh["v_l0_Cb"] = f32(ws["L0_attn_C_bias"]).reshape(64, 1)
    sh["v_l0_D"] = f32(ws["L0_attn_D"]).reshape(1, 16)
    sh["w_l0f_gate"] = f32(ws["L0_ffn_w_gate"])
    sh["w_l0f_Wd"] = f32(ws["L0_ffn_Wd"])
    sh["w_l0f_Wu"] = f32(ws["L0_ffn_Wu"])
    sh["v_l0f_bd"] = f32(ws["L0_ffn_bd"]).T.copy()
    sh["w_l0f_bu"] = f32(ws["L0_ffn_bu"])
    sh["w_l1_inproj"] = f32(ws["L1_attn_in_proj_w"].T)
    sh["w_l1_xproj"] = f32(ws["L1_attn_x_proj_w"].T)
    sh["w_l1_dtproj"] = f32(ws["L1_attn_dt_proj_w"].T)
    sh["w_l1_outproj"] = f32(ws["L1_attn_out_proj_w"].T)
    cw = f32(ws["L1_attn_conv_w"]).reshape(2048, 4)
    sh["v_l1_convw"] = cw.reshape(16, 128, 4).transpose(1, 0, 2).reshape(128, 64).copy()
    sh["v_l1_convb"] = f32(ws["L1_attn_conv_b"]).reshape(16, 128).T.copy()
    sh["v_l1_dtpb"] = f32(ws["L1_attn_dt_proj_b"]).reshape(16, 128).T.copy()
    sh["w_l1_alog"] = f32(ws["L1_attn_A_log"])
    sh["w_l1f_rw1"] = f32(ws["L1_ffn_rw1"])
    sh["w_l1f_rw2"] = f32(ws["L1_ffn_rw2"])
    sh["v_l1f_rb1"] = f32(ws["L1_ffn_rb1"]).reshape(32, 128).T.copy()
    sh["v_l1f_rb2"] = f32(ws["L1_ffn_rb2"]).reshape(1, 8)
    sh["w_l1f_w1"] = f32(ws["L1_ffn_w1"])
    sh["w_l1f_w2"] = f32(ws["L1_ffn_w2"])
    sh["v_l1f_b1"] = f32(ws["L1_ffn_b1"]).T.reshape(32, 128, 8).transpose(1, 0, 2).reshape(128, 256).copy()
    sh["w_l1f_b2"] = f32(ws["L1_ffn_b2"])

    u = np.asarray(ws["L0_attn_u"], dtype=np.float32)

    def core_map(b):
        m = dict(sh)
        m["u_fm"] = np.ascontiguousarray(u[b].T)
        return m

    return core_map


def build_program():
    import contextlib
    nc = bass.Bass("TRN2", target_bir_lowering=False)
    stack = contextlib.ExitStack()

    def ext_in(name, shape, dt=F32):
        return nc.dram_tensor(name, shape, dt, kind="ExternalInput").ap()

    def scratch(name, shape, dt=F32):
        kind = "ExternalOutput" if name in DEBUG_DUMP else "Internal"
        return nc.dram_tensor(name, shape, dt, kind=kind).ap()

    u_fm = ext_in("u_fm", [D, T])
    out_fm = nc.dram_tensor("out_fm", [D, T], F32, kind="ExternalOutput").ap()

    g = {}
    for name, shape in [
        ("c_eye128", [128, 128]), ("c_eye64", [64, 64]), ("c_eye16", [16, 16]),
        ("c_eye32", [32, 32]), ("c_eye8", [8, 8]), ("c_ones_col", [128, 1]), ("c_ones_row", [1, 128]),
        ("c_utri", [128, 128]), ("c_strict", [128, 128]),
        ("w_l0_inproj", [1024, 4256]), ("w_l0_outproj", [2048, 1024]),
        ("w_l0_theta", [1024, 32]),
        ("v_l0_ln_s", [128, 8]), ("v_l0_ln_b", [128, 8]),
        ("v_l0_alog", [16, 1]), ("v_l0_dtb", [16, 1]),
        ("v_l0_bcB", [64, 1]), ("v_l0_bcC", [64, 1]),
        ("v_l0_Bb", [64, 1]), ("v_l0_Cb", [64, 1]), ("v_l0_D", [1, 16]),
        ("w_l0f_gate", [1024, 8]), ("w_l0f_Wd", [8, 1024, 64]),
        ("w_l0f_Wu", [8, 64, 1024]), ("v_l0f_bd", [64, 8]), ("w_l0f_bu", [8, 1024]),
        ("w_l1_inproj", [1024, 4096]), ("w_l1_xproj", [2048, 128]),
        ("w_l1_dtproj", [2048, 2048]), ("w_l1_outproj", [2048, 1024]),
        ("v_l1_convw", [128, 64]), ("v_l1_convb", [128, 16]),
        ("v_l1_dtpb", [128, 16]), ("w_l1_alog", [2048, 64]),
        ("w_l1f_rw1", [1024, 4096]), ("w_l1f_rw2", [4096, 8]),
        ("v_l1f_rb1", [128, 32]), ("v_l1f_rb2", [1, 8]),
        ("w_l1f_w1", [8, 1024, 4096]), ("w_l1f_w2", [8, 4096, 1024]),
        ("v_l1f_b1", [128, 256]), ("w_l1f_b2", [8, 1024]),
    ]:
        g[name] = ext_in(name, shape)

    x1_fm = scratch("x1_fm", [D, T])
    x2_fm = scratch("x2_fm", [D, T])
    x3_fm = scratch("x3_fm", [D, T])
    xin_bf = scratch("xin_bf", [D, T], BF16)
    xu_fm = scratch("xu_fm", [D, T])
    y0_bf = scratch("y0_bf", [2048, T], BF16)
    xs_fm = scratch("xs_fm", [2048, T])
    res_fm = scratch("res_fm", [2048, T])
    delta_fm = scratch("delta_fm", [2048, T])
    dxd_fm = scratch("dxd_fm", [2048, T])
    yg_bf = scratch("yg_bf", [2048, T], BF16)
    xf_bfs = scratch("xf_bfs", [D, T], BF16)
    oparts = scratch("oparts", [8, D, T])
    we_sc = scratch("we_sc", [8, T])

    ARENA_WORDS = 43008
    arena_t = stack.enter_context(nc.sbuf_tensor("arena", [128, ARENA_WORDS], F32))
    psum = [stack.enter_context(nc.psum_tensor(f"pb{i}", [128, 512], F32))
            for i in range(8)]
    pb = [Buf(p[:], f"pb{i}") for i, p in enumerate(psum)]

    P = Prog(nc, stack)
    A = Arena(arena_t[:], ARENA_WORDS)

    C = {}
    CONST_SHAPES = {
        "c_eye128": [128, 128], "c_eye64": [64, 64], "c_eye16": [16, 16],
        "c_eye32": [32, 32], "c_eye8": [8, 8],
        "c_ones_col": [128, 1], "c_ones_row": [1, 128],
        "c_utri": [128, 128], "c_strict": [128, 128],
        "v_l0_ln_s": [128, 8], "v_l0_ln_b": [128, 8],
        "v_l0_alog": [16, 1], "v_l0_dtb": [16, 1],
        "v_l0_bcB": [64, 1], "v_l0_bcC": [64, 1],
        "v_l0_Bb": [64, 1], "v_l0_Cb": [64, 1], "v_l0_D": [1, 16],
        "v_l0f_bd": [64, 8],
        "v_l1_convw": [128, 64], "v_l1_convb": [128, 16],
        "v_l1_dtpb": [128, 16],
        "v_l1f_rb1": [128, 32], "v_l1f_rb2": [1, 8],
        "v_l1f_b1": [128, 256],
    }

    def ld_consts(names):
        for name in names:
            shape = CONST_SHAPES[name]
            b = A.alloc(shape[1], F32, name=name)
            C[name] = Buf(b.t[0:shape[0], 0:shape[1]], name)
            P.dma("sync", C[name].t, g[name], writes=[C[name]])

    ld_consts(["c_eye128", "c_eye64", "c_eye16", "c_eye32", "c_eye8",
               "c_ones_col", "c_ones_row"])
    ln_rows = A.alloc(1536, name="ln_rows")

    EYE = C["c_eye128"]
    EYE64 = C["c_eye64"]
    EYE16 = C["c_eye16"]
    EYE32 = C["c_eye32"]
    EYE8 = C["c_eye8"]
    ONESC = C["c_ones_col"]
    ONESR = C["c_ones_row"]

    def ln_feature_major(xbufs, xaps, tc, eps, tmp, rows2, out_cb,
                         nchunks=8, dsize=1024.0):
        bs, br = rows2
        for i in range(nchunks):
            P.mm(bs.t[0:1, 0:tc], ONESC.t, xaps[i], start=(i == 0),
                 stop=(i == nchunks - 1), reads=[xbufs[i], ONESC], writes=[bs])
        for i in range(nchunks):
            P.act(tmp.t[:, 0:tc], xaps[i], AF.Square, reads=[xbufs[i]],
                  writes=[tmp])
            P.mm(br.t[0:1, 0:tc], ONESC.t, tmp.t[:, 0:tc], start=(i == 0),
                 stop=(i == nchunks - 1), reads=[tmp, ONESC], writes=[br])
        mrow = ln_rows.t[0:1, 0:tc]
        rrow = ln_rows.t[0:1, 512:512 + tc]
        t1 = ln_rows.t[0:1, 1024:1024 + tc]
        P.ts(mrow, bs.t[0:1, 0:tc], 1.0 / dsize, None, OP.mult,
             reads=[bs], writes=[ln_rows])
        P.ts(rrow, br.t[0:1, 0:tc], 1.0 / dsize, None, OP.mult,
             reads=[br], writes=[ln_rows])
        P.tt(t1, mrow, mrow, OP.mult, reads=[ln_rows], writes=[ln_rows])
        P.tt(rrow, rrow, t1, OP.subtract, reads=[ln_rows], writes=[ln_rows])
        P.ts(rrow, rrow, eps, None, OP.add, reads=[ln_rows], writes=[ln_rows])
        P.recip(rrow, rrow, reads=[ln_rows], writes=[ln_rows])
        P.act(rrow, rrow, AF.Sqrt, reads=[ln_rows], writes=[ln_rows])
        P.mm(bs.t[:, 0:tc], ONESR.t, mrow, start=True, stop=True,
             reads=[ln_rows, ONESR], writes=[bs])
        P.mm(br.t[:, 0:tc], ONESR.t, rrow, start=True, stop=True,
             reads=[ln_rows, ONESR], writes=[br])
        for i in range(nchunks):
            out_cb(i, bs, br)

    NCH = T // 128

    A.push()
    ld_consts(["c_utri", "c_strict", "v_l0_ln_s", "v_l0_ln_b", "v_l0_alog",
               "v_l0_dtb", "v_l0_bcB", "v_l0_bcC", "v_l0_Bb", "v_l0_Cb",
               "v_l0_D"])
    UTRI = C["c_utri"]
    STRICT = C["c_strict"]
    A16 = A.alloc(2, F32, name="A16")
    A16 = Buf(A16.t[0:16, 0:1], "A16")
    P.act(A16.t, C["v_l0_alog"].t, AF.Exp, reads=[C["v_l0_alog"]], writes=[A16])
    P.ts(A16.t, A16.t, -1.0, None, OP.mult, reads=[A16], writes=[A16])
    w0 = A.alloc(T, name="cc_fm")
    ap_cc = w0.t[0:16, :]
    wex = A.alloc(T, name="expcc_fm")
    ap_expcc = wex.t[0:16, :]
    wB1 = A.alloc(T, name="B1_fm")
    ap_B1 = wB1.t[0:64, :]
    w1 = A.alloc(T, name="C1_fm")
    ap_C1 = w1.t[0:64, :]
    expic0 = A.alloc(32, name="expic0")
    ap_expic0 = expic0.t[0:16, 0:32]
    colsT = {nm: A.alloc(NCH * 16, name="colsT_" + nm)
             for nm in ("negcc", "u", "w", "exprc")}
    B_tm = A.alloc(NCH * 64, name="B_tm")
    DI = A.alloc(16 * 128, BF16, name="DI")
    phic = A.alloc(2, name="phic")
    ap_phic = phic.t[0:32, 0:1]
    P.memset(ap_phic, 0.0, writes=[phic])

    for h in range(16):
        P.mm(pb[6].t[:, 0:1], ONESR.t, C["v_l0_D"].t[0:1, h:h + 1], start=True,
             stop=True, reads=[C["v_l0_D"], ONESR], writes=[pb[6]])
        P.ts(DI.t[:, h * 128:(h + 1) * 128], EYE.t, pb[6].t[:, 0:1],
             None, OP.mult, reads=[EYE, pb[6]], writes=[DI])

    TSL = 256
    NSL = T // TSL
    A.push()
    x8 = [A.alloc(TSL, name=f"x8_{i}") for i in range(8)]
    u8 = [A.alloc(TSL, name=f"u8_{i}") for i in range(8)]
    tmpc = A.alloc(TSL, name="tmpc")
    xin8 = [A.alloc(TSL, BF16, name=f"xin_{i}") for i in range(8)]
    wsm = [A.alloc(160, BF16, name=f"wsm{k}") for k in range(8)]
    wth = [A.alloc(32, name=f"wth{k}") for k in range(8)]
    for k in range(8):
        P.dma("gpsimd", wsm[k].t, g["w_l0_inproj"][k * 128:(k + 1) * 128, 4096:4256],
              writes=[wsm[k]])
        P.dma("sync", wth[k].t[:, 0:32], g["w_l0_theta"][k * 128:(k + 1) * 128, :],
              writes=[wth[k]])
    theta_sl = A.alloc(TSL, name="theta_sl")
    ap_theta = theta_sl.t[0:32, :]
    braw = A.alloc(TSL, name="braw"); ap_braw = braw.t[0:64, :]
    craw = A.alloc(TSL, name="craw"); ap_craw = craw.t[0:64, :]
    bB0 = A.alloc(TSL, name="B0"); ap_B0 = bB0.t[0:64, :]
    bC0 = A.alloc(TSL, name="C0"); ap_C0 = bC0.t[0:64, :]
    sqbc = A.alloc(TSL, name="sqbc"); ap_sq = sqbc.t[0:64, :]
    bdt = [A.alloc(TSL, name=f"dt{j}") for j in range(2)]
    ap_dt = [bdt[0].t[0:16, :], bdt[1].t[0:16, :]]
    bdtl1 = [A.alloc(TSL, name=f"dtl1{j}") for j in range(2)]
    ap_dtl1 = [bdtl1[0].t[0:16, :], bdtl1[1].t[0:16, :]]
    blam = A.alloc(TSL, name="lam"); ap_lam = blam.t[0:16, :]
    buu = A.alloc(TSL, name="uu"); ap_uu = buu.t[0:16, :]
    bww = A.alloc(TSL, name="ww"); ap_ww = bww.t[0:16, :]
    badt = A.alloc(TSL, name="adt"); ap_adt = badt.t[0:16, :]
    bsp = [A.alloc(TSL, name=f"sp{j}") for j in range(3)]
    ap_sp = [b.t[0:16, :] for b in bsp]
    bphi = A.alloc(TSL, name="phi"); ap_phi = bphi.t[0:32, :]
    bq = A.alloc(TSL, name="q"); ap_q = bq.t[0:32, :]
    trg = A.alloc(192, name="trg")
    tg_phi = trg.t[:, 0:32]
    tg_q = trg.t[:, 32:64]
    tg_r = trg.t[:, 64:96]
    tg_sin = trg.t[:, 96:128]
    tg_cos = trg.t[:, 128:160]
    btm = A.alloc(64, name="btm")
    ctm = A.alloc(64, name="ctm")
    c1tm = A.alloc(64, name="c1tm")
    tp1 = A.alloc(32, name="tp1")
    tp2 = A.alloc(32, name="tp2")
    bch1 = A.alloc(128, name="ch1"); ap_ch1 = bch1.t[0:16, 0:128]
    bch2 = A.alloc(128, name="ch2"); ap_ch2 = bch2.t[0:16, 0:128]
    ones16 = A.alloc(128, name="ones16")
    ap_o16 = ones16.t[0:16, 0:128]
    P.memset(ap_o16, 1.0, writes=[ones16])
    ones32 = A.alloc(TSL, name="ones32")
    ap_o32 = ones32.t[0:32, :]
    P.memset(ap_o32, 1.0, writes=[ones32])
    c116 = A.alloc(2, name="c116")
    ap_c116 = c116.t[0:16, 0:1]
    P.memset(ap_c116, 1.0 / 16.0, writes=[c116])
    o64c = A.alloc(2, name="o64c")
    ap_o64 = o64c.t[0:64, 0:1]
    P.memset(ap_o64, 1.0, writes=[o64c])
    NCPS = TSL // 128

    def emit_ww(src_buf, src_ap, nxt, sl_of):
        P.cp(ap_ww[:, 0:TSL - 1], src_ap[:, 1:TSL], reads=[src_buf], writes=[bww])
        if nxt is None:
            P.memset(ap_ww[:, TSL - 1:TSL], 0.0, writes=[bww])
        else:
            nb_, na_ = nxt
            P.cp(ap_ww[:, TSL - 1:TSL], na_[:, 0:1], reads=[nb_], writes=[bww])
        for c in range(NCPS):
            ch = sl_of * NCPS + c
            P.transpose(pb[4].t[0:128, 0:16], ap_ww[:, c * 128:(c + 1) * 128],
                        EYE16.t, reads=[bww, EYE16], writes=[pb[4]])
            P.cp(colsT["w"].t[:, ch * 16:(ch + 1) * 16], pb[4].t[0:128, 0:16],
                 reads=[pb[4]], writes=[colsT["w"]])

    prev = {"buf": None, "ap": None, "idx": None}
    for sl in range(NSL):
        t0 = sl * TSL
        cur = sl % 2
        for i in range(8):
            P.dma("sync", x8[i].t, u_fm[i * 128:(i + 1) * 128, t0:t0 + TSL],
                  writes=[x8[i]])

        def out1(i, mB, rB):
            P.tt(u8[i].t, x8[i].t, mB.t[:, 0:TSL], OP.subtract,
                 reads=[x8[i], mB], writes=[u8[i]])
            P.tt(u8[i].t, u8[i].t, rB.t[:, 0:TSL], OP.mult,
                 reads=[u8[i], rB], writes=[u8[i]])
        ln_feature_major(x8, [b.t for b in x8], TSL, 1e-5, tmpc,
                         (pb[0], pb[1]), out1)
        for i in range(8):
            P.tt(tmpc.t, x8[i].t, u8[i].t, OP.add, reads=[x8[i], u8[i]],
                 writes=[tmpc])
            P.dma("scalar", xu_fm[i * 128:(i + 1) * 128, t0:t0 + TSL], tmpc.t,
                  reads=[tmpc])

        def out2(i, mB, rB):
            P.tt(x8[i].t, u8[i].t, mB.t[:, 0:TSL], OP.subtract,
                 reads=[u8[i], mB], writes=[x8[i]])
            P.tt(x8[i].t, x8[i].t, rB.t[:, 0:TSL], OP.mult,
                 reads=[x8[i], rB], writes=[x8[i]])
            P.stt(xin8[i].t, x8[i].t, C["v_l0_ln_s"].t[:, i:i + 1],
                  C["v_l0_ln_b"].t[:, i:i + 1].broadcast_to([128, TSL]),
                  OP.mult, OP.add, reads=[x8[i], C["v_l0_ln_s"], C["v_l0_ln_b"]],
                  writes=[xin8[i]])
            P.dma("scalar", xin_bf[i * 128:(i + 1) * 128, t0:t0 + TSL],
                  xin8[i].t, reads=[xin8[i]])
        ln_feature_major(u8, [b.t for b in u8], TSL, 1e-6, tmpc,
                         (pb[0], pb[1]), out2)
        for k in range(8):
            P.mm(pb[2].t[0:32, 0:TSL], wth[k].t[:, 0:32], u8[k].t,
                 start=(k == 0), stop=(k == 7),
                 reads=[wth[k], u8[k]], writes=[pb[2]])
        P.cp(ap_theta, pb[2].t[0:32, 0:TSL], reads=[pb[2]], writes=[theta_sl])
        for (mo, msz) in [(0, 64), (64, 64), (128, 16), (144, 16)]:
            bk = pb[3]
            for k in range(8):
                P.mm(bk.t[0:msz, 0:TSL], wsm[k].t[:, mo:mo + msz], xin8[k].t,
                     start=(k == 0), stop=(k == 7),
                     reads=[wsm[k], xin8[k]], writes=[bk])
            if mo == 0:
                P.cp(ap_braw, bk.t[0:64, 0:TSL], reads=[bk], writes=[braw])
            elif mo == 64:
                P.cp(ap_craw, bk.t[0:64, 0:TSL], reads=[bk], writes=[craw])
            elif mo == 128:
                P.stt(ap_dt[cur], bk.t[0:16, 0:TSL], 1.0,
                      C["v_l0_dtb"].t.broadcast_to([16, TSL]), OP.mult, OP.add,
                      reads=[bk, C["v_l0_dtb"]], writes=[bdt[cur]])
                P.act(ap_sp[0], ap_dt[cur], AF.Abs, reads=[bdt[cur]],
                      writes=[bsp[0]])
                P.act(ap_sp[1], ap_sp[0], AF.Exp, scale=-1.0,
                      reads=[bsp[0]], writes=[bsp[1]])
                P.act(ap_sp[1], ap_sp[1], AF.Ln, bias=1.0,
                      reads=[bsp[1]], writes=[bsp[1]])
                P.op("vector", lambda v, o=ap_sp[2], a=ap_dt[cur]:
                     v.tensor_scalar_max(o, a, 0.0),
                     reads=[bdt[cur]], writes=[bsp[2]])
                P.tt(ap_dt[cur], ap_sp[1], ap_sp[2], OP.add,
                     reads=[bsp[1], bsp[2]], writes=[bdt[cur]])
            else:
                P.act(ap_lam, bk.t[0:16, 0:TSL], AF.Sigmoid,
                      reads=[bk], writes=[blam])
        P.tt(ap_uu, ap_dt[cur], ap_lam, OP.mult, reads=[bdt[cur], blam],
             writes=[buu])
        P.ts(ap_adt, ap_dt[cur], A16.t, None, OP.mult, reads=[bdt[cur], A16],
             writes=[badt])
        P.ts(ap_sp[0], ap_lam, -1.0, 1.0, OP.mult, OP.add,
             reads=[blam], writes=[bsp[0]])
        P.tt(ap_dtl1[cur], ap_dt[cur], ap_sp[0], OP.mult,
             reads=[bdt[cur], bsp[0]], writes=[bdtl1[cur]])
        if prev["buf"] is not None:
            emit_ww(prev["buf"], prev["ap"], (bdtl1[cur], ap_dtl1[cur]),
                    prev["idx"])
        prev["buf"] = bdtl1[cur]
        prev["ap"] = ap_dtl1[cur]
        prev["idx"] = sl

        for c in range(NCPS):
            ch = sl * NCPS + c
            csl = slice(c * 128, (c + 1) * 128)
            P.scan(ap_cc[:, ch * 128:(ch + 1) * 128], ap_o16,
                   ap_adt[:, csl], 0.0, reads=[ones16, badt], writes=[w0])
            P.scan(ap_ch1[:, ::-1], ap_o16, ap_adt[:, csl][:, ::-1], 0.0,
                   reads=[ones16, badt], writes=[bch1])
            P.tt(ap_ch2, ap_ch1, ap_adt[:, csl], OP.subtract,
                 reads=[bch1, badt], writes=[bch2])
            P.act(ap_expic0[:, ch:ch + 1], ap_ch1[:, 0:1], AF.Exp,
                  reads=[bch1], writes=[expic0])
            P.act(ap_expcc[:, ch * 128:(ch + 1) * 128],
                  ap_cc[:, ch * 128:(ch + 1) * 128], AF.Exp,
                  reads=[w0], writes=[wex])
            P.act(ap_ch1, ap_ch2, AF.Exp, reads=[bch2], writes=[bch1])
            P.transpose(pb[4].t[0:128, 0:16], ap_ch1, EYE16.t,
                        reads=[bch1, EYE16], writes=[pb[4]])
            P.cp(colsT["exprc"].t[:, ch * 16:(ch + 1) * 16],
                 pb[4].t[0:128, 0:16], reads=[pb[4]], writes=[colsT["exprc"]])
            P.ts(ap_ch1, ap_cc[:, ch * 128:(ch + 1) * 128], -1.0, None,
                 OP.mult, reads=[w0], writes=[bch1])
            P.transpose(pb[4].t[0:128, 0:16], ap_ch1, EYE16.t,
                        reads=[bch1, EYE16], writes=[pb[4]])
            P.cp(colsT["negcc"].t[:, ch * 16:(ch + 1) * 16],
                 pb[4].t[0:128, 0:16], reads=[pb[4]], writes=[colsT["negcc"]])
            P.transpose(pb[4].t[0:128, 0:16], ap_uu[:, csl], EYE16.t,
                        reads=[buu, EYE16], writes=[pb[4]])
            P.cp(colsT["u"].t[:, ch * 16:(ch + 1) * 16], pb[4].t[0:128, 0:16],
                 reads=[pb[4]], writes=[colsT["u"]])

        def rms(raw_ap, raw_buf, scale_c, bias_c, out_ap, out_buf):
            P.act(ap_sq, raw_ap, AF.Square, reads=[raw_buf], writes=[sqbc])
            P.mm(pb[5].t[0:1, 0:TSL], ap_o64, ap_sq, start=True, stop=True,
                 reads=[sqbc, o64c], writes=[pb[5]])
            rr = tmpc.t[0:1, 0:TSL]
            P.ts(rr, pb[5].t[0:1, 0:TSL], 1.0 / 64.0, 1e-6, OP.mult, OP.add,
                 reads=[pb[5]], writes=[tmpc])
            P.recip(rr, rr, reads=[tmpc], writes=[tmpc])
            P.act(rr, rr, AF.Sqrt, reads=[tmpc], writes=[tmpc])
            P.mm(pb[5].t[0:64, 0:TSL], ONESR.t[:, 0:64], rr, start=True,
                 stop=True, reads=[tmpc, ONESR], writes=[pb[5]])
            P.tt(out_ap, raw_ap, pb[5].t[0:64, 0:TSL], OP.mult,
                 reads=[raw_buf, pb[5]], writes=[out_buf])
            P.stt(out_ap, out_ap, scale_c.t, bias_c.t.broadcast_to([64, TSL]),
                  OP.mult, OP.add, reads=[out_buf, scale_c, bias_c],
                  writes=[out_buf])
        rms(ap_braw, braw, C["v_l0_bcB"], C["v_l0_Bb"], ap_B0, bB0)
        rms(ap_craw, craw, C["v_l0_bcC"], C["v_l0_Cb"], ap_C0, bC0)
        P.mm(pb[5].t[0:1, 0:TSL], ap_c116, ap_dt[cur], start=True, stop=True,
             reads=[bdt[cur], c116], writes=[pb[5]])
        P.cp(tmpc.t[0:1, 0:TSL], pb[5].t[0:1, 0:TSL], reads=[pb[5]],
             writes=[tmpc])
        P.mm(pb[5].t[0:32, 0:TSL], ONESR.t[:, 0:32], tmpc.t[0:1, 0:TSL],
             start=True, stop=True, reads=[tmpc, ONESR], writes=[pb[5]])
        P.tt(ap_q, ap_theta, pb[5].t[0:32, 0:TSL], OP.mult,
             reads=[theta_sl, pb[5]], writes=[bq])
        P.scan(ap_phi, ap_o32, ap_q, ap_phic,
               reads=[ones32, bq, phic], writes=[bphi])
        P.cp(ap_phic, ap_phi[:, TSL - 1:TSL], reads=[bphi], writes=[phic])
        for c in range(NCPS):
            ch = sl * NCPS + c
            csl = slice(c * 128, (c + 1) * 128)
            chsl = slice(ch * 128, (ch + 1) * 128)
            P.transpose(pb[4].t[0:128, 0:32], ap_phi[:, csl], EYE32.t,
                        reads=[bphi, EYE32], writes=[pb[4]])
            P.cp(tg_phi, pb[4].t[0:128, 0:32], reads=[pb[4]], writes=[trg])
            P.ts(tg_q, tg_phi, 1.0 / (2 * PI), MAGIC, OP.mult, OP.add,
                 reads=[trg], writes=[trg])
            P.ts(tg_q, tg_q, -MAGIC, None, OP.add, reads=[trg], writes=[trg])
            P.stt(tg_r, tg_q, -2 * PI, tg_phi, OP.mult, OP.add,
                  reads=[trg], writes=[trg])
            P.act(tg_sin, tg_r, AF.Sin, reads=[trg], writes=[trg])
            P.ts(tg_q, tg_phi, 1.0 / (2 * PI), MAGIC + 0.25, OP.mult, OP.add,
                 reads=[trg], writes=[trg])
            P.ts(tg_q, tg_q, -MAGIC, None, OP.add, reads=[trg], writes=[trg])
            P.stt(tg_r, tg_q, -2 * PI, tg_phi, OP.mult, OP.add,
                  reads=[trg], writes=[trg])
            P.ts(tg_r, tg_r, PI / 2, None, OP.add, reads=[trg], writes=[trg])
            P.act(tg_cos, tg_r, AF.Sin, reads=[trg], writes=[trg])
            P.transpose(pb[4].t[0:128, 0:64], ap_B0[:, csl], EYE64.t,
                        reads=[bB0, EYE64], writes=[pb[4]])
            P.cp(btm.t, pb[4].t[0:128, 0:64], reads=[pb[4]], writes=[btm])
            P.transpose(pb[4].t[0:128, 0:64], ap_C0[:, csl], EYE64.t,
                        reads=[bC0, EYE64], writes=[pb[4]])
            P.cp(ctm.t, pb[4].t[0:128, 0:64], reads=[pb[4]], writes=[ctm])

            def rope_tm(src_b, dst_ap, dst_b):
                P.tt(tp1.t[:, 0:32], src_b.t[:, 0:32], tg_cos, OP.mult,
                     reads=[src_b, trg], writes=[tp1])
                P.tt(tp2.t[:, 0:32], src_b.t[:, 32:64], tg_sin, OP.mult,
                     reads=[src_b, trg], writes=[tp2])
                P.tt(dst_ap[:, 0:32], tp1.t[:, 0:32], tp2.t[:, 0:32],
                     OP.subtract, reads=[tp1, tp2], writes=[dst_b])
                P.tt(tp1.t[:, 0:32], src_b.t[:, 0:32], tg_sin, OP.mult,
                     reads=[src_b, trg], writes=[tp1])
                P.tt(tp2.t[:, 0:32], src_b.t[:, 32:64], tg_cos, OP.mult,
                     reads=[src_b, trg], writes=[tp2])
                P.tt(dst_ap[:, 32:64], tp1.t[:, 0:32], tp2.t[:, 0:32],
                     OP.add, reads=[tp1, tp2], writes=[dst_b])
            rope_tm(btm, B_tm.t[:, ch * 64:(ch + 1) * 64], B_tm)
            rope_tm(ctm, c1tm.t, c1tm)
            P.transpose(pb[4].t[0:64, 0:128], B_tm.t[:, ch * 64:(ch + 1) * 64],
                        EYE.t, reads=[B_tm, EYE], writes=[pb[4]])
            P.cp(ap_B1[:, chsl], pb[4].t[0:64, 0:128], reads=[pb[4]],
                 writes=[wB1])
            P.transpose(pb[4].t[0:64, 0:128], c1tm.t, EYE.t,
                        reads=[c1tm, EYE], writes=[pb[4]])
            P.cp(ap_C1[:, chsl], pb[4].t[0:64, 0:128], reads=[pb[4]],
                 writes=[w1])
    emit_ww(prev["buf"], prev["ap"], None, prev["idx"])
    A.pop()
    P.barrier()

    TSB = 256
    NSB = T // TSB
    A.push()
    w_x = [A.alloc(2048, BF16, name=f"wx{k}") for k in range(8)]
    for k in range(8):
        P.dma("gpsimd", w_x[k].t, g["w_l0_inproj"][k * 128:(k + 1) * 128, 0:2048],
              writes=[w_x[k]])
    xin_s = [A.alloc(TSB, BF16, name=f"xins{i}") for i in range(8)]
    x_tm = [A.alloc(2048, BF16, name=f"xtm{i}") for i in range(2)]
    ybuf = [A.alloc(128, BF16, name=f"yb{i}") for i in range(2)]
    d0t = A.alloc(128, name="d0t")
    lt = A.alloc(128, name="lt")
    smt = A.alloc(128, name="smt")
    smst = A.alloc(128, name="smst")
    vut = A.alloc(128, name="vut")
    vwt = A.alloc(128, name="vwt")
    vft = A.alloc(128, name="vft")
    bhat = A.alloc(64, name="bhat")
    cdec = A.alloc(128, name="cdec")
    ap_cdec = cdec.t[0:64, 0:128]
    eit = A.alloc(64, name="eit")
    ap_eit = eit.t[0:64, 0:64]
    Hb = [A.alloc(16 * 128, name=f"H{j}") for j in range(2)]
    ap_H = [Hb[0].t[0:64, :], Hb[1].t[0:64, :]]
    P.memset(ap_H[0], 0.0, writes=[Hb[0]])

    for sb in range(NSB):
        t0 = sb * TSB
        for k in range(8):
            P.dma("sync", xin_s[k].t, xin_bf[k * 128:(k + 1) * 128, t0:t0 + TSB],
                  writes=[xin_s[k]])
        for tc in range(2):
            tokap = slice(tc * 128, (tc + 1) * 128)
            for fs in range(4):
                bk = pb[fs % 2]
                for k in range(8):
                    P.mm(bk.t[:, 0:512], xin_s[k].t[:, tokap],
                         w_x[k].t[:, fs * 512:(fs + 1) * 512],
                         start=(k == 0), stop=(k == 7),
                         reads=[xin_s[k], w_x[k]], writes=[bk])
                P.act(x_tm[tc].t[:, fs * 512:(fs + 1) * 512], bk.t[:, 0:512],
                      AF.Silu, reads=[bk], writes=[x_tm[tc]])
        for c in range(TSB // 128):
            ch = sb * (TSB // 128) + c
            curH = ap_H[ch % 2]
            nxtH = ap_H[(ch + 1) % 2]
            curHb = Hb[ch % 2]
            nxtHb = Hb[(ch + 1) % 2]
            chsl = slice(ch * 128, (ch + 1) * 128)
            P.mm(pb[4].t[:, 0:128], ap_B1[:, chsl], ap_C1[:, chsl],
                 start=True, stop=True, reads=[wB1, w1], writes=[pb[4]])
            for h in range(16):
                hcol = slice(ch * 16 + h, ch * 16 + h + 1)
                P.mm(pb[5].t[:, 0:128],
                     EYE16.t[:, h:h + 1].broadcast_to([16, 128]),
                     ap_cc[:, chsl], start=True, stop=True,
                     reads=[EYE16, w0], writes=[pb[5]])
                P.tt(d0t.t[:, 0:128], pb[5].t[:, 0:128], UTRI.t, OP.add,
                     reads=[pb[5], UTRI], writes=[d0t])
                P.act(lt.t[:, 0:128], d0t.t[:, 0:128], AF.Exp,
                      bias=colsT["negcc"].t[:, hcol],
                      reads=[d0t, colsT["negcc"]], writes=[lt])
                P.tt(smt.t[:, 0:128], lt.t[:, 0:128], pb[4].t[:, 0:128],
                     OP.mult, reads=[lt, pb[4]], writes=[smt])
                P.tt(smst.t[:, 0:128], smt.t[:, 0:128], STRICT.t, OP.mult,
                     reads=[smt, STRICT], writes=[smst])
                xs_ap = x_tm[c].t[:, h * 128:(h + 1) * 128]
                P.ts(vut.t[:, 0:128], xs_ap, colsT["u"].t[:, hcol], None,
                     OP.mult, reads=[x_tm[c], colsT["u"]], writes=[vut])
                P.ts(vwt.t[:, 0:128], xs_ap, colsT["w"].t[:, hcol], None,
                     OP.mult, reads=[x_tm[c], colsT["w"]], writes=[vwt])
                P.tt(vft.t[:, 0:128], vut.t[:, 0:128], vwt.t[:, 0:128],
                     OP.add, reads=[vut, vwt], writes=[vft])
                yb = pb[5]
                P.mm(yb.t[:, 128:256], vut.t[:, 0:128], smt.t[:, 0:128],
                     start=True, stop=False, reads=[vut, smt], writes=[yb])
                P.mm(yb.t[:, 128:256], vwt.t[:, 0:128], smst.t[:, 0:128],
                     start=False, stop=False, reads=[vwt, smst], writes=[yb])
                P.mm(yb.t[:, 128:256], xs_ap, DI.t[:, h * 128:(h + 1) * 128],
                     start=False, stop=False, reads=[x_tm[c], DI], writes=[yb])
                P.mm(pb[3].t[0:64, 0:128],
                     EYE16.t[:, h:h + 1].broadcast_to([16, 64]),
                     ap_expcc[:, chsl], start=True, stop=True,
                     reads=[EYE16, wex], writes=[pb[3]])
                P.tt(ap_cdec, ap_C1[:, chsl], pb[3].t[0:64, 0:128], OP.mult,
                     reads=[w1, pb[3]], writes=[cdec])
                P.mm(yb.t[:, 128:256], curH[:, h * 128:(h + 1) * 128], ap_cdec,
                     start=False, stop=True, reads=[curHb, cdec], writes=[yb])
                P.mm(pb[3].t[0:64, 256:257],
                     EYE16.t[:, h:h + 1].broadcast_to([16, 64]),
                     ap_expic0[:, ch:ch + 1], start=True, stop=True,
                     reads=[EYE16, expic0], writes=[pb[3]])
                P.ts(ap_eit, EYE64.t, pb[3].t[0:64, 256:257], None, OP.mult,
                     reads=[EYE64, pb[3]], writes=[eit])
                P.ts(bhat.t[:, 0:64], B_tm.t[:, ch * 64:(ch + 1) * 64],
                     colsT["exprc"].t[:, hcol], None, OP.mult,
                     reads=[B_tm, colsT["exprc"]], writes=[bhat])
                P.mm(pb[3].t[0:64, 128:256], bhat.t[:, 0:64],
                     vft.t[:, 0:128], start=True, stop=False,
                     reads=[bhat, vft], writes=[pb[3]])
                P.mm(pb[3].t[0:64, 128:256], ap_eit,
                     curH[:, h * 128:(h + 1) * 128], start=False, stop=True,
                     reads=[eit, curHb], writes=[pb[3]])
                P.cp(nxtH[:, h * 128:(h + 1) * 128], pb[3].t[0:64, 128:256],
                     reads=[pb[3]], writes=[nxtHb])
                yo = ybuf[h % 2]
                P.cp(yo.t, yb.t[:, 128:256], reads=[yb], writes=[yo])
                P.dma("scalar",
                      y0_bf[h * 128:(h + 1) * 128, (ch * 128):(ch * 128) + 128],
                      yo.t, reads=[yo])
    A.pop()
    A.pop()
    P.barrier()

    TSC = 512
    A.push()
    w_z = [A.alloc(2048, BF16, name=f"wz{k}") for k in range(8)]
    w_o0 = [A.alloc(1024, BF16, name=f"wo0_{k}") for k in range(16)]
    for k in range(8):
        P.dma("gpsimd", w_z[k].t,
              g["w_l0_inproj"][k * 128:(k + 1) * 128, 2048:4096], writes=[w_z[k]])
    for k in range(16):
        P.dma("gpsimd", w_o0[k].t, g["w_l0_outproj"][k * 128:(k + 1) * 128, :],
              writes=[w_o0[k]])
    xinc = [A.alloc(TSC, BF16, name=f"xinc{i}") for i in range(8)]
    xuc = [A.alloc(TSC, name=f"xuc{i}") for i in range(8)]
    y0c = [A.alloc(TSC, BF16, name=f"y0c{i}") for i in range(16)]
    ygc = [A.alloc(TSC, BF16, name=f"ygc{i}") for i in range(16)]
    zsl = A.alloc(TSC, name="zsl")
    for sc in range(T // TSC):
        t0 = sc * TSC
        for k in range(8):
            P.dma("sync", xinc[k].t, xin_bf[k * 128:(k + 1) * 128, t0:t0 + TSC],
                  writes=[xinc[k]])
            P.dma("sync", xuc[k].t, xu_fm[k * 128:(k + 1) * 128, t0:t0 + TSC],
                  writes=[xuc[k]])
        for m in range(16):
            P.dma("scalar", y0c[m].t, y0_bf[m * 128:(m + 1) * 128, t0:t0 + TSC],
                  writes=[y0c[m]])
        for m in range(16):
            bk = pb[m % 2]
            for k in range(8):
                P.mm(bk.t[:, 0:TSC], w_z[k].t[:, m * 128:(m + 1) * 128],
                     xinc[k].t, start=(k == 0), stop=(k == 7),
                     reads=[w_z[k], xinc[k]], writes=[bk])
            P.act(zsl.t, bk.t[:, 0:TSC], AF.Silu, reads=[bk], writes=[zsl])
            P.tt(ygc[m].t, y0c[m].t, zsl.t, OP.mult,
                 reads=[y0c[m], zsl], writes=[ygc[m]])
        for m in range(8):
            bk = pb[2 + (m % 2)]
            for k in range(16):
                P.mm(bk.t[:, 0:TSC], w_o0[k].t[:, m * 128:(m + 1) * 128],
                     ygc[k].t, start=(k == 0), stop=(k == 15),
                     reads=[w_o0[k], ygc[k]], writes=[bk])
            P.tt(xuc[m].t, bk.t[:, 0:TSC], xuc[m].t, OP.add,
                 reads=[bk, xuc[m]], writes=[xuc[m]])
            P.dma("scalar", x1_fm[m * 128:(m + 1) * 128, t0:t0 + TSC],
                  xuc[m].t, reads=[xuc[m]])
    A.pop()
    P.barrier()

    TS1 = 256
    A.push()
    ld_consts(["v_l0f_bd"])
    wgate = [A.alloc(8, name=f"wg{k}") for k in range(8)]
    wd = [[A.alloc(64, BF16, name=f"wd{e}_{k}") for k in range(8)]
          for e in range(8)]
    wu = [A.alloc(1024, BF16, name=f"wu{e}") for e in range(8)]
    bu_l = [A.alloc(1024, name=f"bul{j}") for j in range(1)]
    bu_l = bu_l[0]
    ap_bu = bu_l.t[0:8, 0:1024]
    for k in range(8):
        P.dma("sync", wgate[k].t, g["w_l0f_gate"][k * 128:(k + 1) * 128, :],
              writes=[wgate[k]])
    for e in range(8):
        for k in range(8):
            P.dma("gpsimd", wd[e][k].t, g["w_l0f_Wd"][e, k * 128:(k + 1) * 128, :],
                  writes=[wd[e][k]])
        P.dma("gpsimd", wu[e].t[0:64, :], g["w_l0f_Wu"][e], writes=[wu[e]])
    P.dma("sync", ap_bu, g["w_l0f_bu"], writes=[bu_l])
    x1c = [A.alloc(TS1, name=f"x1c{i}") for i in range(8)]
    xfc = [A.alloc(TS1, name=f"xfc{i}") for i in range(8)]
    xfb = [A.alloc(TS1, BF16, name=f"xfb{i}") for i in range(8)]
    tmp1 = A.alloc(TS1, name="tmp1")
    weT8 = A.alloc(TS1, name="weT8")
    ap_weT = weT8.t[0:8, :]
    tp = A.alloc(64, name="tp")
    ap_lg = tp.t[:, 0:8]
    ap_msk1 = tp.t[:, 8:16]
    ap_msk2 = tp.t[:, 16:24]
    ap_d = tp.t[:, 24:32]
    ap_m1 = tp.t[:, 32:33]
    ap_m2 = tp.t[:, 33:34]
    ap_g1 = tp.t[:, 34:35]
    ap_g2 = tp.t[:, 35:36]
    hbf = A.alloc(TS1, BF16, name="hbf")
    ap_hbf = hbf.t[0:64, :]
    hsc = A.alloc(TS1, BF16, name="hsc")
    ap_hsc = hsc.t[0:64, :]
    wcb = A.alloc(TS1, BF16, name="wcb")
    ap_wcb = wcb.t[0:64, :]

    def top2_we(lg_psum_ap, tokchunk_ap):
        P.cp(ap_lg, lg_psum_ap, reads=[pb[6]], writes=[tp])
        P.red_max(ap_m1, ap_lg, reads=[tp], writes=[tp])
        P.stt(ap_d, ap_lg, 1.0, tp.t[:, 32:33].broadcast_to([128, 8]),
              OP.mult, OP.subtract, reads=[tp], writes=[tp])
        P.act(ap_d, ap_d, AF.Sign, reads=[tp], writes=[tp])
        P.tt(ap_msk1, ap_d, ap_d, OP.mult, reads=[tp], writes=[tp])
        P.ts(ap_msk1, ap_msk1, -1.0, 1.0, OP.mult, OP.add, reads=[tp], writes=[tp])
        P.stt(ap_d, ap_msk1, -1e30, ap_lg, OP.mult, OP.add, reads=[tp], writes=[tp])
        P.red_max(ap_m2, ap_d, reads=[tp], writes=[tp])
        P.stt(ap_d, ap_d, 1.0, tp.t[:, 33:34].broadcast_to([128, 8]),
              OP.mult, OP.subtract, reads=[tp], writes=[tp])
        P.act(ap_d, ap_d, AF.Sign, reads=[tp], writes=[tp])
        P.tt(ap_msk2, ap_d, ap_d, OP.mult, reads=[tp], writes=[tp])
        P.ts(ap_msk2, ap_msk2, -1.0, 1.0, OP.mult, OP.add, reads=[tp], writes=[tp])
        P.tt(ap_d[:, 0:1], ap_m1, ap_m2, OP.subtract, reads=[tp], writes=[tp])
        P.act(ap_g1, ap_d[:, 0:1], AF.Sigmoid, reads=[tp], writes=[tp])
        P.ts(ap_g2, ap_g1, -1.0, 1.0, OP.mult, OP.add, reads=[tp], writes=[tp])
        P.ts(ap_msk1, ap_msk1, tp.t[:, 34:35], None, OP.mult, reads=[tp],
             writes=[tp])
        P.stt(ap_msk2, ap_msk2, tp.t[:, 35:36], ap_msk1, OP.mult, OP.add,
              reads=[tp], writes=[tp])
        P.transpose(pb[7].t[0:8, 0:128], ap_msk2, EYE.t, reads=[tp, EYE],
                    writes=[pb[7]])
        P.cp(ap_weT[:, tokchunk_ap], pb[7].t[0:8, 0:128], reads=[pb[7]],
             writes=[weT8])

    for s1 in range(T // TS1):
        t0 = s1 * TS1
        for i in range(8):
            P.dma("sync", x1c[i].t, x1_fm[i * 128:(i + 1) * 128, t0:t0 + TS1],
                  writes=[x1c[i]])

        def outf(i, mB, rB):
            P.tt(xfc[i].t, x1c[i].t, mB.t[:, 0:TS1], OP.subtract,
                 reads=[x1c[i], mB], writes=[xfc[i]])
            P.tt(xfc[i].t, xfc[i].t, rB.t[:, 0:TS1], OP.mult,
                 reads=[xfc[i], rB], writes=[xfc[i]])
            P.cp(xfb[i].t, xfc[i].t, reads=[xfc[i]], writes=[xfb[i]])
        ln_feature_major(x1c, [b.t for b in x1c], TS1, 1e-5, tmp1,
                         (pb[0], pb[1]), outf)
        for tc in range(2):
            tokap = slice(tc * 128, (tc + 1) * 128)
            for k in range(8):
                P.mm(pb[6].t[:, 0:8], xfc[k].t[:, tokap], wgate[k].t,
                     start=(k == 0), stop=(k == 7),
                     reads=[xfc[k], wgate[k]], writes=[pb[6]])
            top2_we(pb[6].t[:, 0:8], slice(tc * 128, (tc + 1) * 128))
        for e in range(8):
            P.mm(pb[1].t[0:64, 0:TS1],
                 EYE8.t[:, e:e + 1].broadcast_to([8, 64]), ap_weT,
                 start=True, stop=True, reads=[EYE8, weT8], writes=[pb[1]])
            P.cp(ap_wcb, pb[1].t[0:64, 0:TS1], reads=[pb[1]], writes=[wcb])
            bk = pb[0]
            for k in range(8):
                P.mm(bk.t[0:64, 0:TS1], wd[e][k].t, xfb[k].t,
                     start=(k == 0), stop=(k == 7),
                     reads=[wd[e][k], xfb[k]], writes=[bk])
            P.act(ap_hbf, bk.t[0:64, 0:TS1], AF.Relu,
                  bias=C["v_l0f_bd"].t[:, e:e + 1],
                  reads=[bk, C["v_l0f_bd"]], writes=[hbf])
            P.tt(ap_hsc, ap_hbf, ap_wcb, OP.mult, reads=[hbf, wcb], writes=[hsc])
            for mp in range(8):
                ob = pb[4 + mp // 2]
                oap = ob.t[:, (mp % 2) * 256:(mp % 2) * 256 + TS1]
                P.mm(oap, wu[e].t[0:64, mp * 128:(mp + 1) * 128], ap_hsc,
                     start=(e == 0), stop=False, reads=[wu[e], hsc], writes=[ob])
        for mp in range(8):
            ob = pb[4 + mp // 2]
            oap = ob.t[:, (mp % 2) * 256:(mp % 2) * 256 + TS1]
            P.mm(oap, ap_bu[:, mp * 128:(mp + 1) * 128], ap_weT,
                 start=False, stop=True, reads=[bu_l, weT8], writes=[ob])
            P.stt(x1c[mp].t, oap, 0.5, x1c[mp].t, OP.mult, OP.add,
                  reads=[ob, x1c[mp]], writes=[x1c[mp]])
            P.dma("scalar", x2_fm[mp * 128:(mp + 1) * 128, t0:t0 + TS1],
                  x1c[mp].t, reads=[x1c[mp]])
    A.pop()
    P.barrier()

    TS2 = 512
    A.push()
    ld_consts(["v_l1_convw", "v_l1_convb"])
    w_in1 = [A.alloc(4096, BF16, name=f"wi1_{k}") for k in range(8)]
    for k in range(8):
        P.dma("gpsimd", w_in1[k].t, g["w_l1_inproj"][k * 128:(k + 1) * 128, :],
              writes=[w_in1[k]])
    x2c = [A.alloc(TS2, name=f"x2c{i}") for i in range(8)]
    hsb = [A.alloc(TS2, BF16, name=f"hsb{i}") for i in range(8)]
    tmp2 = A.alloc(TS2, name="tmp2")
    xcat = A.alloc(TS2 + 4, name="xcat")
    carry3 = A.alloc(48, name="carry3")
    P.memset(carry3.t, 0.0, writes=[carry3])
    acc2 = [A.alloc(TS2, name=f"acc2_{j}") for j in range(2)]
    resb = A.alloc(TS2, name="resb")
    for s2 in range(T // TS2):
        t0 = s2 * TS2
        for i in range(8):
            P.dma("sync", x2c[i].t, x2_fm[i * 128:(i + 1) * 128, t0:t0 + TS2],
                  writes=[x2c[i]])

        def outh(i, mB, rB):
            P.tt(tmp2.t, x2c[i].t, mB.t[:, 0:TS2], OP.subtract,
                 reads=[x2c[i], mB], writes=[tmp2])
            P.tt(hsb[i].t, tmp2.t, rB.t[:, 0:TS2], OP.mult,
                 reads=[tmp2, rB], writes=[hsb[i]])
        ln_feature_major(x2c, [b.t for b in x2c], TS2, 1e-5, tmp2,
                         (pb[0], pb[1]), outh)
        for m in range(32):
            bk = pb[2 + (m % 2)]
            for k in range(8):
                P.mm(bk.t[:, 0:TS2], w_in1[k].t[:, m * 128:(m + 1) * 128],
                     hsb[k].t, start=(k == 0), stop=(k == 7),
                     reads=[w_in1[k], hsb[k]], writes=[bk])
            if m < 16:
                P.cp(xcat.t[:, 0:3], carry3.t[:, m * 3:m * 3 + 3],
                     reads=[carry3], writes=[xcat])
                P.cp(xcat.t[:, 3:3 + TS2], bk.t[:, 0:TS2], reads=[bk],
                     writes=[xcat])
                P.cp(carry3.t[:, m * 3:m * 3 + 3],
                     xcat.t[:, TS2:TS2 + 3], reads=[xcat], writes=[carry3])
                P.ts(acc2[0].t, xcat.t[:, 0:TS2], C["v_l1_convw"].t[:, m * 4:m * 4 + 1],
                     None, OP.mult, reads=[xcat, C["v_l1_convw"]], writes=[acc2[0]])
                for kk in range(1, 4):
                    P.stt(acc2[kk % 2].t, xcat.t[:, kk:kk + TS2],
                          C["v_l1_convw"].t[:, m * 4 + kk:m * 4 + kk + 1],
                          acc2[(kk + 1) % 2].t, OP.mult, OP.add,
                          reads=[xcat, C["v_l1_convw"], acc2[(kk + 1) % 2]],
                          writes=[acc2[kk % 2]])
                P.act(acc2[0].t, acc2[1].t, AF.Silu,
                      bias=C["v_l1_convb"].t[:, m:m + 1],
                      reads=[acc2[1], C["v_l1_convb"]], writes=[acc2[0]])
                P.dma("scalar", xs_fm[m * 128:(m + 1) * 128, t0:t0 + TS2],
                      acc2[0].t, reads=[acc2[0]])
            else:
                mm_ = m - 16
                P.cp(resb.t, bk.t[:, 0:TS2], reads=[bk], writes=[resb])
                P.dma("scalar", res_fm[mm_ * 128:(mm_ + 1) * 128, t0:t0 + TS2],
                      resb.t, reads=[resb])
    A.pop()
    P.barrier()

    A.push()
    bBl1 = A.alloc(T, name="Bl1")
    ap_Bl1 = bBl1.t[0:64, :]
    bCl1 = A.alloc(T, name="Cl1")
    ap_Cl1 = bCl1.t[0:64, :]
    a_sb = A.alloc(1024, name="a_sb")
    st_sb = A.alloc(1024, name="st_sb")
    P.memset(st_sb.t, 0.0, writes=[st_sb])
    for dc in range(16):
        P.dma("sync", a_sb.t[:, dc * 64:(dc + 1) * 64],
              g["w_l1_alog"][dc * 128:(dc + 1) * 128, :], writes=[a_sb])
    P.act(a_sb.t, a_sb.t, AF.Exp, reads=[a_sb], writes=[a_sb])
    P.ts(a_sb.t, a_sb.t, -1.0, None, OP.mult, reads=[a_sb], writes=[a_sb])

    A.push()
    ld_consts(["v_l1_dtpb"])
    w_dt = [A.alloc(2048, BF16, name=f"wdt{k}") for k in range(16)]
    w_xp = [A.alloc(128, BF16, name=f"wxp{k}") for k in range(16)]
    for k in range(16):
        P.dma("gpsimd", w_dt[k].t, g["w_l1_dtproj"][k * 128:(k + 1) * 128, :],
              writes=[w_dt[k]])
        P.dma("gpsimd", w_xp[k].t, g["w_l1_xproj"][k * 128:(k + 1) * 128, :],
              writes=[w_xp[k]])
    xsb = [A.alloc(TS2, BF16, name=f"xsb{i}") for i in range(16)]
    xs32 = A.alloc(TS2, name="xs32")
    dl32 = A.alloc(TS2, name="dl32")
    sp1 = A.alloc(TS2, name="sp1")
    sp2 = A.alloc(TS2, name="sp2")
    for s2 in range(T // TS2):
        t0 = s2 * TS2
        for k in range(16):
            P.dma("gpsimd", xsb[k].t, xs_fm[k * 128:(k + 1) * 128, t0:t0 + TS2],
                  writes=[xsb[k]])
        bcw_of = {0: bBl1, 64: bCl1}
        for (mo, rows) in [(0, ap_Bl1), (64, ap_Cl1)]:
            bk = pb[5]
            for k in range(16):
                P.mm(bk.t[0:64, 0:TS2], w_xp[k].t[:, mo:mo + 64], xsb[k].t,
                     start=(k == 0), stop=(k == 15),
                     reads=[w_xp[k], xsb[k]], writes=[bk])
            P.cp(rows[:, t0:t0 + TS2], bk.t[0:64, 0:TS2], reads=[bk],
                 writes=[bcw_of[mo]])
        for m in range(16):
            bk = pb[2 + (m % 2)]
            for k in range(16):
                P.mm(bk.t[:, 0:TS2], w_dt[k].t[:, m * 128:(m + 1) * 128],
                     xsb[k].t, start=(k == 0), stop=(k == 15),
                     reads=[w_dt[k], xsb[k]], writes=[bk])
            P.stt(dl32.t, bk.t[:, 0:TS2], 1.0,
                  C["v_l1_dtpb"].t[:, m:m + 1].broadcast_to([128, TS2]),
                  OP.mult, OP.add, reads=[bk, C["v_l1_dtpb"]], writes=[dl32])
            P.act(sp1.t, dl32.t, AF.Abs, reads=[dl32], writes=[sp1])
            P.act(sp1.t, sp1.t, AF.Exp, scale=-1.0, reads=[sp1], writes=[sp1])
            P.act(sp1.t, sp1.t, AF.Ln, bias=1.0, reads=[sp1], writes=[sp1])
            P.op("vector", lambda v, o=sp2.t, a=dl32.t:
                 v.tensor_scalar_max(o, a, 0.0), reads=[dl32], writes=[sp2])
            P.tt(dl32.t, sp1.t, sp2.t, OP.add, reads=[sp1, sp2], writes=[dl32])
            P.dma("scalar", delta_fm[m * 128:(m + 1) * 128, t0:t0 + TS2],
                  dl32.t, reads=[dl32])
            P.dma("sync", xs32.t, xs_fm[m * 128:(m + 1) * 128, t0:t0 + TS2],
                  writes=[xs32])
            P.tt(sp2.t, dl32.t, xs32.t, OP.mult, reads=[dl32, xs32], writes=[sp2])
            P.dma("scalar", dxd_fm[m * 128:(m + 1) * 128, t0:t0 + TS2],
                  sp2.t, reads=[sp2])
    A.pop()
    P.barrier()

    TS3 = 512
    A.push()
    bbcb = [A.alloc(TS3, BF16, name=f"bbcb{j}") for j in range(64)]
    y_sb = [A.alloc(TS3, name=f"ysb{d}") for d in range(16)]
    dlc = [A.alloc(TS3, name=f"dlc{j}") for j in range(2)]
    dxc = [A.alloc(TS3, name=f"dxc{j}") for j in range(2)]
    da2 = [A.alloc(TS3, name=f"da{j}") for j in range(2)]
    dbx = A.alloc(TS3, name="dbx")
    hti = A.alloc(TS3, name="hti")
    ytm = [A.alloc(TS3, name=f"ytm{j}") for j in range(2)]
    silr = A.alloc(TS3, name="silr")
    ygt = A.alloc(TS3, BF16, name="ygt")
    for s3 in range(T // TS3):
        t0 = s3 * TS3
        for nb in range(2):
            for j in range(32):
                n = nb * 32 + j
                bk = pb[j % 2]
                P.mm(bk.t[:, 0:TS3],
                     EYE64.t[:, n:n + 1].broadcast_to([64, 128]),
                     ap_Bl1[:, t0:t0 + TS3], start=True, stop=True,
                     reads=[EYE64, bBl1], writes=[bk])
                P.cp(bbcb[j].t, bk.t[:, 0:TS3], reads=[bk], writes=[bbcb[j]])
                bk = pb[j % 2]
                P.mm(bk.t[:, 0:TS3],
                     EYE64.t[:, n:n + 1].broadcast_to([64, 128]),
                     ap_Cl1[:, t0:t0 + TS3], start=True, stop=True,
                     reads=[EYE64, bCl1], writes=[bk])
                P.cp(bbcb[32 + j].t, bk.t[:, 0:TS3], reads=[bk],
                     writes=[bbcb[32 + j]], eng="scalar")
            for dc in range(16):
                dbuf = dc % 2
                P.dma("sync", dlc[dbuf].t,
                      delta_fm[dc * 128:(dc + 1) * 128, t0:t0 + TS3],
                      writes=[dlc[dbuf]])
                P.dma("gpsimd", dxc[dbuf].t,
                      dxd_fm[dc * 128:(dc + 1) * 128, t0:t0 + TS3],
                      writes=[dxc[dbuf]])
                yb = pb[2 + (dc % 2)]
                for j in range(32):
                    n = nb * 32 + j
                    dab = da2[j % 2]
                    P.act(dab.t, dlc[dbuf].t, AF.Exp,
                          scale=a_sb.t[:, dc * 64 + n:dc * 64 + n + 1],
                          reads=[dlc[dbuf], a_sb], writes=[dab])
                    P.tt(dbx.t, dxc[dbuf].t, bbcb[j].t, OP.mult,
                         reads=[dxc[dbuf], bbcb[j]], writes=[dbx])
                    stcol = st_sb.t[:, dc * 64 + n:dc * 64 + n + 1]
                    P.scan(hti.t, dab.t, dbx.t, stcol,
                           reads=[dab, dbx, st_sb], writes=[hti])
                    P.cp(stcol, hti.t[:, TS3 - 1:TS3], reads=[hti],
                         writes=[st_sb])
                    yt = ytm[j % 2]
                    P.tt(yt.t, hti.t, bbcb[32 + j].t, OP.mult,
                         reads=[hti, bbcb[32 + j]], writes=[yt])
                    P.mm(yb.t[:, 0:TS3], EYE.t, yt.t, start=(j == 0),
                         stop=(j == 31), reads=[EYE, yt], writes=[yb])
                if nb == 0:
                    P.cp(y_sb[dc].t, yb.t[:, 0:TS3], reads=[yb],
                         writes=[y_sb[dc]])
                else:
                    P.tt(y_sb[dc].t, y_sb[dc].t, yb.t[:, 0:TS3], OP.add,
                         reads=[yb, y_sb[dc]], writes=[y_sb[dc]])
        for dc in range(16):
            P.dma("sync", silr.t, res_fm[dc * 128:(dc + 1) * 128, t0:t0 + TS3],
                  writes=[silr])
            P.act(silr.t, silr.t, AF.Silu, reads=[silr], writes=[silr])
            P.tt(ygt.t, y_sb[dc].t, silr.t, OP.mult,
                 reads=[y_sb[dc], silr], writes=[ygt])
            P.dma("scalar", yg_bf[dc * 128:(dc + 1) * 128, t0:t0 + TS3],
                  ygt.t, reads=[ygt])
    A.pop()
    A.pop()
    P.barrier()

    TS4 = 512
    A.push()
    w_o1 = [A.alloc(1024, BF16, name=f"wo1_{k}") for k in range(16)]
    for k in range(16):
        P.dma("gpsimd", w_o1[k].t, g["w_l1_outproj"][k * 128:(k + 1) * 128, :],
              writes=[w_o1[k]])
    ygc4 = [A.alloc(TS4, BF16, name=f"yg4_{k}") for k in range(16)]
    x2c4 = [A.alloc(TS4, name=f"x2c4_{m}") for m in range(8)]
    for s4 in range(T // TS4):
        t0 = s4 * TS4
        for k in range(16):
            P.dma("sync", ygc4[k].t, yg_bf[k * 128:(k + 1) * 128, t0:t0 + TS4],
                  writes=[ygc4[k]])
        for m in range(8):
            P.dma("sync", x2c4[m].t, x2_fm[m * 128:(m + 1) * 128, t0:t0 + TS4],
                  writes=[x2c4[m]])
            bk = pb[m % 2]
            for k in range(16):
                P.mm(bk.t[:, 0:TS4], w_o1[k].t[:, m * 128:(m + 1) * 128],
                     ygc4[k].t, start=(k == 0), stop=(k == 15),
                     reads=[w_o1[k], ygc4[k]], writes=[bk])
            P.tt(x2c4[m].t, bk.t[:, 0:TS4], x2c4[m].t, OP.add,
                 reads=[bk, x2c4[m]], writes=[x2c4[m]])
            P.dma("scalar", x3_fm[m * 128:(m + 1) * 128, t0:t0 + TS4],
                  x2c4[m].t, reads=[x2c4[m]])
    A.pop()
    P.barrier()

    A.push()
    ld_consts(["v_l1f_rb1", "v_l1f_rb2", "v_l1f_b1"])
    rb2B = A.alloc(8, name="rb2B")
    rb2B = Buf(rb2B.t[:, 0:8], "rb2B")
    P.mm(pb[0].t[:, 0:8], ONESR.t, C["v_l1f_rb2"].t, start=True,
         stop=True, reads=[ONESR, C["v_l1f_rb2"]], writes=[pb[0]])
    P.cp(rb2B.t, pb[0].t[:, 0:8], reads=[pb[0]], writes=[rb2B])
    weL = A.alloc(T, name="weL")
    ap_weL = weL.t[0:8, :]
    A.push()
    w_r1 = [A.alloc(4096, BF16, name=f"wr1_{k}") for k in range(8)]
    w_r2 = [A.alloc(8, BF16, name=f"wr2_{k}") for k in range(32)]
    for k in range(8):
        P.dma("gpsimd", w_r1[k].t, g["w_l1f_rw1"][k * 128:(k + 1) * 128, :],
              writes=[w_r1[k]])
    for k in range(32):
        P.dma("gpsimd", w_r2[k].t, g["w_l1f_rw2"][k * 128:(k + 1) * 128, :],
              writes=[w_r2[k]])
    TS5 = 512
    x3c = [A.alloc(TS5, name=f"x3c{i}") for i in range(8)]
    xfc5 = [A.alloc(TS5, BF16, name=f"xf5_{i}") for i in range(8)]
    hmid = [A.alloc(TS5, BF16, name=f"hm{m}") for m in range(32)]
    tmp5 = A.alloc(TS5, name="tmp5")
    tp5 = A.alloc(64, name="tp5")
    ap_lg = tp5.t[:, 0:8]
    ap_msk1 = tp5.t[:, 8:16]
    ap_msk2 = tp5.t[:, 16:24]
    ap_d = tp5.t[:, 24:32]
    ap_m1 = tp5.t[:, 32:33]
    ap_m2 = tp5.t[:, 33:34]
    ap_g1 = tp5.t[:, 34:35]
    ap_g2 = tp5.t[:, 35:36]

    def top2_we5(lg_psum_buf, lg_psum_ap, wecols):
        P.stt(ap_lg, lg_psum_ap, 1.0, rb2B.t, OP.mult, OP.add,
              reads=[lg_psum_buf, rb2B], writes=[tp5])
        P.red_max(ap_m1, ap_lg, reads=[tp5], writes=[tp5])
        P.stt(ap_d, ap_lg, 1.0, tp5.t[:, 32:33].broadcast_to([128, 8]),
              OP.mult, OP.subtract, reads=[tp5], writes=[tp5])
        P.act(ap_d, ap_d, AF.Sign, reads=[tp5], writes=[tp5])
        P.tt(ap_msk1, ap_d, ap_d, OP.mult, reads=[tp5], writes=[tp5])
        P.ts(ap_msk1, ap_msk1, -1.0, 1.0, OP.mult, OP.add, reads=[tp5],
             writes=[tp5])
        P.stt(ap_d, ap_msk1, -1e30, ap_lg, OP.mult, OP.add, reads=[tp5],
              writes=[tp5])
        P.red_max(ap_m2, ap_d, reads=[tp5], writes=[tp5])
        P.stt(ap_d, ap_d, 1.0, tp5.t[:, 33:34].broadcast_to([128, 8]),
              OP.mult, OP.subtract, reads=[tp5], writes=[tp5])
        P.act(ap_d, ap_d, AF.Sign, reads=[tp5], writes=[tp5])
        P.tt(ap_msk2, ap_d, ap_d, OP.mult, reads=[tp5], writes=[tp5])
        P.ts(ap_msk2, ap_msk2, -1.0, 1.0, OP.mult, OP.add, reads=[tp5],
             writes=[tp5])
        P.tt(ap_d[:, 0:1], ap_m1, ap_m2, OP.subtract, reads=[tp5], writes=[tp5])
        P.act(ap_g1, ap_d[:, 0:1], AF.Sigmoid, reads=[tp5], writes=[tp5])
        P.ts(ap_g2, ap_g1, -1.0, 1.0, OP.mult, OP.add, reads=[tp5], writes=[tp5])
        P.ts(ap_msk1, ap_msk1, tp5.t[:, 34:35], None, OP.mult, reads=[tp5],
             writes=[tp5])
        P.stt(ap_msk2, ap_msk2, tp5.t[:, 35:36], ap_msk1, OP.mult, OP.add,
              reads=[tp5], writes=[tp5])
        P.transpose(pb[7].t[0:8, 0:128], ap_msk2, EYE.t, reads=[tp5, EYE],
                    writes=[pb[7]])
        P.cp(ap_weL[:, wecols], pb[7].t[0:8, 0:128], reads=[pb[7]],
             writes=[weL])

    for s5 in range(T // TS5):
        t0 = s5 * TS5
        for i in range(8):
            P.dma("sync", x3c[i].t, x3_fm[i * 128:(i + 1) * 128, t0:t0 + TS5],
                  writes=[x3c[i]])

        def outr(i, mB, rB):
            P.tt(tmp5.t, x3c[i].t, mB.t[:, 0:TS5], OP.subtract,
                 reads=[x3c[i], mB], writes=[tmp5])
            P.tt(tmp5.t, tmp5.t, rB.t[:, 0:TS5], OP.mult,
                 reads=[tmp5, rB], writes=[tmp5])
            P.cp(xfc5[i].t, tmp5.t, reads=[tmp5], writes=[xfc5[i]])
            P.dma("scalar", xf_bfs[i * 128:(i + 1) * 128, t0:t0 + TS5],
                  xfc5[i].t, reads=[xfc5[i]])
        ln_feature_major(x3c, [b.t for b in x3c], TS5, 1e-5, tmp5,
                         (pb[0], pb[1]), outr)
        for m in range(32):
            bk = pb[2 + (m % 2)]
            for k in range(8):
                P.mm(bk.t[:, 0:TS5], w_r1[k].t[:, m * 128:(m + 1) * 128],
                     xfc5[k].t, start=(k == 0), stop=(k == 7),
                     reads=[w_r1[k], xfc5[k]], writes=[bk])
            P.act(hmid[m].t, bk.t[:, 0:TS5], AF.Silu,
                  bias=C["v_l1f_rb1"].t[:, m:m + 1],
                  reads=[bk, C["v_l1f_rb1"]], writes=[hmid[m]])
        for tc in range(4):
            tokap = slice(tc * 128, (tc + 1) * 128)
            for k in range(32):
                P.mm(pb[6].t[:, 0:8], hmid[k].t[:, tokap], w_r2[k].t,
                     start=(k == 0), stop=(k == 31),
                     reads=[hmid[k], w_r2[k]], writes=[pb[6]])
            top2_we5(pb[6], pb[6].t[:, 0:8],
                     slice(t0 + tc * 128, t0 + (tc + 1) * 128))
    P.dma("scalar", we_sc, ap_weL, reads=[weL])
    A.pop()
    P.barrier()

    A.push()
    w1e = [A.alloc(4096, BF16, name=f"w1e{k}") for k in range(8)]
    w2e = [A.alloc(1024, BF16, name=f"w2e{k}") for k in range(32)]
    TSE = 256
    xfe = [A.alloc(TSE, BF16, name=f"xfe{k}") for k in range(8)]
    hscb = [A.alloc(TSE, BF16, name=f"hscb{j}") for j in range(2)]
    wcb5 = A.alloc(TSE, BF16, name="wcb5")
    oev = A.alloc(TSE, name="oev")
    for e in range(8):
        for k in range(8):
            P.dma("gpsimd", w1e[k].t, g["w_l1f_w1"][e, k * 128:(k + 1) * 128, :],
                  writes=[w1e[k]])
        for k in range(32):
            P.dma("gpsimd", w2e[k].t, g["w_l1f_w2"][e, k * 128:(k + 1) * 128, :],
                  writes=[w2e[k]])
        for se in range(T // TSE):
            t0 = se * TSE
            for k in range(8):
                P.dma("sync", xfe[k].t,
                      xf_bfs[k * 128:(k + 1) * 128, t0:t0 + TSE],
                      writes=[xfe[k]])
            P.mm(pb[1].t[:, 0:TSE],
                 EYE8.t[:, e:e + 1].broadcast_to([8, 128]),
                 ap_weL[:, t0:t0 + TSE], start=True, stop=True,
                 reads=[EYE8, weL], writes=[pb[1]])
            P.cp(wcb5.t, pb[1].t[:, 0:TSE], reads=[pb[1]], writes=[wcb5])
            for m in range(32):
                bk = pb[2 + (m % 2)]
                for k in range(8):
                    P.mm(bk.t[:, 0:TSE], w1e[k].t[:, m * 128:(m + 1) * 128],
                         xfe[k].t, start=(k == 0), stop=(k == 7),
                         reads=[w1e[k], xfe[k]], writes=[bk])
                hb = hscb[m % 2]
                P.act(hb.t, bk.t[:, 0:TSE], AF.Silu,
                      bias=g_b1_col(C, m, e),
                      reads=[bk, C["v_l1f_b1"]], writes=[hb])
                P.tt(hb.t, hb.t, wcb5.t, OP.mult, reads=[hb, wcb5], writes=[hb])
                for mp in range(8):
                    ob = pb[4 + mp // 2]
                    oap = ob.t[:, (mp % 2) * 256:(mp % 2) * 256 + TSE]
                    P.mm(oap, w2e[m].t[:, mp * 128:(mp + 1) * 128], hb.t,
                         start=(m == 0), stop=(m == 31),
                         reads=[w2e[m], hb], writes=[ob])
            for mp in range(8):
                ob = pb[4 + mp // 2]
                oap = ob.t[:, (mp % 2) * 256:(mp % 2) * 256 + TSE]
                P.cp(oev.t, oap, reads=[ob], writes=[oev])
                P.dma("scalar", oparts[e, mp * 128:(mp + 1) * 128, t0:t0 + TSE],
                      oev.t, reads=[oev])
    A.pop()
    A.pop()
    P.barrier()

    A.push()
    TSR = 512
    accp = [A.alloc(TSR, name=f"accp{j}") for j in range(2)]
    x3r = A.alloc(TSR, name="x3r")
    b2l = A.alloc(1024, name="b2l")
    ap_b2 = b2l.t[0:8, 0:1024]
    P.dma("sync", ap_b2, g["w_l1f_b2"], writes=[b2l])
    wer = A.alloc(TSR, name="wer")
    ap_wer = wer.t[0:8, :]
    for sr in range(T // TSR):
        t0 = sr * TSR
        P.dma("sync", ap_wer, we_sc[:, t0:t0 + TSR], writes=[wer])
        for mp in range(8):
            bk = pb[mp % 2]
            P.mm(bk.t[:, 0:TSR], ap_b2[:, mp * 128:(mp + 1) * 128], ap_wer,
                 start=True, stop=True, reads=[b2l, wer], writes=[bk])
            P.dma("sync", x3r.t, x3_fm[mp * 128:(mp + 1) * 128, t0:t0 + TSR],
                  writes=[x3r])
            P.dma("sync", accp[0].t, oparts[0, mp * 128:(mp + 1) * 128, t0:t0 + TSR],
                  writes=[accp[0]])
            for e in range(1, 8):
                P.dma("sync", accp[1].t,
                      oparts[e, mp * 128:(mp + 1) * 128, t0:t0 + TSR],
                      writes=[accp[1]])
                P.tt(accp[0].t, accp[0].t, accp[1].t, OP.add,
                     reads=[accp[0], accp[1]], writes=[accp[0]])
            P.tt(x3r.t, x3r.t, accp[0].t, OP.add, reads=[x3r, accp[0]],
                 writes=[x3r])
            P.tt(x3r.t, x3r.t, bk.t[:, 0:TSR], OP.add, reads=[x3r, bk],
                 writes=[x3r])
            P.dma("scalar", out_fm[mp * 128:(mp + 1) * 128, t0:t0 + TSR],
                  x3r.t, reads=[x3r])
    A.pop()
    P.barrier()
    return nc, P, dict(locals())


def g_b1_col(C, m, e):
    return C["v_l1f_b1"].t[:, m * 8 + e:m * 8 + e + 1]


def finish_program(nc, P, env):
    with nc.Block() as block:
        P.finalize(block)
    return nc


_CACHE = {}


def kernel(**inputs):
    core_map = prep_inputs(inputs)
    key = "prog"
    if key not in _CACHE:
        nc, P, env = build_program()
        finish_program(nc, P, env)
        _CACHE[key] = (nc, env)
    nc, env = _CACHE[key]
    in_maps = [core_map(b) for b in range(NCORES)]
    res = run_bass_kernel_spmd(nc, in_maps, core_ids=list(range(NCORES)))
    out = np.stack([np.ascontiguousarray(np.asarray(r["out_fm"]).T)
                    for r in res.results])
    return out.astype(np.float32)


# version 03
# speedup vs baseline: 1.8126x; 1.8126x over previous
import numpy as np
import ml_dtypes
import concourse.bass as bass
import concourse.mybir as mybir
from concourse.bass_utils import run_bass_kernel_spmd
from concourse.alu_op_type import AluOpType as OP

F32 = mybir.dt.float32
BF16 = mybir.dt.bfloat16
AF = mybir.ActivationFunctionType
BFNP = ml_dtypes.bfloat16

T = 4096
D = 1024
NCORES = 8
MAGIC = float(1.5 * 2 ** 23)
PI = float(np.pi)

DEBUG_DUMP = set()
PHASE_LIMIT = 99


class Buf:
    __slots__ = ("t", "wr", "rd", "name")

    def __init__(self, ap, name=""):
        self.t = ap
        self.wr = None
        self.rd = []
        self.name = name

    def __getitem__(self, key):
        return self.t[key]


class Prog:
    ENG = ("tensor", "vector", "scalar", "gpsimd", "sync")

    def __init__(self, nc, stack):
        self.nc = nc
        self.q = {e: [] for e in self.ENG}
        self.sems = {}
        for name in ("T", "V", "A", "G", "DS", "DG", "DA",
                     "FS", "FG", "FA"):
            self.sems[name] = stack.enter_context(nc.semaphore("sem" + name))
        self.cnt = {k: 0 for k in self.sems}
        self.opidx = {e: 0 for e in self.ENG}
        self.val2idx = {k: {} for k in self.sems}
        self.waited = {}
        self.csem = {"tensor": "T", "vector": "V", "scalar": "A",
                     "gpsimd": "G", "sync": None}
        self.dsem = {"sync": "DS", "gpsimd": "DG", "scalar": "DA"}
        self.fsem = {"sync": "FS", "gpsimd": "FG", "scalar": "FA"}
        self.sem_eng = {"T": "tensor", "V": "vector", "A": "scalar",
                        "G": "gpsimd", "DS": "sync", "DG": "gpsimd",
                        "DA": "scalar", "FS": "sync", "FG": "gpsimd",
                        "FA": "scalar"}
        self.dma_issue = {"sync": 0, "gpsimd": 0, "scalar": 0}
        self.flushed_issue = {"sync": 0, "gpsimd": 0, "scalar": 0}

    def _flag_for(self, queue, idx):
        fname = self.fsem[queue]
        if self.flushed_issue[queue] < idx:
            dname = self.dsem[queue]
            need = 16 * self.dma_issue[queue]
            self.cnt[fname] += 1
            fval = self.cnt[fname]
            sems = self.sems

            def runner(eng, dn=dname, nv=need, fs=fname, fv=fval):
                eng.wait_ge(sems[dn], nv)
                eng.nop().then_inc(sems[fs], 1)

            self.q[queue].append(runner)
            self.flushed_issue[queue] = self.dma_issue[queue]
            self.opidx[queue] += 1
            self.val2idx[fname][fval] = self.opidx[queue] - 1
        return (fname, self.cnt[fname])

    def _emit(self, engine, fn, deps, dma):
        sem_name = self.dsem[engine] if dma else self.csem[engine]
        assert sem_name is not None, f"{engine} cannot emit dma={dma}"
        step = 16 if dma else 1
        deps2 = []
        for (dsname, dval) in deps:
            if dval is None:
                continue
            if dsname.startswith("Q"):
                queue = dsname[1:]
                if engine == queue:
                    deps2.append(self._flag_for(queue, dval))
                else:
                    deps2.append(self._flag_for(queue, dval))
            else:
                deps2.append((dsname, dval))
        deps = deps2
        myidx = self.opidx[engine]
        waits = []
        need_drain = False
        for (dsname, dval) in deps:
            if dval is None:
                continue
            if dsname == sem_name and not dma and self.sem_eng[dsname] == engine:
                pidx = self.val2idx[dsname].get(dval, -10)
                if pidx >= myidx - 1:
                    need_drain = True
                continue
            if self.sem_eng[dsname] == engine and not (dsname in ("DS", "DG", "DA")):
                pidx = self.val2idx[dsname].get(dval, -10)
                if pidx >= myidx - 1:
                    need_drain = True
                continue
            key = (engine, dsname)
            if self.waited.get(key, -1) >= dval:
                continue
            self.waited[key] = dval
            waits.append((dsname, dval))
        self.cnt[sem_name] += step
        if dma:
            self.dma_issue[engine] += 1
            token = ("Q" + engine, self.dma_issue[engine])
        else:
            token = (sem_name, self.cnt[sem_name])
        self.val2idx[sem_name][self.cnt[sem_name]] = myidx
        self.opidx[engine] += 1
        sems = self.sems

        def runner(eng, waits=waits, need_drain=need_drain, fn=fn,
                   sem=sems[sem_name], step=step):
            for (wn, wv) in waits:
                eng.wait_ge(sems[wn], wv)
            if need_drain and hasattr(eng, "drain"):
                eng.drain()
            ins = fn(eng)
            ins.then_inc(sem, step)

        self.q[engine].append(runner)
        return token

    def op(self, engine, fn, reads=(), writes=(), deps=(), dma=False,
           track=True):
        dl = list(deps)
        for b in reads:
            if b.wr is not None:
                dl.append(b.wr)
        for b in writes:
            if b.wr is not None:
                dl.append(b.wr)
            dl.extend(b.rd)
        if not track:
            self._emit_notrack(engine, fn, dl)
            return None
        tok = self._emit(engine, fn, dl, dma)
        for b in reads:
            b.rd.append(tok)
        for b in writes:
            b.wr = tok
            b.rd = []
        return tok

    def _emit_notrack(self, engine, fn, deps):
        deps2 = []
        for (dsname, dval) in deps:
            if dval is None:
                continue
            if dsname.startswith("Q"):
                deps2.append(self._flag_for(dsname[1:], dval))
            else:
                deps2.append((dsname, dval))
        waits = []
        for (dsname, dval) in deps2:
            if self.sem_eng[dsname] == engine and dsname not in ("DS", "DG", "DA"):
                continue
            key = (engine, dsname)
            if self.waited.get(key, -1) >= dval:
                continue
            self.waited[key] = dval
            waits.append((dsname, dval))
        self.opidx[engine] += 1
        sems = self.sems

        def runner(eng, waits=waits, fn=fn):
            for (wn, wv) in waits:
                eng.wait_ge(sems[wn], wv)
            fn(eng)

        self.q[engine].append(runner)

    def barrier(self):
        snap = dict(self.cnt)
        for e in self.ENG:
            waits = []
            for sn, sv in snap.items():
                if sv == 0:
                    continue
                if self.sem_eng[sn] == e and sn not in ("DS", "DG", "DA"):
                    continue
                key = (e, sn)
                if self.waited.get(key, -1) >= sv:
                    continue
                self.waited[key] = sv
                waits.append((sn, sv))
            if waits:
                sems = self.sems

                def runner(eng, waits=waits):
                    for (wn, wv) in waits:
                        eng.wait_ge(sems[wn], wv)

                self.q[e].append(runner)

    def finalize(self, block):
        for e in self.ENG:
            ops = self.q[e]

            def body(eng, ops=ops):
                for r in ops:
                    r(eng)

            getattr(block, e)(body)

    def mm(self, out_ap, lhsT, rhs, start, stop, reads=(), writes=(), deps=(),
           track=True):
        return self.op("tensor",
                       lambda t: t.matmul(out_ap, lhsT, rhs, start=start, stop=stop),
                       reads=reads, writes=writes, deps=deps, track=track)

    def transpose(self, out_ap, in_ap, ident_ap, reads=(), writes=(), deps=()):
        return self.op("tensor",
                       lambda t: t.transpose(out_ap, in_ap, ident_ap),
                       reads=reads, writes=writes, deps=deps)

    def cp(self, out_ap, in_ap, reads=(), writes=(), deps=(), eng="vector"):
        if eng == "scalar":
            return self.op(eng, lambda s: s.copy(out_ap, in_ap),
                           reads=reads, writes=writes, deps=deps)
        return self.op(eng, lambda v: v.tensor_copy(out_ap, in_ap),
                       reads=reads, writes=writes, deps=deps)

    def tt(self, out_ap, a, b, opk, reads=(), writes=(), deps=(), eng="vector"):
        return self.op(eng, lambda v: v.tensor_tensor(out_ap, a, b, opk),
                       reads=reads, writes=writes, deps=deps)

    def ts(self, out_ap, in_ap, s1, s2, op0, op1=None, reads=(), writes=(),
           deps=(), eng="vector"):
        if op1 is None:
            return self.op(eng, lambda v: v.tensor_scalar(out_ap, in_ap, s1, None, op0),
                           reads=reads, writes=writes, deps=deps)
        return self.op(eng,
                       lambda v: v.tensor_scalar(out_ap, in_ap, s1, s2, op0, op1),
                       reads=reads, writes=writes, deps=deps)

    def stt(self, out_ap, in0, scalar, in1, op0, op1, reads=(), writes=(),
            deps=(), eng="vector"):
        return self.op(eng,
                       lambda v: v.scalar_tensor_tensor(out_ap, in0, scalar, in1, op0, op1),
                       reads=reads, writes=writes, deps=deps)

    def act(self, out_ap, in_ap, func, bias=None, scale=None, reads=(),
            writes=(), deps=()):
        kw = {}
        if bias is not None:
            kw["bias"] = bias
        if scale is not None:
            kw["scale"] = scale
        return self.op("scalar",
                       lambda s: s.activation(out_ap, in_ap, func, **kw),
                       reads=reads, writes=writes, deps=deps)

    def scan(self, out_ap, d0, d1, init, reads=(), writes=(), deps=()):
        return self.op("vector",
                       lambda v: v.tensor_tensor_scan(out_ap, d0, d1, init,
                                                      OP.mult, OP.add),
                       reads=reads, writes=writes, deps=deps)

    def red_max(self, out_ap, in_ap, reads=(), writes=(), deps=()):
        return self.op("vector",
                       lambda v: v.reduce_max(out_ap, in_ap,
                                              axis=mybir.AxisListType.X),
                       reads=reads, writes=writes, deps=deps)

    def recip(self, out_ap, in_ap, reads=(), writes=(), deps=()):
        return self.op("vector", lambda v: v.reciprocal(out_ap, in_ap),
                       reads=reads, writes=writes, deps=deps)

    def memset(self, ap, val, reads=(), writes=(), deps=(), eng="vector"):
        return self.op(eng, lambda v: v.memset(ap, val),
                       reads=reads, writes=writes, deps=deps)

    def dma(self, queue, out_ap, in_ap, reads=(), writes=(), deps=()):
        return self.op(queue, lambda q: q.dma_start(out=out_ap, in_=in_ap),
                       reads=reads, writes=writes, deps=deps, dma=True)


class Arena:

    def __init__(self, ap, words):
        self.ap = ap
        self.words = words
        self.off = 0
        self.marks = []

    def alloc(self, words, dtype=F32, parts=128, name=""):
        w32 = words if dtype == F32 else (words + 1) // 2
        w32 = (w32 + 1) & ~1
        assert self.off + w32 <= self.words, \
            f"arena overflow: {self.off}+{w32} > {self.words} ({name})"
        sl = self.ap[0:parts, self.off:self.off + w32]
        self.off += w32
        if dtype != F32:
            sl = sl.bitcast(dtype)
        return Buf(sl, name)

    def push(self):
        self.marks.append(self.off)

    def pop(self):
        self.off = self.marks.pop()


def prep_inputs(ws):
    f32 = lambda a: np.ascontiguousarray(a, dtype=np.float32)
    sh = {}
    sh["c_eye128"] = np.eye(128, dtype=np.float32)
    sh["c_eye64"] = np.eye(64, dtype=np.float32)
    sh["c_eye16"] = np.eye(16, dtype=np.float32)
    sh["c_eye32"] = np.eye(32, dtype=np.float32)
    sh["c_eye8"] = np.eye(8, dtype=np.float32)
    sh["c_ones_col"] = np.ones((128, 1), dtype=np.float32)
    sh["c_ones_row"] = np.ones((1, 128), dtype=np.float32)
    q = np.arange(128)
    sh["c_utri"] = np.where(q[:, None] <= q[None, :], 0.0, -1e5).astype(np.float32)
    sh["c_strict"] = (q[:, None] < q[None, :]).astype(np.float32)
    sh["w_l0_inproj"] = f32(ws["L0_attn_in_proj_w"])
    sh["w_l0_outproj"] = f32(ws["L0_attn_out_proj_w"])
    sh["w_l0_theta"] = f32(ws["L0_attn_theta_w"])
    sh["v_l0_ln_s"] = f32(ws["L0_attn_norm_scale"]).reshape(8, 128).T.copy()
    sh["v_l0_ln_b"] = f32(ws["L0_attn_norm_bias"]).reshape(8, 128).T.copy()
    sh["v_l0_alog"] = f32(ws["L0_attn_A_log"]).reshape(16, 1)
    sh["v_l0_dtb"] = f32(ws["L0_attn_dt_bias"]).reshape(16, 1)
    sh["v_l0_bcB"] = f32(ws["L0_attn_bcB_scale"]).reshape(64, 1)
    sh["v_l0_bcC"] = f32(ws["L0_attn_bcC_scale"]).reshape(64, 1)
    sh["v_l0_Bb"] = f32(ws["L0_attn_B_bias"]).reshape(64, 1)
    sh["v_l0_Cb"] = f32(ws["L0_attn_C_bias"]).reshape(64, 1)
    sh["v_l0_D"] = f32(ws["L0_attn_D"]).reshape(1, 16)
    sh["w_l0f_gate"] = f32(ws["L0_ffn_w_gate"])
    sh["w_l0f_Wd"] = f32(ws["L0_ffn_Wd"])
    sh["w_l0f_Wu"] = f32(ws["L0_ffn_Wu"])
    sh["v_l0f_bd"] = f32(ws["L0_ffn_bd"]).T.copy()
    sh["w_l0f_bu"] = f32(ws["L0_ffn_bu"])
    sh["w_l1_inproj"] = f32(ws["L1_attn_in_proj_w"].T)
    sh["w_l1_xproj"] = f32(ws["L1_attn_x_proj_w"].T)
    sh["w_l1_dtproj"] = f32(ws["L1_attn_dt_proj_w"].T)
    sh["w_l1_outproj"] = f32(ws["L1_attn_out_proj_w"].T)
    cw = f32(ws["L1_attn_conv_w"]).reshape(2048, 4)
    sh["v_l1_convw"] = cw.reshape(16, 128, 4).transpose(1, 0, 2).reshape(128, 64).copy()
    sh["v_l1_convb"] = f32(ws["L1_attn_conv_b"]).reshape(16, 128).T.copy()
    sh["v_l1_dtpb"] = f32(ws["L1_attn_dt_proj_b"]).reshape(16, 128).T.copy()
    sh["w_l1_alog"] = f32(ws["L1_attn_A_log"])
    sh["w_l1f_rw1"] = f32(ws["L1_ffn_rw1"])
    sh["w_l1f_rw2"] = f32(ws["L1_ffn_rw2"])
    sh["v_l1f_rb1"] = f32(ws["L1_ffn_rb1"]).reshape(32, 128).T.copy()
    sh["v_l1f_rb2"] = f32(ws["L1_ffn_rb2"]).reshape(1, 8)
    sh["w_l1f_w1"] = f32(ws["L1_ffn_w1"])
    sh["w_l1f_w2"] = f32(ws["L1_ffn_w2"])
    sh["v_l1f_b1"] = f32(ws["L1_ffn_b1"]).T.reshape(32, 128, 8).transpose(1, 0, 2).reshape(128, 256).copy()
    sh["w_l1f_b2"] = f32(ws["L1_ffn_b2"])

    u = np.asarray(ws["L0_attn_u"], dtype=np.float32)

    def core_map(b):
        m = dict(sh)
        m["u_fm"] = np.ascontiguousarray(u[b].T)
        return m

    return core_map


def build_program():
    import contextlib
    nc = bass.Bass("TRN2", target_bir_lowering=False)
    stack = contextlib.ExitStack()

    def ext_in(name, shape, dt=F32):
        return nc.dram_tensor(name, shape, dt, kind="ExternalInput").ap()

    def scratch(name, shape, dt=F32):
        kind = "ExternalOutput" if name in DEBUG_DUMP else "Internal"
        return nc.dram_tensor(name, shape, dt, kind=kind).ap()

    u_fm = ext_in("u_fm", [D, T])
    out_fm = nc.dram_tensor("out_fm", [D, T], F32, kind="ExternalOutput").ap()

    g = {}
    for name, shape in [
        ("c_eye128", [128, 128]), ("c_eye64", [64, 64]), ("c_eye16", [16, 16]),
        ("c_eye32", [32, 32]), ("c_eye8", [8, 8]), ("c_ones_col", [128, 1]), ("c_ones_row", [1, 128]),
        ("c_utri", [128, 128]), ("c_strict", [128, 128]),
        ("w_l0_inproj", [1024, 4256]), ("w_l0_outproj", [2048, 1024]),
        ("w_l0_theta", [1024, 32]),
        ("v_l0_ln_s", [128, 8]), ("v_l0_ln_b", [128, 8]),
        ("v_l0_alog", [16, 1]), ("v_l0_dtb", [16, 1]),
        ("v_l0_bcB", [64, 1]), ("v_l0_bcC", [64, 1]),
        ("v_l0_Bb", [64, 1]), ("v_l0_Cb", [64, 1]), ("v_l0_D", [1, 16]),
        ("w_l0f_gate", [1024, 8]), ("w_l0f_Wd", [8, 1024, 64]),
        ("w_l0f_Wu", [8, 64, 1024]), ("v_l0f_bd", [64, 8]), ("w_l0f_bu", [8, 1024]),
        ("w_l1_inproj", [1024, 4096]), ("w_l1_xproj", [2048, 128]),
        ("w_l1_dtproj", [2048, 2048]), ("w_l1_outproj", [2048, 1024]),
        ("v_l1_convw", [128, 64]), ("v_l1_convb", [128, 16]),
        ("v_l1_dtpb", [128, 16]), ("w_l1_alog", [2048, 64]),
        ("w_l1f_rw1", [1024, 4096]), ("w_l1f_rw2", [4096, 8]),
        ("v_l1f_rb1", [128, 32]), ("v_l1f_rb2", [1, 8]),
        ("w_l1f_w1", [8, 1024, 4096]), ("w_l1f_w2", [8, 4096, 1024]),
        ("v_l1f_b1", [128, 256]), ("w_l1f_b2", [8, 1024]),
    ]:
        g[name] = ext_in(name, shape)

    x1_fm = scratch("x1_fm", [D, T])
    x2_fm = scratch("x2_fm", [D, T])
    x3_fm = scratch("x3_fm", [D, T])
    xin_bf = scratch("xin_bf", [D, T], BF16)
    xu_fm = scratch("xu_fm", [D, T])
    y0_bf = scratch("y0_bf", [2048, T], BF16)
    xs_fm = scratch("xs_fm", [2048, T])
    res_fm = scratch("res_fm", [2048, T])
    delta_fm = scratch("delta_fm", [2048, T])
    dxd_fm = scratch("dxd_fm", [2048, T])
    yg_bf = scratch("yg_bf", [2048, T], BF16)
    xf_bfs = scratch("xf_bfs", [D, T], BF16)
    oparts = scratch("oparts", [8, D, T], BF16)
    we_sc = scratch("we_sc", [8, T], BF16)

    ARENA_WORDS = 47104
    arena_t = stack.enter_context(nc.sbuf_tensor("arena", [128, ARENA_WORDS], F32))
    psum = [stack.enter_context(nc.psum_tensor(f"pb{i}", [128, 512], F32))
            for i in range(8)]
    pb = [Buf(p[:], f"pb{i}") for i, p in enumerate(psum)]

    P = Prog(nc, stack)
    A = Arena(arena_t[:], ARENA_WORDS)

    C = {}
    CONST_SHAPES = {
        "c_eye128": [128, 128], "c_eye64": [64, 64], "c_eye16": [16, 16],
        "c_eye32": [32, 32], "c_eye8": [8, 8],
        "c_ones_col": [128, 1], "c_ones_row": [1, 128],
        "c_utri": [128, 128], "c_strict": [128, 128],
        "v_l0_ln_s": [128, 8], "v_l0_ln_b": [128, 8],
        "v_l0_alog": [16, 1], "v_l0_dtb": [16, 1],
        "v_l0_bcB": [64, 1], "v_l0_bcC": [64, 1],
        "v_l0_Bb": [64, 1], "v_l0_Cb": [64, 1], "v_l0_D": [1, 16],
        "v_l0f_bd": [64, 8],
        "v_l1_convw": [128, 64], "v_l1_convb": [128, 16],
        "v_l1_dtpb": [128, 16],
        "v_l1f_rb1": [128, 32], "v_l1f_rb2": [1, 8],
        "v_l1f_b1": [128, 256],
    }

    def ld_consts(names):
        for name in names:
            shape = CONST_SHAPES[name]
            b = A.alloc(shape[1], F32, name=name)
            C[name] = Buf(b.t[0:shape[0], 0:shape[1]], name)
            P.dma("sync", C[name].t, g[name], writes=[C[name]])

    ld_consts(["c_eye128", "c_eye64", "c_eye16", "c_eye32", "c_eye8",
               "c_ones_col", "c_ones_row"])
    ln_state = {}

    def mk_ln_rows():
        ln_state["buf"] = A.alloc(1024, name="ln_rows")

    EYE = C["c_eye128"]
    EYE64 = C["c_eye64"]
    EYE16 = C["c_eye16"]
    EYE32 = C["c_eye32"]
    EYE8 = C["c_eye8"]
    ONESC = C["c_ones_col"]
    ONESR = C["c_ones_row"]

    def ln_feature_major(xbufs, xaps, tc, eps, tmp, rows2, out_cb,
                         nchunks=8, dsize=1024.0):
        ln_rows = ln_state["buf"]
        bs, br = rows2
        for i in range(nchunks):
            P.mm(bs.t[0:1, 0:tc], ONESC.t, xaps[i], start=(i == 0),
                 stop=(i == nchunks - 1), reads=[xbufs[i], ONESC], writes=[bs])
        for i in range(nchunks):
            P.act(tmp.t[:, 0:tc], xaps[i], AF.Square, reads=[xbufs[i]],
                  writes=[tmp])
            P.mm(br.t[0:1, 0:tc], ONESC.t, tmp.t[:, 0:tc], start=(i == 0),
                 stop=(i == nchunks - 1), reads=[tmp, ONESC], writes=[br])
        mrow = ln_rows.t[0:1, 0:tc]
        rrow = ln_rows.t[0:1, 512:512 + tc]
        t1 = pb[2].t[0:1, 0:tc]
        P.ts(mrow, bs.t[0:1, 0:tc], 1.0 / dsize, None, OP.mult,
             reads=[bs], writes=[ln_rows])
        P.ts(rrow, br.t[0:1, 0:tc], 1.0 / dsize, None, OP.mult,
             reads=[br], writes=[ln_rows])
        P.tt(t1, mrow, mrow, OP.mult, reads=[ln_rows], writes=[pb[2]])
        P.tt(rrow, rrow, t1, OP.subtract, reads=[ln_rows, pb[2]],
             writes=[ln_rows])
        P.ts(rrow, rrow, eps, None, OP.add, reads=[ln_rows], writes=[ln_rows])
        P.recip(rrow, rrow, reads=[ln_rows], writes=[ln_rows])
        P.act(rrow, rrow, AF.Sqrt, reads=[ln_rows], writes=[ln_rows])
        P.mm(bs.t[:, 0:tc], ONESR.t, mrow, start=True, stop=True,
             reads=[ln_rows, ONESR], writes=[bs])
        P.mm(br.t[:, 0:tc], ONESR.t, rrow, start=True, stop=True,
             reads=[ln_rows, ONESR], writes=[br])
        for i in range(nchunks):
            out_cb(i, bs, br)

    NCH = T // 128

    A.push()
    ld_consts(["c_utri", "c_strict", "v_l0_ln_s", "v_l0_ln_b", "v_l0_alog",
               "v_l0_dtb", "v_l0_bcB", "v_l0_bcC", "v_l0_Bb", "v_l0_Cb",
               "v_l0_D"])
    UTRI = C["c_utri"]
    STRICT = C["c_strict"]
    A16 = A.alloc(2, F32, name="A16")
    A16 = Buf(A16.t[0:16, 0:1], "A16")
    P.act(A16.t, C["v_l0_alog"].t, AF.Exp, reads=[C["v_l0_alog"]], writes=[A16])
    P.ts(A16.t, A16.t, -1.0, None, OP.mult, reads=[A16], writes=[A16])
    w0 = A.alloc(T, name="cc_fm")
    ap_cc = w0.t[0:16, :]
    wex = A.alloc(T, name="expcc_fm")
    ap_expcc = wex.t[0:16, :]
    wB1 = A.alloc(T, name="B1_fm")
    ap_B1 = wB1.t[0:64, :]
    w1 = A.alloc(T, name="C1_fm")
    ap_C1 = w1.t[0:64, :]
    expic0 = A.alloc(32, name="expic0")
    ap_expic0 = expic0.t[0:16, 0:32]
    colsT = {nm: A.alloc(NCH * 16, name="colsT_" + nm)
             for nm in ("negcc", "u", "w", "exprc")}
    B_tm = A.alloc(NCH * 64, name="B_tm")
    DI = A.alloc(16 * 128, BF16, name="DI")
    phic = A.alloc(2, name="phic")
    ap_phic = phic.t[0:32, 0:1]
    P.memset(ap_phic, 0.0, writes=[phic])

    for h in range(16):
        P.mm(pb[6].t[:, 0:1], ONESR.t, C["v_l0_D"].t[0:1, h:h + 1], start=True,
             stop=True, reads=[C["v_l0_D"], ONESR], writes=[pb[6]])
        P.ts(DI.t[:, h * 128:(h + 1) * 128], EYE.t, pb[6].t[:, 0:1],
             None, OP.mult, reads=[EYE, pb[6]], writes=[DI])

    if PHASE_LIMIT <= 0:
        return nc, P, dict(locals())
    TSL = 256
    NSL = T // TSL
    A.push()
    mk_ln_rows()
    x8 = [A.alloc(TSL, name=f"x8_{i}") for i in range(8)]
    u8 = [A.alloc(TSL, name=f"u8_{i}") for i in range(8)]
    tmpc = A.alloc(TSL, name="tmpc")
    xin8 = [A.alloc(TSL, BF16, name=f"xin_{i}") for i in range(8)]
    wsm = [A.alloc(160, BF16, name=f"wsm{k}") for k in range(8)]
    wth = [A.alloc(32, name=f"wth{k}") for k in range(8)]
    for k in range(8):
        P.dma("gpsimd", wsm[k].t, g["w_l0_inproj"][k * 128:(k + 1) * 128, 4096:4256],
              writes=[wsm[k]])
        P.dma("sync", wth[k].t[:, 0:32], g["w_l0_theta"][k * 128:(k + 1) * 128, :],
              writes=[wth[k]])
    theta_sl = A.alloc(TSL, name="theta_sl")
    ap_theta = theta_sl.t[0:32, :]
    braw = A.alloc(TSL, name="braw"); ap_braw = braw.t[0:64, :]
    craw = A.alloc(TSL, name="craw"); ap_craw = craw.t[0:64, :]
    bB0 = A.alloc(TSL, name="B0"); ap_B0 = bB0.t[0:64, :]
    bC0 = A.alloc(TSL, name="C0"); ap_C0 = bC0.t[0:64, :]
    sqbc = A.alloc(TSL, name="sqbc"); ap_sq = sqbc.t[0:64, :]
    bdt = [A.alloc(TSL, name=f"dt{j}") for j in range(2)]
    ap_dt = [bdt[0].t[0:16, :], bdt[1].t[0:16, :]]
    bdtl1 = [A.alloc(TSL, name=f"dtl1{j}") for j in range(2)]
    ap_dtl1 = [bdtl1[0].t[0:16, :], bdtl1[1].t[0:16, :]]
    blam = A.alloc(TSL, name="lam"); ap_lam = blam.t[0:16, :]
    buu = A.alloc(TSL, name="uu"); ap_uu = buu.t[0:16, :]
    bww = A.alloc(TSL, name="ww"); ap_ww = bww.t[0:16, :]
    badt = A.alloc(TSL, name="adt"); ap_adt = badt.t[0:16, :]
    bsp = [A.alloc(TSL, name=f"sp{j}") for j in range(3)]
    ap_sp = [b.t[0:16, :] for b in bsp]
    bphi = A.alloc(TSL, name="phi"); ap_phi = bphi.t[0:32, :]
    bq = A.alloc(TSL, name="q"); ap_q = bq.t[0:32, :]
    trg = A.alloc(192, name="trg")
    tg_phi = trg.t[:, 0:32]
    tg_q = trg.t[:, 32:64]
    tg_r = trg.t[:, 64:96]
    tg_sin = trg.t[:, 96:128]
    tg_cos = trg.t[:, 128:160]
    btm = A.alloc(64, name="btm")
    ctm = A.alloc(64, name="ctm")
    c1tm = A.alloc(64, name="c1tm")
    tp1 = A.alloc(32, name="tp1")
    tp2 = A.alloc(32, name="tp2")
    bch1 = A.alloc(128, name="ch1"); ap_ch1 = bch1.t[0:16, 0:128]
    bch2 = A.alloc(128, name="ch2"); ap_ch2 = bch2.t[0:16, 0:128]
    ones16 = A.alloc(128, name="ones16")
    ap_o16 = ones16.t[0:16, 0:128]
    P.memset(ap_o16, 1.0, writes=[ones16])
    ones32 = A.alloc(TSL, name="ones32")
    ap_o32 = ones32.t[0:32, :]
    P.memset(ap_o32, 1.0, writes=[ones32])
    c116 = A.alloc(2, name="c116")
    ap_c116 = c116.t[0:16, 0:1]
    P.memset(ap_c116, 1.0 / 16.0, writes=[c116])
    o64c = A.alloc(2, name="o64c")
    ap_o64 = o64c.t[0:64, 0:1]
    P.memset(ap_o64, 1.0, writes=[o64c])
    NCPS = TSL // 128

    def emit_ww(src_buf, src_ap, nxt, sl_of):
        P.cp(ap_ww[:, 0:TSL - 1], src_ap[:, 1:TSL], reads=[src_buf], writes=[bww])
        if nxt is None:
            P.memset(ap_ww[:, TSL - 1:TSL], 0.0, writes=[bww])
        else:
            nb_, na_ = nxt
            P.cp(ap_ww[:, TSL - 1:TSL], na_[:, 0:1], reads=[nb_], writes=[bww])
        for c in range(NCPS):
            ch = sl_of * NCPS + c
            P.transpose(pb[4].t[0:128, 0:16], ap_ww[:, c * 128:(c + 1) * 128],
                        EYE16.t, reads=[bww, EYE16], writes=[pb[4]])
            P.cp(colsT["w"].t[:, ch * 16:(ch + 1) * 16], pb[4].t[0:128, 0:16],
                 reads=[pb[4]], writes=[colsT["w"]])

    prev = {"buf": None, "ap": None, "idx": None}
    for sl in range(NSL):
        t0 = sl * TSL
        cur = sl % 2
        for i in range(8):
            P.dma("sync", x8[i].t, u_fm[i * 128:(i + 1) * 128, t0:t0 + TSL],
                  writes=[x8[i]])

        def out1(i, mB, rB):
            P.tt(u8[i].t, x8[i].t, mB.t[:, 0:TSL], OP.subtract,
                 reads=[x8[i], mB], writes=[u8[i]])
            P.tt(u8[i].t, u8[i].t, rB.t[:, 0:TSL], OP.mult,
                 reads=[u8[i], rB], writes=[u8[i]])
        ln_feature_major(x8, [b.t for b in x8], TSL, 1e-5, tmpc,
                         (pb[0], pb[1]), out1)
        for i in range(8):
            P.tt(tmpc.t, x8[i].t, u8[i].t, OP.add, reads=[x8[i], u8[i]],
                 writes=[tmpc])
            P.dma("scalar", xu_fm[i * 128:(i + 1) * 128, t0:t0 + TSL], tmpc.t,
                  reads=[tmpc])

        def out2(i, mB, rB):
            P.tt(x8[i].t, u8[i].t, mB.t[:, 0:TSL], OP.subtract,
                 reads=[u8[i], mB], writes=[x8[i]])
            P.tt(x8[i].t, x8[i].t, rB.t[:, 0:TSL], OP.mult,
                 reads=[x8[i], rB], writes=[x8[i]])
            P.stt(xin8[i].t, x8[i].t, C["v_l0_ln_s"].t[:, i:i + 1],
                  C["v_l0_ln_b"].t[:, i:i + 1].broadcast_to([128, TSL]),
                  OP.mult, OP.add, reads=[x8[i], C["v_l0_ln_s"], C["v_l0_ln_b"]],
                  writes=[xin8[i]])
            P.dma("scalar", xin_bf[i * 128:(i + 1) * 128, t0:t0 + TSL],
                  xin8[i].t, reads=[xin8[i]])
        ln_feature_major(u8, [b.t for b in u8], TSL, 1e-6, tmpc,
                         (pb[0], pb[1]), out2)
        for k in range(8):
            P.mm(pb[2].t[0:32, 0:TSL], wth[k].t[:, 0:32], u8[k].t,
                 start=(k == 0), stop=(k == 7),
                 reads=[wth[k], u8[k]], writes=[pb[2]])
        P.cp(ap_theta, pb[2].t[0:32, 0:TSL], reads=[pb[2]], writes=[theta_sl])
        for (mo, msz) in [(0, 64), (64, 64), (128, 16), (144, 16)]:
            bk = pb[3]
            for k in range(8):
                P.mm(bk.t[0:msz, 0:TSL], wsm[k].t[:, mo:mo + msz], xin8[k].t,
                     start=(k == 0), stop=(k == 7),
                     reads=[wsm[k], xin8[k]], writes=[bk])
            if mo == 0:
                P.cp(ap_braw, bk.t[0:64, 0:TSL], reads=[bk], writes=[braw])
            elif mo == 64:
                P.cp(ap_craw, bk.t[0:64, 0:TSL], reads=[bk], writes=[craw])
            elif mo == 128:
                P.stt(ap_dt[cur], bk.t[0:16, 0:TSL], 1.0,
                      C["v_l0_dtb"].t.broadcast_to([16, TSL]), OP.mult, OP.add,
                      reads=[bk, C["v_l0_dtb"]], writes=[bdt[cur]])
                P.act(ap_sp[0], ap_dt[cur], AF.Abs, reads=[bdt[cur]],
                      writes=[bsp[0]])
                P.act(ap_sp[1], ap_sp[0], AF.Exp, scale=-1.0,
                      reads=[bsp[0]], writes=[bsp[1]])
                P.act(ap_sp[1], ap_sp[1], AF.Ln, bias=1.0,
                      reads=[bsp[1]], writes=[bsp[1]])
                P.op("vector", lambda v, o=ap_sp[2], a=ap_dt[cur]:
                     v.tensor_scalar_max(o, a, 0.0),
                     reads=[bdt[cur]], writes=[bsp[2]])
                P.tt(ap_dt[cur], ap_sp[1], ap_sp[2], OP.add,
                     reads=[bsp[1], bsp[2]], writes=[bdt[cur]])
            else:
                P.act(ap_lam, bk.t[0:16, 0:TSL], AF.Sigmoid,
                      reads=[bk], writes=[blam])
        P.tt(ap_uu, ap_dt[cur], ap_lam, OP.mult, reads=[bdt[cur], blam],
             writes=[buu])
        P.ts(ap_adt, ap_dt[cur], A16.t, None, OP.mult, reads=[bdt[cur], A16],
             writes=[badt])
        P.ts(ap_sp[0], ap_lam, -1.0, 1.0, OP.mult, OP.add,
             reads=[blam], writes=[bsp[0]])
        P.tt(ap_dtl1[cur], ap_dt[cur], ap_sp[0], OP.mult,
             reads=[bdt[cur], bsp[0]], writes=[bdtl1[cur]])
        if prev["buf"] is not None:
            emit_ww(prev["buf"], prev["ap"], (bdtl1[cur], ap_dtl1[cur]),
                    prev["idx"])
        prev["buf"] = bdtl1[cur]
        prev["ap"] = ap_dtl1[cur]
        prev["idx"] = sl

        for c in range(NCPS):
            ch = sl * NCPS + c
            csl = slice(c * 128, (c + 1) * 128)
            P.scan(ap_cc[:, ch * 128:(ch + 1) * 128], ap_o16,
                   ap_adt[:, csl], 0.0, reads=[ones16, badt], writes=[w0])
            P.scan(ap_ch1[:, ::-1], ap_o16, ap_adt[:, csl][:, ::-1], 0.0,
                   reads=[ones16, badt], writes=[bch1])
            P.tt(ap_ch2, ap_ch1, ap_adt[:, csl], OP.subtract,
                 reads=[bch1, badt], writes=[bch2])
            P.act(ap_expic0[:, ch:ch + 1], ap_ch1[:, 0:1], AF.Exp,
                  reads=[bch1], writes=[expic0])
            P.act(ap_expcc[:, ch * 128:(ch + 1) * 128],
                  ap_cc[:, ch * 128:(ch + 1) * 128], AF.Exp,
                  reads=[w0], writes=[wex])
            P.act(ap_ch1, ap_ch2, AF.Exp, reads=[bch2], writes=[bch1])
            P.transpose(pb[4].t[0:128, 0:16], ap_ch1, EYE16.t,
                        reads=[bch1, EYE16], writes=[pb[4]])
            P.cp(colsT["exprc"].t[:, ch * 16:(ch + 1) * 16],
                 pb[4].t[0:128, 0:16], reads=[pb[4]], writes=[colsT["exprc"]])
            P.ts(ap_ch1, ap_cc[:, ch * 128:(ch + 1) * 128], -1.0, None,
                 OP.mult, reads=[w0], writes=[bch1])
            P.transpose(pb[4].t[0:128, 0:16], ap_ch1, EYE16.t,
                        reads=[bch1, EYE16], writes=[pb[4]])
            P.cp(colsT["negcc"].t[:, ch * 16:(ch + 1) * 16],
                 pb[4].t[0:128, 0:16], reads=[pb[4]], writes=[colsT["negcc"]])
            P.transpose(pb[4].t[0:128, 0:16], ap_uu[:, csl], EYE16.t,
                        reads=[buu, EYE16], writes=[pb[4]])
            P.cp(colsT["u"].t[:, ch * 16:(ch + 1) * 16], pb[4].t[0:128, 0:16],
                 reads=[pb[4]], writes=[colsT["u"]])

        def rms(raw_ap, raw_buf, scale_c, bias_c, out_ap, out_buf):
            P.act(ap_sq, raw_ap, AF.Square, reads=[raw_buf], writes=[sqbc])
            P.mm(pb[5].t[0:1, 0:TSL], ap_o64, ap_sq, start=True, stop=True,
                 reads=[sqbc, o64c], writes=[pb[5]])
            rr = tmpc.t[0:1, 0:TSL]
            P.ts(rr, pb[5].t[0:1, 0:TSL], 1.0 / 64.0, 1e-6, OP.mult, OP.add,
                 reads=[pb[5]], writes=[tmpc])
            P.recip(rr, rr, reads=[tmpc], writes=[tmpc])
            P.act(rr, rr, AF.Sqrt, reads=[tmpc], writes=[tmpc])
            P.mm(pb[5].t[0:64, 0:TSL], ONESR.t[:, 0:64], rr, start=True,
                 stop=True, reads=[tmpc, ONESR], writes=[pb[5]])
            P.tt(out_ap, raw_ap, pb[5].t[0:64, 0:TSL], OP.mult,
                 reads=[raw_buf, pb[5]], writes=[out_buf])
            P.stt(out_ap, out_ap, scale_c.t, bias_c.t.broadcast_to([64, TSL]),
                  OP.mult, OP.add, reads=[out_buf, scale_c, bias_c],
                  writes=[out_buf])
        rms(ap_braw, braw, C["v_l0_bcB"], C["v_l0_Bb"], ap_B0, bB0)
        rms(ap_craw, craw, C["v_l0_bcC"], C["v_l0_Cb"], ap_C0, bC0)
        P.mm(pb[5].t[0:1, 0:TSL], ap_c116, ap_dt[cur], start=True, stop=True,
             reads=[bdt[cur], c116], writes=[pb[5]])
        P.cp(tmpc.t[0:1, 0:TSL], pb[5].t[0:1, 0:TSL], reads=[pb[5]],
             writes=[tmpc])
        P.mm(pb[5].t[0:32, 0:TSL], ONESR.t[:, 0:32], tmpc.t[0:1, 0:TSL],
             start=True, stop=True, reads=[tmpc, ONESR], writes=[pb[5]])
        P.tt(ap_q, ap_theta, pb[5].t[0:32, 0:TSL], OP.mult,
             reads=[theta_sl, pb[5]], writes=[bq])
        P.scan(ap_phi, ap_o32, ap_q, ap_phic,
               reads=[ones32, bq, phic], writes=[bphi])
        P.cp(ap_phic, ap_phi[:, TSL - 1:TSL], reads=[bphi], writes=[phic])
        for c in range(NCPS):
            ch = sl * NCPS + c
            csl = slice(c * 128, (c + 1) * 128)
            chsl = slice(ch * 128, (ch + 1) * 128)
            P.transpose(pb[4].t[0:128, 0:32], ap_phi[:, csl], EYE32.t,
                        reads=[bphi, EYE32], writes=[pb[4]])
            P.cp(tg_phi, pb[4].t[0:128, 0:32], reads=[pb[4]], writes=[trg])
            P.ts(tg_q, tg_phi, 1.0 / (2 * PI), MAGIC, OP.mult, OP.add,
                 reads=[trg], writes=[trg])
            P.ts(tg_q, tg_q, -MAGIC, None, OP.add, reads=[trg], writes=[trg])
            P.stt(tg_r, tg_q, -2 * PI, tg_phi, OP.mult, OP.add,
                  reads=[trg], writes=[trg])
            P.act(tg_sin, tg_r, AF.Sin, reads=[trg], writes=[trg])
            P.ts(tg_q, tg_phi, 1.0 / (2 * PI), MAGIC + 0.25, OP.mult, OP.add,
                 reads=[trg], writes=[trg])
            P.ts(tg_q, tg_q, -MAGIC, None, OP.add, reads=[trg], writes=[trg])
            P.stt(tg_r, tg_q, -2 * PI, tg_phi, OP.mult, OP.add,
                  reads=[trg], writes=[trg])
            P.ts(tg_r, tg_r, PI / 2, None, OP.add, reads=[trg], writes=[trg])
            P.act(tg_cos, tg_r, AF.Sin, reads=[trg], writes=[trg])
            P.transpose(pb[4].t[0:128, 0:64], ap_B0[:, csl], EYE64.t,
                        reads=[bB0, EYE64], writes=[pb[4]])
            P.cp(btm.t, pb[4].t[0:128, 0:64], reads=[pb[4]], writes=[btm])
            P.transpose(pb[4].t[0:128, 0:64], ap_C0[:, csl], EYE64.t,
                        reads=[bC0, EYE64], writes=[pb[4]])
            P.cp(ctm.t, pb[4].t[0:128, 0:64], reads=[pb[4]], writes=[ctm])

            def rope_tm(src_b, dst_ap, dst_b):
                P.tt(tp1.t[:, 0:32], src_b.t[:, 0:32], tg_cos, OP.mult,
                     reads=[src_b, trg], writes=[tp1])
                P.tt(tp2.t[:, 0:32], src_b.t[:, 32:64], tg_sin, OP.mult,
                     reads=[src_b, trg], writes=[tp2])
                P.tt(dst_ap[:, 0:32], tp1.t[:, 0:32], tp2.t[:, 0:32],
                     OP.subtract, reads=[tp1, tp2], writes=[dst_b])
                P.tt(tp1.t[:, 0:32], src_b.t[:, 0:32], tg_sin, OP.mult,
                     reads=[src_b, trg], writes=[tp1])
                P.tt(tp2.t[:, 0:32], src_b.t[:, 32:64], tg_cos, OP.mult,
                     reads=[src_b, trg], writes=[tp2])
                P.tt(dst_ap[:, 32:64], tp1.t[:, 0:32], tp2.t[:, 0:32],
                     OP.add, reads=[tp1, tp2], writes=[dst_b])
            rope_tm(btm, B_tm.t[:, ch * 64:(ch + 1) * 64], B_tm)
            rope_tm(ctm, c1tm.t, c1tm)
            P.transpose(pb[4].t[0:64, 0:128], B_tm.t[:, ch * 64:(ch + 1) * 64],
                        EYE.t, reads=[B_tm, EYE], writes=[pb[4]])
            P.cp(ap_B1[:, chsl], pb[4].t[0:64, 0:128], reads=[pb[4]],
                 writes=[wB1])
            P.transpose(pb[4].t[0:64, 0:128], c1tm.t, EYE.t,
                        reads=[c1tm, EYE], writes=[pb[4]])
            P.cp(ap_C1[:, chsl], pb[4].t[0:64, 0:128], reads=[pb[4]],
                 writes=[w1])
    emit_ww(prev["buf"], prev["ap"], None, prev["idx"])
    A.pop()
    P.barrier()

    if PHASE_LIMIT <= 1:
        return nc, P, dict(locals())
    TSB = 256
    NSB = T // TSB
    A.push()
    w_x = [A.alloc(2048, BF16, name=f"wx{k}") for k in range(8)]
    for k in range(8):
        P.dma("gpsimd", w_x[k].t, g["w_l0_inproj"][k * 128:(k + 1) * 128, 0:2048],
              writes=[w_x[k]])
    xin_s = [A.alloc(TSB, BF16, name=f"xins{i}") for i in range(8)]
    x_tm = [A.alloc(2048, BF16, name=f"xtm{i}") for i in range(2)]
    ybuf = [A.alloc(128, BF16, name=f"yb{i}") for i in range(2)]
    d0t = A.alloc(128, name="d0t")
    lt = A.alloc(128, name="lt")
    smt = A.alloc(128, name="smt")
    smst = A.alloc(128, name="smst")
    vut = A.alloc(128, name="vut")
    vwt = A.alloc(128, name="vwt")
    vft = A.alloc(128, name="vft")
    bhat = A.alloc(64, name="bhat")
    cdec = A.alloc(128, name="cdec")
    ap_cdec = cdec.t[0:64, 0:128]
    eit = A.alloc(64, name="eit")
    ap_eit = eit.t[0:64, 0:64]
    Hb = [A.alloc(16 * 128, name=f"H{j}") for j in range(2)]
    ap_H = [Hb[0].t[0:64, :], Hb[1].t[0:64, :]]
    P.memset(ap_H[0], 0.0, writes=[Hb[0]])

    for sb in range(NSB):
        t0 = sb * TSB
        for k in range(8):
            P.dma("sync", xin_s[k].t, xin_bf[k * 128:(k + 1) * 128, t0:t0 + TSB],
                  writes=[xin_s[k]])
        for tc in range(2):
            tokap = slice(tc * 128, (tc + 1) * 128)
            for fs in range(4):
                bk = pb[fs % 2]
                for k in range(8):
                    P.mm(bk.t[:, 0:512], xin_s[k].t[:, tokap],
                         w_x[k].t[:, fs * 512:(fs + 1) * 512],
                         start=(k == 0), stop=(k == 7),
                         reads=[xin_s[k], w_x[k]], writes=[bk])
                P.act(x_tm[tc].t[:, fs * 512:(fs + 1) * 512], bk.t[:, 0:512],
                      AF.Silu, reads=[bk], writes=[x_tm[tc]])
        for c in range(TSB // 128):
            ch = sb * (TSB // 128) + c
            curH = ap_H[ch % 2]
            nxtH = ap_H[(ch + 1) % 2]
            curHb = Hb[ch % 2]
            nxtHb = Hb[(ch + 1) % 2]
            chsl = slice(ch * 128, (ch + 1) * 128)
            P.mm(pb[4].t[:, 0:128], ap_B1[:, chsl], ap_C1[:, chsl],
                 start=True, stop=True, reads=[wB1, w1], writes=[pb[4]])
            for h in range(16):
                hcol = slice(ch * 16 + h, ch * 16 + h + 1)
                P.mm(pb[5].t[:, 0:128],
                     EYE16.t[:, h:h + 1].broadcast_to([16, 128]),
                     ap_cc[:, chsl], start=True, stop=True,
                     reads=[EYE16, w0], writes=[pb[5]])
                P.tt(d0t.t[:, 0:128], pb[5].t[:, 0:128], UTRI.t, OP.add,
                     reads=[pb[5], UTRI], writes=[d0t])
                P.act(lt.t[:, 0:128], d0t.t[:, 0:128], AF.Exp,
                      bias=colsT["negcc"].t[:, hcol],
                      reads=[d0t, colsT["negcc"]], writes=[lt])
                P.tt(smt.t[:, 0:128], lt.t[:, 0:128], pb[4].t[:, 0:128],
                     OP.mult, reads=[lt, pb[4]], writes=[smt])
                P.tt(smst.t[:, 0:128], smt.t[:, 0:128], STRICT.t, OP.mult,
                     reads=[smt, STRICT], writes=[smst])
                xs_ap = x_tm[c].t[:, h * 128:(h + 1) * 128]
                P.ts(vut.t[:, 0:128], xs_ap, colsT["u"].t[:, hcol], None,
                     OP.mult, reads=[x_tm[c], colsT["u"]], writes=[vut])
                P.ts(vwt.t[:, 0:128], xs_ap, colsT["w"].t[:, hcol], None,
                     OP.mult, reads=[x_tm[c], colsT["w"]], writes=[vwt])
                P.tt(vft.t[:, 0:128], vut.t[:, 0:128], vwt.t[:, 0:128],
                     OP.add, reads=[vut, vwt], writes=[vft])
                yb = pb[5]
                P.mm(yb.t[:, 128:256], vut.t[:, 0:128], smt.t[:, 0:128],
                     start=True, stop=False, reads=[vut, smt], writes=[yb])
                P.mm(yb.t[:, 128:256], vwt.t[:, 0:128], smst.t[:, 0:128],
                     start=False, stop=False, reads=[vwt, smst], writes=[yb])
                P.mm(yb.t[:, 128:256], xs_ap, DI.t[:, h * 128:(h + 1) * 128],
                     start=False, stop=False, reads=[x_tm[c], DI], writes=[yb])
                P.mm(pb[3].t[0:64, 0:128],
                     EYE16.t[:, h:h + 1].broadcast_to([16, 64]),
                     ap_expcc[:, chsl], start=True, stop=True,
                     reads=[EYE16, wex], writes=[pb[3]])
                P.tt(ap_cdec, ap_C1[:, chsl], pb[3].t[0:64, 0:128], OP.mult,
                     reads=[w1, pb[3]], writes=[cdec])
                P.mm(yb.t[:, 128:256], curH[:, h * 128:(h + 1) * 128], ap_cdec,
                     start=False, stop=True, reads=[curHb, cdec], writes=[yb])
                P.mm(pb[3].t[0:64, 256:257],
                     EYE16.t[:, h:h + 1].broadcast_to([16, 64]),
                     ap_expic0[:, ch:ch + 1], start=True, stop=True,
                     reads=[EYE16, expic0], writes=[pb[3]])
                P.ts(ap_eit, EYE64.t, pb[3].t[0:64, 256:257], None, OP.mult,
                     reads=[EYE64, pb[3]], writes=[eit])
                P.ts(bhat.t[:, 0:64], B_tm.t[:, ch * 64:(ch + 1) * 64],
                     colsT["exprc"].t[:, hcol], None, OP.mult,
                     reads=[B_tm, colsT["exprc"]], writes=[bhat])
                P.mm(pb[3].t[0:64, 128:256], bhat.t[:, 0:64],
                     vft.t[:, 0:128], start=True, stop=False,
                     reads=[bhat, vft], writes=[pb[3]])
                P.mm(pb[3].t[0:64, 128:256], ap_eit,
                     curH[:, h * 128:(h + 1) * 128], start=False, stop=True,
                     reads=[eit, curHb], writes=[pb[3]])
                P.cp(nxtH[:, h * 128:(h + 1) * 128], pb[3].t[0:64, 128:256],
                     reads=[pb[3]], writes=[nxtHb])
                yo = ybuf[h % 2]
                P.cp(yo.t, yb.t[:, 128:256], reads=[yb], writes=[yo])
                P.dma("scalar",
                      y0_bf[h * 128:(h + 1) * 128, (ch * 128):(ch * 128) + 128],
                      yo.t, reads=[yo])
    A.pop()
    A.pop()
    P.barrier()

    if PHASE_LIMIT <= 2:
        return nc, P, dict(locals())
    TSC = 512
    A.push()
    w_z = [A.alloc(2048, BF16, name=f"wz{k}") for k in range(8)]
    w_o0 = [A.alloc(1024, BF16, name=f"wo0_{k}") for k in range(16)]
    for k in range(8):
        P.dma("gpsimd", w_z[k].t,
              g["w_l0_inproj"][k * 128:(k + 1) * 128, 2048:4096], writes=[w_z[k]])
    for k in range(16):
        P.dma("gpsimd", w_o0[k].t, g["w_l0_outproj"][k * 128:(k + 1) * 128, :],
              writes=[w_o0[k]])
    xinc = [A.alloc(TSC, BF16, name=f"xinc{i}") for i in range(8)]
    xuc = [A.alloc(TSC, name=f"xuc{i}") for i in range(8)]
    y0c = [A.alloc(TSC, BF16, name=f"y0c{i}") for i in range(16)]
    ygc = [A.alloc(TSC, BF16, name=f"ygc{i}") for i in range(16)]
    zsl = A.alloc(TSC, name="zsl")
    for sc in range(T // TSC):
        t0 = sc * TSC
        for k in range(8):
            P.dma("sync", xinc[k].t, xin_bf[k * 128:(k + 1) * 128, t0:t0 + TSC],
                  writes=[xinc[k]])
            P.dma("sync", xuc[k].t, xu_fm[k * 128:(k + 1) * 128, t0:t0 + TSC],
                  writes=[xuc[k]])
        for m in range(16):
            P.dma("scalar", y0c[m].t, y0_bf[m * 128:(m + 1) * 128, t0:t0 + TSC],
                  writes=[y0c[m]])
        for m in range(16):
            bk = pb[m % 2]
            for k in range(8):
                P.mm(bk.t[:, 0:TSC], w_z[k].t[:, m * 128:(m + 1) * 128],
                     xinc[k].t, start=(k == 0), stop=(k == 7),
                     reads=[w_z[k], xinc[k]], writes=[bk])
            P.act(zsl.t, bk.t[:, 0:TSC], AF.Silu, reads=[bk], writes=[zsl])
            P.tt(ygc[m].t, y0c[m].t, zsl.t, OP.mult,
                 reads=[y0c[m], zsl], writes=[ygc[m]])
        for m in range(8):
            bk = pb[2 + (m % 2)]
            for k in range(16):
                P.mm(bk.t[:, 0:TSC], w_o0[k].t[:, m * 128:(m + 1) * 128],
                     ygc[k].t, start=(k == 0), stop=(k == 15),
                     reads=[w_o0[k], ygc[k]], writes=[bk])
            P.tt(xuc[m].t, bk.t[:, 0:TSC], xuc[m].t, OP.add,
                 reads=[bk, xuc[m]], writes=[xuc[m]])
            P.dma("scalar", x1_fm[m * 128:(m + 1) * 128, t0:t0 + TSC],
                  xuc[m].t, reads=[xuc[m]])
    A.pop()
    P.barrier()

    if PHASE_LIMIT <= 3:
        return nc, P, dict(locals())
    TS1 = 256
    A.push()
    mk_ln_rows()
    ld_consts(["v_l0f_bd"])
    wgate = [A.alloc(8, name=f"wg{k}") for k in range(8)]
    wd = [[A.alloc(64, BF16, name=f"wd{e}_{k}") for k in range(8)]
          for e in range(8)]
    wu = [A.alloc(1024, BF16, name=f"wu{e}") for e in range(8)]
    bu_l = [A.alloc(1024, name=f"bul{j}") for j in range(1)]
    bu_l = bu_l[0]
    ap_bu = bu_l.t[0:8, 0:1024]
    for k in range(8):
        P.dma("sync", wgate[k].t, g["w_l0f_gate"][k * 128:(k + 1) * 128, :],
              writes=[wgate[k]])
    for e in range(8):
        for k in range(8):
            P.dma("gpsimd", wd[e][k].t, g["w_l0f_Wd"][e, k * 128:(k + 1) * 128, :],
                  writes=[wd[e][k]])
        P.dma("gpsimd", wu[e].t[0:64, :], g["w_l0f_Wu"][e], writes=[wu[e]])
    P.dma("sync", ap_bu, g["w_l0f_bu"], writes=[bu_l])
    x1c = [A.alloc(TS1, name=f"x1c{i}") for i in range(8)]
    xfc = [A.alloc(TS1, name=f"xfc{i}") for i in range(8)]
    xfb = [A.alloc(TS1, BF16, name=f"xfb{i}") for i in range(8)]
    tmp1 = A.alloc(TS1, name="tmp1")
    weT8 = A.alloc(TS1, name="weT8")
    ap_weT = weT8.t[0:8, :]
    tp = A.alloc(64, name="tp")
    ap_lg = tp.t[:, 0:8]
    ap_msk1 = tp.t[:, 8:16]
    ap_msk2 = tp.t[:, 16:24]
    ap_d = tp.t[:, 24:32]
    ap_m1 = tp.t[:, 32:33]
    ap_m2 = tp.t[:, 33:34]
    ap_g1 = tp.t[:, 34:35]
    ap_g2 = tp.t[:, 35:36]
    hbf = A.alloc(TS1, BF16, name="hbf")
    ap_hbf = hbf.t[0:64, :]
    hsc = A.alloc(TS1, BF16, name="hsc")
    ap_hsc = hsc.t[0:64, :]
    wcb = A.alloc(TS1, BF16, name="wcb")
    ap_wcb = wcb.t[0:64, :]

    def top2_we(lg_psum_ap, tokchunk_ap):
        P.cp(ap_lg, lg_psum_ap, reads=[pb[6]], writes=[tp])
        P.red_max(ap_m1, ap_lg, reads=[tp], writes=[tp])
        P.stt(ap_d, ap_lg, 1.0, tp.t[:, 32:33].broadcast_to([128, 8]),
              OP.mult, OP.subtract, reads=[tp], writes=[tp])
        P.act(ap_d, ap_d, AF.Sign, reads=[tp], writes=[tp])
        P.tt(ap_msk1, ap_d, ap_d, OP.mult, reads=[tp], writes=[tp])
        P.ts(ap_msk1, ap_msk1, -1.0, 1.0, OP.mult, OP.add, reads=[tp], writes=[tp])
        P.stt(ap_d, ap_msk1, -1e30, ap_lg, OP.mult, OP.add, reads=[tp], writes=[tp])
        P.red_max(ap_m2, ap_d, reads=[tp], writes=[tp])
        P.stt(ap_d, ap_d, 1.0, tp.t[:, 33:34].broadcast_to([128, 8]),
              OP.mult, OP.subtract, reads=[tp], writes=[tp])
        P.act(ap_d, ap_d, AF.Sign, reads=[tp], writes=[tp])
        P.tt(ap_msk2, ap_d, ap_d, OP.mult, reads=[tp], writes=[tp])
        P.ts(ap_msk2, ap_msk2, -1.0, 1.0, OP.mult, OP.add, reads=[tp], writes=[tp])
        P.tt(ap_d[:, 0:1], ap_m1, ap_m2, OP.subtract, reads=[tp], writes=[tp])
        P.act(ap_g1, ap_d[:, 0:1], AF.Sigmoid, reads=[tp], writes=[tp])
        P.ts(ap_g2, ap_g1, -1.0, 1.0, OP.mult, OP.add, reads=[tp], writes=[tp])
        P.ts(ap_msk1, ap_msk1, tp.t[:, 34:35], None, OP.mult, reads=[tp],
             writes=[tp])
        P.stt(ap_msk2, ap_msk2, tp.t[:, 35:36], ap_msk1, OP.mult, OP.add,
              reads=[tp], writes=[tp])
        P.transpose(pb[7].t[0:8, 0:128], ap_msk2, EYE.t, reads=[tp, EYE],
                    writes=[pb[7]])
        P.cp(ap_weT[:, tokchunk_ap], pb[7].t[0:8, 0:128], reads=[pb[7]],
             writes=[weT8])

    for s1 in range(T // TS1):
        t0 = s1 * TS1
        for i in range(8):
            P.dma("sync", x1c[i].t, x1_fm[i * 128:(i + 1) * 128, t0:t0 + TS1],
                  writes=[x1c[i]])

        def outf(i, mB, rB):
            P.tt(xfc[i].t, x1c[i].t, mB.t[:, 0:TS1], OP.subtract,
                 reads=[x1c[i], mB], writes=[xfc[i]])
            P.tt(xfc[i].t, xfc[i].t, rB.t[:, 0:TS1], OP.mult,
                 reads=[xfc[i], rB], writes=[xfc[i]])
            P.cp(xfb[i].t, xfc[i].t, reads=[xfc[i]], writes=[xfb[i]])
        ln_feature_major(x1c, [b.t for b in x1c], TS1, 1e-5, tmp1,
                         (pb[0], pb[1]), outf)
        for tc in range(2):
            tokap = slice(tc * 128, (tc + 1) * 128)
            for k in range(8):
                P.mm(pb[6].t[:, 0:8], xfc[k].t[:, tokap], wgate[k].t,
                     start=(k == 0), stop=(k == 7),
                     reads=[xfc[k], wgate[k]], writes=[pb[6]])
            top2_we(pb[6].t[:, 0:8], slice(tc * 128, (tc + 1) * 128))
        for e in range(8):
            P.mm(pb[1].t[0:64, 0:TS1],
                 EYE8.t[:, e:e + 1].broadcast_to([8, 64]), ap_weT,
                 start=True, stop=True, reads=[EYE8, weT8], writes=[pb[1]])
            P.cp(ap_wcb, pb[1].t[0:64, 0:TS1], reads=[pb[1]], writes=[wcb])
            bk = pb[0]
            for k in range(8):
                P.mm(bk.t[0:64, 0:TS1], wd[e][k].t, xfb[k].t,
                     start=(k == 0), stop=(k == 7),
                     reads=[wd[e][k], xfb[k]], writes=[bk])
            P.act(ap_hbf, bk.t[0:64, 0:TS1], AF.Relu,
                  bias=C["v_l0f_bd"].t[:, e:e + 1],
                  reads=[bk, C["v_l0f_bd"]], writes=[hbf])
            P.tt(ap_hsc, ap_hbf, ap_wcb, OP.mult, reads=[hbf, wcb], writes=[hsc])
            for mp in range(8):
                ob = pb[4 + mp // 2]
                oap = ob.t[:, (mp % 2) * 256:(mp % 2) * 256 + TS1]
                P.mm(oap, wu[e].t[0:64, mp * 128:(mp + 1) * 128], ap_hsc,
                     start=(e == 0), stop=False, reads=[wu[e], hsc], writes=[ob])
        for mp in range(8):
            ob = pb[4 + mp // 2]
            oap = ob.t[:, (mp % 2) * 256:(mp % 2) * 256 + TS1]
            P.mm(oap, ap_bu[:, mp * 128:(mp + 1) * 128], ap_weT,
                 start=False, stop=True, reads=[bu_l, weT8], writes=[ob])
            P.stt(x1c[mp].t, oap, 0.5, x1c[mp].t, OP.mult, OP.add,
                  reads=[ob, x1c[mp]], writes=[x1c[mp]])
            P.dma("scalar", x2_fm[mp * 128:(mp + 1) * 128, t0:t0 + TS1],
                  x1c[mp].t, reads=[x1c[mp]])
    A.pop()
    P.barrier()

    if PHASE_LIMIT <= 4:
        return nc, P, dict(locals())
    TS2 = 512
    A.push()
    mk_ln_rows()
    ld_consts(["v_l1_convw", "v_l1_convb"])
    w_in1 = [A.alloc(4096, BF16, name=f"wi1_{k}") for k in range(8)]
    for k in range(8):
        P.dma("gpsimd", w_in1[k].t, g["w_l1_inproj"][k * 128:(k + 1) * 128, :],
              writes=[w_in1[k]])
    x2c = [A.alloc(TS2, name=f"x2c{i}") for i in range(8)]
    hsb = [A.alloc(TS2, BF16, name=f"hsb{i}") for i in range(8)]
    tmp2 = A.alloc(TS2, name="tmp2")
    xcat = A.alloc(TS2 + 4, name="xcat")
    carry3 = A.alloc(48, name="carry3")
    P.memset(carry3.t, 0.0, writes=[carry3])
    acc2 = [A.alloc(TS2, name=f"acc2_{j}") for j in range(2)]
    resb = A.alloc(TS2, name="resb")
    for s2 in range(T // TS2):
        t0 = s2 * TS2
        for i in range(8):
            P.dma("sync", x2c[i].t, x2_fm[i * 128:(i + 1) * 128, t0:t0 + TS2],
                  writes=[x2c[i]])

        def outh(i, mB, rB):
            P.tt(tmp2.t, x2c[i].t, mB.t[:, 0:TS2], OP.subtract,
                 reads=[x2c[i], mB], writes=[tmp2])
            P.tt(hsb[i].t, tmp2.t, rB.t[:, 0:TS2], OP.mult,
                 reads=[tmp2, rB], writes=[hsb[i]])
        ln_feature_major(x2c, [b.t for b in x2c], TS2, 1e-5, tmp2,
                         (pb[0], pb[1]), outh)
        for m in range(32):
            bk = pb[2 + (m % 2)]
            for k in range(8):
                P.mm(bk.t[:, 0:TS2], w_in1[k].t[:, m * 128:(m + 1) * 128],
                     hsb[k].t, start=(k == 0), stop=(k == 7),
                     reads=[w_in1[k], hsb[k]], writes=[bk])
            if m < 16:
                P.cp(xcat.t[:, 0:3], carry3.t[:, m * 3:m * 3 + 3],
                     reads=[carry3], writes=[xcat])
                P.cp(xcat.t[:, 3:3 + TS2], bk.t[:, 0:TS2], reads=[bk],
                     writes=[xcat])
                P.cp(carry3.t[:, m * 3:m * 3 + 3],
                     xcat.t[:, TS2:TS2 + 3], reads=[xcat], writes=[carry3])
                P.ts(acc2[0].t, xcat.t[:, 0:TS2], C["v_l1_convw"].t[:, m * 4:m * 4 + 1],
                     None, OP.mult, reads=[xcat, C["v_l1_convw"]], writes=[acc2[0]])
                for kk in range(1, 4):
                    P.stt(acc2[kk % 2].t, xcat.t[:, kk:kk + TS2],
                          C["v_l1_convw"].t[:, m * 4 + kk:m * 4 + kk + 1],
                          acc2[(kk + 1) % 2].t, OP.mult, OP.add,
                          reads=[xcat, C["v_l1_convw"], acc2[(kk + 1) % 2]],
                          writes=[acc2[kk % 2]])
                P.act(acc2[0].t, acc2[1].t, AF.Silu,
                      bias=C["v_l1_convb"].t[:, m:m + 1],
                      reads=[acc2[1], C["v_l1_convb"]], writes=[acc2[0]])
                P.dma("scalar", xs_fm[m * 128:(m + 1) * 128, t0:t0 + TS2],
                      acc2[0].t, reads=[acc2[0]])
            else:
                mm_ = m - 16
                P.cp(resb.t, bk.t[:, 0:TS2], reads=[bk], writes=[resb])
                P.dma("scalar", res_fm[mm_ * 128:(mm_ + 1) * 128, t0:t0 + TS2],
                      resb.t, reads=[resb])
    A.pop()
    P.barrier()

    if PHASE_LIMIT <= 5:
        return nc, P, dict(locals())
    A.push()
    bBl1 = A.alloc(T, BF16, name="Bl1")
    ap_Bl1 = bBl1.t[0:64, :]
    bCl1 = A.alloc(T, BF16, name="Cl1")
    ap_Cl1 = bCl1.t[0:64, :]
    eye64bf = A.alloc(64, BF16, name="eye64bf")
    ap_e64b = eye64bf.t[0:64, 0:64]
    P.cp(ap_e64b, EYE64.t, reads=[EYE64], writes=[eye64bf])
    eye128bf = A.alloc(128, BF16, name="eye128bf")
    ap_e128b = eye128bf.t[:, 0:128]
    P.cp(ap_e128b, EYE.t, reads=[EYE], writes=[eye128bf])
    a_sb = A.alloc(1024, name="a_sb")
    st_sb = A.alloc(1024, name="st_sb")
    P.memset(st_sb.t, 0.0, writes=[st_sb])
    for dc in range(16):
        P.dma("sync", a_sb.t[:, dc * 64:(dc + 1) * 64],
              g["w_l1_alog"][dc * 128:(dc + 1) * 128, :], writes=[a_sb])
    P.act(a_sb.t, a_sb.t, AF.Exp, reads=[a_sb], writes=[a_sb])
    P.ts(a_sb.t, a_sb.t, -1.0, None, OP.mult, reads=[a_sb], writes=[a_sb])

    A.push()
    ld_consts(["v_l1_dtpb"])
    w_dt = [A.alloc(2048, BF16, name=f"wdt{k}") for k in range(16)]
    w_xp = [A.alloc(128, BF16, name=f"wxp{k}") for k in range(16)]
    for k in range(16):
        P.dma("gpsimd", w_dt[k].t, g["w_l1_dtproj"][k * 128:(k + 1) * 128, :],
              writes=[w_dt[k]])
        P.dma("gpsimd", w_xp[k].t, g["w_l1_xproj"][k * 128:(k + 1) * 128, :],
              writes=[w_xp[k]])
    xsb = [A.alloc(TS2, BF16, name=f"xsb{i}") for i in range(16)]
    xs32 = A.alloc(TS2, name="xs32")
    dl32 = A.alloc(TS2, name="dl32")
    sp1 = A.alloc(TS2, name="sp1")
    sp2 = A.alloc(TS2, name="sp2")
    for s2 in range(T // TS2):
        t0 = s2 * TS2
        for k in range(16):
            P.dma("gpsimd", xsb[k].t, xs_fm[k * 128:(k + 1) * 128, t0:t0 + TS2],
                  writes=[xsb[k]])
        bcw_of = {0: bBl1, 64: bCl1}
        for (mo, rows) in [(0, ap_Bl1), (64, ap_Cl1)]:
            bk = pb[5]
            for k in range(16):
                P.mm(bk.t[0:64, 0:TS2], w_xp[k].t[:, mo:mo + 64], xsb[k].t,
                     start=(k == 0), stop=(k == 15),
                     reads=[w_xp[k], xsb[k]], writes=[bk])
            P.cp(rows[:, t0:t0 + TS2], bk.t[0:64, 0:TS2], reads=[bk],
                 writes=[bcw_of[mo]])
        for m in range(16):
            bk = pb[2 + (m % 2)]
            for k in range(16):
                P.mm(bk.t[:, 0:TS2], w_dt[k].t[:, m * 128:(m + 1) * 128],
                     xsb[k].t, start=(k == 0), stop=(k == 15),
                     reads=[w_dt[k], xsb[k]], writes=[bk])
            P.stt(dl32.t, bk.t[:, 0:TS2], 1.0,
                  C["v_l1_dtpb"].t[:, m:m + 1].broadcast_to([128, TS2]),
                  OP.mult, OP.add, reads=[bk, C["v_l1_dtpb"]], writes=[dl32])
            P.act(sp1.t, dl32.t, AF.Abs, reads=[dl32], writes=[sp1])
            P.act(sp1.t, sp1.t, AF.Exp, scale=-1.0, reads=[sp1], writes=[sp1])
            P.act(sp1.t, sp1.t, AF.Ln, bias=1.0, reads=[sp1], writes=[sp1])
            P.op("vector", lambda v, o=sp2.t, a=dl32.t:
                 v.tensor_scalar_max(o, a, 0.0), reads=[dl32], writes=[sp2])
            P.tt(dl32.t, sp1.t, sp2.t, OP.add, reads=[sp1, sp2], writes=[dl32])
            P.dma("scalar", delta_fm[m * 128:(m + 1) * 128, t0:t0 + TS2],
                  dl32.t, reads=[dl32])
            P.dma("sync", xs32.t, xs_fm[m * 128:(m + 1) * 128, t0:t0 + TS2],
                  writes=[xs32])
            P.tt(sp2.t, dl32.t, xs32.t, OP.mult, reads=[dl32, xs32], writes=[sp2])
            P.dma("scalar", dxd_fm[m * 128:(m + 1) * 128, t0:t0 + TS2],
                  sp2.t, reads=[sp2])
    A.pop()
    P.barrier()

    if PHASE_LIMIT <= 6:
        return nc, P, dict(locals())
    TS3 = 512
    A.push()
    bbcb = [A.alloc(TS3, BF16, name=f"bbcb{j}") for j in range(64)]
    y_sb = [A.alloc(TS3, name=f"ysb{d}") for d in range(16)]
    dlc = [A.alloc(TS3, name=f"dlc{j}") for j in range(2)]
    dxc = [A.alloc(TS3, BF16, name=f"dxc{j}") for j in range(2)]
    da2 = [A.alloc(TS3, BF16, name=f"da{j}") for j in range(2)]
    dbx2 = [A.alloc(TS3, BF16, name=f"dbx{j}") for j in range(2)]
    hti2 = [A.alloc(TS3, BF16, name=f"hti{j}") for j in range(2)]
    ytm = [A.alloc(TS3, BF16, name=f"ytm{j}") for j in range(2)]
    gsc = A.alloc(TS3, name="gsc")

    for s3 in range(T // TS3):
        t0 = s3 * TS3
        for nb in range(2):
            for j in range(32):
                n = nb * 32 + j
                bk = pb[j % 2]
                P.mm(bk.t[:, 0:TS3],
                     ap_e64b[:, n:n + 1].broadcast_to([64, 128]),
                     ap_Bl1[:, t0:t0 + TS3], start=True, stop=True,
                     reads=[eye64bf, bBl1], writes=[bk])
                P.cp(bbcb[j].t, bk.t[:, 0:TS3], reads=[bk], writes=[bbcb[j]])
                bk = pb[j % 2]
                P.mm(bk.t[:, 0:TS3],
                     ap_e64b[:, n:n + 1].broadcast_to([64, 128]),
                     ap_Cl1[:, t0:t0 + TS3], start=True, stop=True,
                     reads=[eye64bf, bCl1], writes=[bk])
                P.cp(bbcb[32 + j].t, bk.t[:, 0:TS3], reads=[bk],
                     writes=[bbcb[32 + j]], eng="scalar")
            for dc in range(16):
                dbuf = dc % 2
                P.dma("sync", dlc[dbuf].t,
                      delta_fm[dc * 128:(dc + 1) * 128, t0:t0 + TS3],
                      writes=[dlc[dbuf]])
                P.dma("gpsimd", dxc[dbuf].t,
                      dxd_fm[dc * 128:(dc + 1) * 128, t0:t0 + TS3],
                      writes=[dxc[dbuf]])
                yb = pb[2 + (dc % 2)]
                for j in range(32):
                    n = nb * 32 + j
                    dab = da2[j % 2]
                    P.act(dab.t, dlc[dbuf].t, AF.Exp,
                          scale=a_sb.t[:, dc * 64 + n:dc * 64 + n + 1],
                          reads=[dlc[dbuf], a_sb], writes=[dab])
                    dbb = dbx2[j % 2]
                    P.tt(dbb.t, dxc[dbuf].t, bbcb[j].t, OP.mult,
                         reads=[dxc[dbuf], bbcb[j]], writes=[dbb])
                    if j > 0:
                        pn = nb * 32 + j - 1
                        pstcol = st_sb.t[:, dc * 64 + pn:dc * 64 + pn + 1]
                        ph = hti2[(j - 1) % 2]
                        P.cp(pstcol, ph.t[:, TS3 - 1:TS3], reads=[ph],
                             writes=[st_sb])
                        yt = ytm[(j - 1) % 2]
                        P.tt(yt.t, ph.t, bbcb[32 + j - 1].t, OP.mult,
                             reads=[ph, bbcb[32 + j - 1]], writes=[yt])
                        P.mm(yb.t[:, 0:TS3], ap_e128b, yt.t, start=(j == 1),
                             stop=False, reads=[eye128bf, yt], writes=[yb])
                    stcol = st_sb.t[:, dc * 64 + n:dc * 64 + n + 1]
                    P.scan(hti2[j % 2].t, dab.t, dbb.t, stcol,
                           reads=[dab, dbb, st_sb], writes=[hti2[j % 2]])
                pn = nb * 32 + 31
                pstcol = st_sb.t[:, dc * 64 + pn:dc * 64 + pn + 1]
                ph = hti2[31 % 2]
                P.cp(pstcol, ph.t[:, TS3 - 1:TS3], reads=[ph], writes=[st_sb])
                yt = ytm[31 % 2]
                P.tt(yt.t, ph.t, bbcb[63].t, OP.mult,
                     reads=[ph, bbcb[63]], writes=[yt])
                P.mm(yb.t[:, 0:TS3], ap_e128b, yt.t, start=False, stop=True,
                     reads=[eye128bf, yt], writes=[yb])
                if nb == 0:
                    P.cp(y_sb[dc].t, yb.t[:, 0:TS3], reads=[yb],
                         writes=[y_sb[dc]])
                else:
                    P.tt(y_sb[dc].t, y_sb[dc].t, yb.t[:, 0:TS3], OP.add,
                         reads=[yb, y_sb[dc]], writes=[y_sb[dc]])
        silr = dlc[0]
        ygt_ap = gsc.t.bitcast(BF16)[:, 0:TS3]
        for dc in range(16):
            P.dma("sync", silr.t, res_fm[dc * 128:(dc + 1) * 128, t0:t0 + TS3],
                  writes=[silr])
            P.act(silr.t, silr.t, AF.Silu, reads=[silr], writes=[silr])
            P.tt(ygt_ap, y_sb[dc].t, silr.t, OP.mult,
                 reads=[y_sb[dc], silr], writes=[gsc])
            P.dma("scalar", yg_bf[dc * 128:(dc + 1) * 128, t0:t0 + TS3],
                  ygt_ap, reads=[gsc])
    A.pop()
    A.pop()
    P.barrier()

    if PHASE_LIMIT <= 7:
        return nc, P, dict(locals())
    TS4 = 512
    A.push()
    w_o1 = [A.alloc(1024, BF16, name=f"wo1_{k}") for k in range(16)]
    for k in range(16):
        P.dma("gpsimd", w_o1[k].t, g["w_l1_outproj"][k * 128:(k + 1) * 128, :],
              writes=[w_o1[k]])
    ygc4 = [A.alloc(TS4, BF16, name=f"yg4_{k}") for k in range(16)]
    x2c4 = [A.alloc(TS4, name=f"x2c4_{m}") for m in range(8)]
    for s4 in range(T // TS4):
        t0 = s4 * TS4
        for k in range(16):
            P.dma("sync", ygc4[k].t, yg_bf[k * 128:(k + 1) * 128, t0:t0 + TS4],
                  writes=[ygc4[k]])
        for m in range(8):
            P.dma("sync", x2c4[m].t, x2_fm[m * 128:(m + 1) * 128, t0:t0 + TS4],
                  writes=[x2c4[m]])
            bk = pb[m % 2]
            for k in range(16):
                P.mm(bk.t[:, 0:TS4], w_o1[k].t[:, m * 128:(m + 1) * 128],
                     ygc4[k].t, start=(k == 0), stop=(k == 15),
                     reads=[w_o1[k], ygc4[k]], writes=[bk])
            P.tt(x2c4[m].t, bk.t[:, 0:TS4], x2c4[m].t, OP.add,
                 reads=[bk, x2c4[m]], writes=[x2c4[m]])
            P.dma("scalar", x3_fm[m * 128:(m + 1) * 128, t0:t0 + TS4],
                  x2c4[m].t, reads=[x2c4[m]])
    A.pop()
    P.barrier()

    if PHASE_LIMIT <= 8:
        return nc, P, dict(locals())
    A.push()
    ld_consts(["v_l1f_b1"])
    weL = A.alloc(T, BF16, name="weL")
    ap_weL = weL.t[0:8, :]
    eye8bf = A.alloc(8, BF16, name="eye8bf")
    ap_e8b = eye8bf.t[0:8, 0:8]
    P.cp(ap_e8b, EYE8.t, reads=[EYE8], writes=[eye8bf])
    A.push()
    mk_ln_rows()
    ld_consts(["v_l1f_rb1", "v_l1f_rb2"])
    rb2B = A.alloc(8, name="rb2B")
    rb2B = Buf(rb2B.t[:, 0:8], "rb2B")
    P.mm(pb[0].t[:, 0:8], ONESR.t, C["v_l1f_rb2"].t, start=True,
         stop=True, reads=[ONESR, C["v_l1f_rb2"]], writes=[pb[0]])
    P.cp(rb2B.t, pb[0].t[:, 0:8], reads=[pb[0]], writes=[rb2B])
    w_r1 = [A.alloc(4096, BF16, name=f"wr1_{k}") for k in range(8)]
    w_r2 = [A.alloc(8, BF16, name=f"wr2_{k}") for k in range(32)]
    for k in range(8):
        P.dma("gpsimd", w_r1[k].t, g["w_l1f_rw1"][k * 128:(k + 1) * 128, :],
              writes=[w_r1[k]])
    for k in range(32):
        P.dma("gpsimd", w_r2[k].t, g["w_l1f_rw2"][k * 128:(k + 1) * 128, :],
              writes=[w_r2[k]])
    TS5 = 512
    x3c = [A.alloc(TS5, name=f"x3c{i}") for i in range(8)]
    xfc5 = [A.alloc(TS5, BF16, name=f"xf5_{i}") for i in range(8)]
    hmid = [A.alloc(TS5, BF16, name=f"hm{m}") for m in range(32)]
    tmp5 = A.alloc(TS5, name="tmp5")
    tp5 = A.alloc(64, name="tp5")
    ap_lg = tp5.t[:, 0:8]
    ap_msk1 = tp5.t[:, 8:16]
    ap_msk2 = tp5.t[:, 16:24]
    ap_d = tp5.t[:, 24:32]
    ap_m1 = tp5.t[:, 32:33]
    ap_m2 = tp5.t[:, 33:34]
    ap_g1 = tp5.t[:, 34:35]
    ap_g2 = tp5.t[:, 35:36]

    def top2_we5(lg_psum_buf, lg_psum_ap, wecols):
        P.stt(ap_lg, lg_psum_ap, 1.0, rb2B.t, OP.mult, OP.add,
              reads=[lg_psum_buf, rb2B], writes=[tp5])
        P.red_max(ap_m1, ap_lg, reads=[tp5], writes=[tp5])
        P.stt(ap_d, ap_lg, 1.0, tp5.t[:, 32:33].broadcast_to([128, 8]),
              OP.mult, OP.subtract, reads=[tp5], writes=[tp5])
        P.act(ap_d, ap_d, AF.Sign, reads=[tp5], writes=[tp5])
        P.tt(ap_msk1, ap_d, ap_d, OP.mult, reads=[tp5], writes=[tp5])
        P.ts(ap_msk1, ap_msk1, -1.0, 1.0, OP.mult, OP.add, reads=[tp5],
             writes=[tp5])
        P.stt(ap_d, ap_msk1, -1e30, ap_lg, OP.mult, OP.add, reads=[tp5],
              writes=[tp5])
        P.red_max(ap_m2, ap_d, reads=[tp5], writes=[tp5])
        P.stt(ap_d, ap_d, 1.0, tp5.t[:, 33:34].broadcast_to([128, 8]),
              OP.mult, OP.subtract, reads=[tp5], writes=[tp5])
        P.act(ap_d, ap_d, AF.Sign, reads=[tp5], writes=[tp5])
        P.tt(ap_msk2, ap_d, ap_d, OP.mult, reads=[tp5], writes=[tp5])
        P.ts(ap_msk2, ap_msk2, -1.0, 1.0, OP.mult, OP.add, reads=[tp5],
             writes=[tp5])
        P.tt(ap_d[:, 0:1], ap_m1, ap_m2, OP.subtract, reads=[tp5], writes=[tp5])
        P.act(ap_g1, ap_d[:, 0:1], AF.Sigmoid, reads=[tp5], writes=[tp5])
        P.ts(ap_g2, ap_g1, -1.0, 1.0, OP.mult, OP.add, reads=[tp5], writes=[tp5])
        P.ts(ap_msk1, ap_msk1, tp5.t[:, 34:35], None, OP.mult, reads=[tp5],
             writes=[tp5])
        P.stt(ap_msk2, ap_msk2, tp5.t[:, 35:36], ap_msk1, OP.mult, OP.add,
              reads=[tp5], writes=[tp5])
        P.transpose(pb[7].t[0:8, 0:128], ap_msk2, EYE.t, reads=[tp5, EYE],
                    writes=[pb[7]])
        P.cp(ap_weL[:, wecols], pb[7].t[0:8, 0:128], reads=[pb[7]],
             writes=[weL])

    for s5 in range(T // TS5):
        t0 = s5 * TS5
        for i in range(8):
            P.dma("sync", x3c[i].t, x3_fm[i * 128:(i + 1) * 128, t0:t0 + TS5],
                  writes=[x3c[i]])

        def outr(i, mB, rB):
            P.tt(tmp5.t, x3c[i].t, mB.t[:, 0:TS5], OP.subtract,
                 reads=[x3c[i], mB], writes=[tmp5])
            P.tt(tmp5.t, tmp5.t, rB.t[:, 0:TS5], OP.mult,
                 reads=[tmp5, rB], writes=[tmp5])
            P.cp(xfc5[i].t, tmp5.t, reads=[tmp5], writes=[xfc5[i]])
            P.dma("scalar", xf_bfs[i * 128:(i + 1) * 128, t0:t0 + TS5],
                  xfc5[i].t, reads=[xfc5[i]])
        ln_feature_major(x3c, [b.t for b in x3c], TS5, 1e-5, tmp5,
                         (pb[0], pb[1]), outr)
        for m in range(32):
            bk = pb[2 + (m % 2)]
            for k in range(8):
                P.mm(bk.t[:, 0:TS5], w_r1[k].t[:, m * 128:(m + 1) * 128],
                     xfc5[k].t, start=(k == 0), stop=(k == 7),
                     reads=[w_r1[k], xfc5[k]], writes=[bk])
            P.act(hmid[m].t, bk.t[:, 0:TS5], AF.Silu,
                  bias=C["v_l1f_rb1"].t[:, m:m + 1],
                  reads=[bk, C["v_l1f_rb1"]], writes=[hmid[m]])
        for tc in range(4):
            tokap = slice(tc * 128, (tc + 1) * 128)
            for k in range(32):
                P.mm(pb[6].t[:, 0:8], hmid[k].t[:, tokap], w_r2[k].t,
                     start=(k == 0), stop=(k == 31),
                     reads=[hmid[k], w_r2[k]], writes=[pb[6]])
            top2_we5(pb[6], pb[6].t[:, 0:8],
                     slice(t0 + tc * 128, t0 + (tc + 1) * 128))
    P.dma("scalar", we_sc, ap_weL, reads=[weL])
    A.pop()
    P.barrier()

    if PHASE_LIMIT <= 9:
        return nc, P, dict(locals())
    A.push()
    w1e = [A.alloc(4096, BF16, name=f"w1e{k}") for k in range(8)]
    w2e = [A.alloc(1024, BF16, name=f"w2e{k}") for k in range(32)]
    TSE = 512
    xfe = [A.alloc(TSE, BF16, name=f"xfe{k}") for k in range(8)]
    hstore = [A.alloc(TSE, BF16, name=f"hst{m}") for m in range(32)]
    wcb5 = A.alloc(TSE, BF16, name="wcb5")
    oev = [A.alloc(TSE, BF16, name=f"oev{j}") for j in range(2)]
    for e in range(8):
        for k in range(8):
            P.dma("gpsimd", w1e[k].t, g["w_l1f_w1"][e, k * 128:(k + 1) * 128, :],
                  writes=[w1e[k]])
        for k in range(32):
            P.dma("gpsimd", w2e[k].t, g["w_l1f_w2"][e, k * 128:(k + 1) * 128, :],
                  writes=[w2e[k]])
        for se in range(T // TSE):
            t0 = se * TSE
            for k in range(8):
                P.dma("sync", xfe[k].t,
                      xf_bfs[k * 128:(k + 1) * 128, t0:t0 + TSE],
                      writes=[xfe[k]])
            P.mm(pb[0].t[:, 0:TSE],
                 ap_e8b[:, e:e + 1].broadcast_to([8, 128]),
                 ap_weL[:, t0:t0 + TSE], start=True, stop=True,
                 reads=[eye8bf, weL], writes=[pb[0]])
            P.cp(wcb5.t, pb[0].t[:, 0:TSE], reads=[pb[0]], writes=[wcb5])

            def h_mm(m):
                bk = pb[2 + (m % 2)]
                for k in range(8):
                    last = (k == 7)
                    P.mm(bk.t[:, 0:TSE], w1e[k].t[:, m * 128:(m + 1) * 128],
                         xfe[k].t, start=(k == 0), stop=last,
                         reads=(w1e + xfe) if last else [w1e[k], xfe[k]],
                         writes=[bk], track=last)
            h_mm(0)
            for m in range(32):
                bk = pb[2 + (m % 2)]
                P.act(hstore[m].t, bk.t[:, 0:TSE], AF.Silu,
                      bias=g_b1_col(C, m, e),
                      reads=[bk, C["v_l1f_b1"]], writes=[hstore[m]])
                if m + 1 < 32:
                    h_mm(m + 1)
            for half in range(2):
                for m in range(32):
                    for mp4 in range(4):
                        mp = half * 4 + mp4
                        ob = pb[4 + mp4]
                        last = (m == 31)
                        full = last and (mp4 == 3)
                        P.mm(ob.t[:, 0:TSE],
                             w2e[m].t[:, mp * 128:(mp + 1) * 128],
                             hstore[m].t, start=(m == 0), stop=last,
                             reads=(w2e + hstore) if full
                             else [w2e[m], hstore[m]],
                             writes=[ob], track=last)
                for mp4 in range(4):
                    mp = half * 4 + mp4
                    ob = pb[4 + mp4]
                    ov = oev[mp4 % 2]
                    P.tt(ov.t, ob.t[:, 0:TSE], wcb5.t, OP.mult,
                         reads=[ob, wcb5], writes=[ov])
                    P.dma("scalar",
                          oparts[e, mp * 128:(mp + 1) * 128, t0:t0 + TSE],
                          ov.t, reads=[ov])
    A.pop()
    A.pop()
    P.barrier()

    if PHASE_LIMIT <= 10:
        return nc, P, dict(locals())
    A.push()
    TSR = 512
    accp = [A.alloc(TSR, name=f"accp{j}") for j in range(8)]
    x3r = A.alloc(TSR, name="x3r")
    b2l = A.alloc(1024, name="b2l")
    ap_b2 = b2l.t[0:8, 0:1024]
    P.dma("sync", ap_b2, g["w_l1f_b2"], writes=[b2l])
    wer = A.alloc(TSR, name="wer")
    ap_wer = wer.t[0:8, :]
    for sr in range(T // TSR):
        t0 = sr * TSR
        P.dma("gpsimd", ap_wer, we_sc[:, t0:t0 + TSR], writes=[wer])
        for mp in range(8):
            bk = pb[mp % 2]
            P.mm(bk.t[:, 0:TSR], ap_b2[:, mp * 128:(mp + 1) * 128], ap_wer,
                 start=True, stop=True, reads=[b2l, wer], writes=[bk])
            P.dma("sync", x3r.t, x3_fm[mp * 128:(mp + 1) * 128, t0:t0 + TSR],
                  writes=[x3r])
            for e in range(8):
                P.dma("gpsimd", accp[e].t,
                      oparts[e, mp * 128:(mp + 1) * 128, t0:t0 + TSR],
                      writes=[accp[e]])
            for e in range(1, 8):
                P.tt(accp[0].t, accp[0].t, accp[e].t, OP.add,
                     reads=[accp[0], accp[e]], writes=[accp[0]])
            P.tt(x3r.t, x3r.t, accp[0].t, OP.add, reads=[x3r, accp[0]],
                 writes=[x3r])
            P.tt(x3r.t, x3r.t, bk.t[:, 0:TSR], OP.add, reads=[x3r, bk],
                 writes=[x3r])
            P.dma("scalar", out_fm[mp * 128:(mp + 1) * 128, t0:t0 + TSR],
                  x3r.t, reads=[x3r])
    A.pop()
    P.barrier()
    return nc, P, dict(locals())


def g_b1_col(C, m, e):
    return C["v_l1f_b1"].t[:, m * 8 + e:m * 8 + e + 1]


def finish_program(nc, P, env):
    with nc.Block() as block:
        P.finalize(block)
    return nc


_CACHE = {}


def kernel(**inputs):
    core_map = prep_inputs(inputs)
    key = "prog"
    if key not in _CACHE:
        nc, P, env = build_program()
        finish_program(nc, P, env)
        _CACHE[key] = (nc, env)
    nc, env = _CACHE[key]
    in_maps = [core_map(b) for b in range(NCORES)]
    res = run_bass_kernel_spmd(nc, in_maps, core_ids=list(range(NCORES)))
    out = np.stack([np.ascontiguousarray(np.asarray(r["out_fm"]).T)
                    for r in res.results])
    return out.astype(np.float32)


# version 04
# speedup vs baseline: 1.9101x; 1.0538x over previous
import numpy as np
import ml_dtypes
import concourse.bass as bass
import concourse.mybir as mybir
from concourse.bass_utils import run_bass_kernel_spmd
from concourse.alu_op_type import AluOpType as OP

F32 = mybir.dt.float32
BF16 = mybir.dt.bfloat16
AF = mybir.ActivationFunctionType
BFNP = ml_dtypes.bfloat16

T = 4096
D = 1024
NCORES = 8
MAGIC = float(1.5 * 2 ** 23)
PI = float(np.pi)

DEBUG_DUMP = set()
PHASE_LIMIT = 99


class Buf:
    __slots__ = ("t", "wr", "rd", "name")

    def __init__(self, ap, name=""):
        self.t = ap
        self.wr = None
        self.rd = []
        self.name = name

    def __getitem__(self, key):
        return self.t[key]


class Prog:
    ENG = ("tensor", "vector", "scalar", "gpsimd", "sync")

    def __init__(self, nc, stack):
        self.nc = nc
        self.q = {e: [] for e in self.ENG}
        self.sems = {}
        for name in ("T", "V", "A", "G", "DS", "DG", "DA",
                     "FS", "FG", "FA"):
            self.sems[name] = stack.enter_context(nc.semaphore("sem" + name))
        self.cnt = {k: 0 for k in self.sems}
        self.opidx = {e: 0 for e in self.ENG}
        self.val2idx = {k: {} for k in self.sems}
        self.waited = {}
        self.csem = {"tensor": "T", "vector": "V", "scalar": "A",
                     "gpsimd": "G", "sync": None}
        self.dsem = {"sync": "DS", "gpsimd": "DG", "scalar": "DA"}
        self.fsem = {"sync": "FS", "gpsimd": "FG", "scalar": "FA"}
        self.sem_eng = {"T": "tensor", "V": "vector", "A": "scalar",
                        "G": "gpsimd", "DS": "sync", "DG": "gpsimd",
                        "DA": "scalar", "FS": "sync", "FG": "gpsimd",
                        "FA": "scalar"}
        self.dma_issue = {"sync": 0, "gpsimd": 0, "scalar": 0}
        self.flushed_issue = {"sync": 0, "gpsimd": 0, "scalar": 0}

    def _flag_for(self, queue, idx):
        fname = self.fsem[queue]
        if self.flushed_issue[queue] < idx:
            dname = self.dsem[queue]
            need = 16 * self.dma_issue[queue]
            self.cnt[fname] += 1
            fval = self.cnt[fname]
            sems = self.sems

            def runner(eng, dn=dname, nv=need, fs=fname, fv=fval):
                eng.wait_ge(sems[dn], nv)
                eng.nop().then_inc(sems[fs], 1)

            self.q[queue].append(runner)
            self.flushed_issue[queue] = self.dma_issue[queue]
            self.opidx[queue] += 1
            self.val2idx[fname][fval] = self.opidx[queue] - 1
        return (fname, self.cnt[fname])

    def _emit(self, engine, fn, deps, dma):
        sem_name = self.dsem[engine] if dma else self.csem[engine]
        assert sem_name is not None, f"{engine} cannot emit dma={dma}"
        step = 16 if dma else 1
        deps2 = []
        for (dsname, dval) in deps:
            if dval is None:
                continue
            if dsname.startswith("Q"):
                queue = dsname[1:]
                if engine == queue:
                    deps2.append(self._flag_for(queue, dval))
                else:
                    deps2.append(self._flag_for(queue, dval))
            else:
                deps2.append((dsname, dval))
        deps = deps2
        myidx = self.opidx[engine]
        waits = []
        need_drain = False
        for (dsname, dval) in deps:
            if dval is None:
                continue
            if dsname == sem_name and not dma and self.sem_eng[dsname] == engine:
                pidx = self.val2idx[dsname].get(dval, -10)
                if pidx >= myidx - 1:
                    need_drain = True
                continue
            if self.sem_eng[dsname] == engine and not (dsname in ("DS", "DG", "DA")):
                pidx = self.val2idx[dsname].get(dval, -10)
                if pidx >= myidx - 1:
                    need_drain = True
                continue
            key = (engine, dsname)
            if self.waited.get(key, -1) >= dval:
                continue
            self.waited[key] = dval
            waits.append((dsname, dval))
        self.cnt[sem_name] += step
        if dma:
            self.dma_issue[engine] += 1
            token = ("Q" + engine, self.dma_issue[engine])
        else:
            token = (sem_name, self.cnt[sem_name])
        self.val2idx[sem_name][self.cnt[sem_name]] = myidx
        self.opidx[engine] += 1
        sems = self.sems

        def runner(eng, waits=waits, need_drain=need_drain, fn=fn,
                   sem=sems[sem_name], step=step):
            for (wn, wv) in waits:
                eng.wait_ge(sems[wn], wv)
            if need_drain and hasattr(eng, "drain"):
                eng.drain()
            ins = fn(eng)
            ins.then_inc(sem, step)

        self.q[engine].append(runner)
        return token

    def op(self, engine, fn, reads=(), writes=(), deps=(), dma=False,
           track=True):
        dl = list(deps)
        for b in reads:
            if b.wr is not None:
                dl.append(b.wr)
        for b in writes:
            if b.wr is not None:
                dl.append(b.wr)
            dl.extend(b.rd)
        if not track:
            self._emit_notrack(engine, fn, dl)
            return None
        tok = self._emit(engine, fn, dl, dma)
        for b in reads:
            b.rd.append(tok)
        for b in writes:
            b.wr = tok
            b.rd = []
        return tok

    def _emit_notrack(self, engine, fn, deps):
        deps2 = []
        for (dsname, dval) in deps:
            if dval is None:
                continue
            if dsname.startswith("Q"):
                deps2.append(self._flag_for(dsname[1:], dval))
            else:
                deps2.append((dsname, dval))
        waits = []
        for (dsname, dval) in deps2:
            if self.sem_eng[dsname] == engine and dsname not in ("DS", "DG", "DA"):
                continue
            key = (engine, dsname)
            if self.waited.get(key, -1) >= dval:
                continue
            self.waited[key] = dval
            waits.append((dsname, dval))
        self.opidx[engine] += 1
        sems = self.sems

        def runner(eng, waits=waits, fn=fn):
            for (wn, wv) in waits:
                eng.wait_ge(sems[wn], wv)
            fn(eng)

        self.q[engine].append(runner)

    def barrier(self):
        snap = dict(self.cnt)
        for e in self.ENG:
            waits = []
            for sn, sv in snap.items():
                if sv == 0:
                    continue
                if self.sem_eng[sn] == e and sn not in ("DS", "DG", "DA"):
                    continue
                key = (e, sn)
                if self.waited.get(key, -1) >= sv:
                    continue
                self.waited[key] = sv
                waits.append((sn, sv))
            if waits:
                sems = self.sems

                def runner(eng, waits=waits):
                    for (wn, wv) in waits:
                        eng.wait_ge(sems[wn], wv)

                self.q[e].append(runner)

    def finalize(self, block):
        for e in self.ENG:
            ops = self.q[e]

            def body(eng, ops=ops):
                for r in ops:
                    r(eng)

            getattr(block, e)(body)

    def mm(self, out_ap, lhsT, rhs, start, stop, reads=(), writes=(), deps=(),
           track=True):
        return self.op("tensor",
                       lambda t: t.matmul(out_ap, lhsT, rhs, start=start, stop=stop),
                       reads=reads, writes=writes, deps=deps, track=track)

    def transpose(self, out_ap, in_ap, ident_ap, reads=(), writes=(), deps=()):
        return self.op("tensor",
                       lambda t: t.transpose(out_ap, in_ap, ident_ap),
                       reads=reads, writes=writes, deps=deps)

    def cp(self, out_ap, in_ap, reads=(), writes=(), deps=(), eng="vector"):
        if eng == "scalar":
            return self.op(eng, lambda s: s.copy(out_ap, in_ap),
                           reads=reads, writes=writes, deps=deps)
        return self.op(eng, lambda v: v.tensor_copy(out_ap, in_ap),
                       reads=reads, writes=writes, deps=deps)

    def tt(self, out_ap, a, b, opk, reads=(), writes=(), deps=(), eng="vector"):
        return self.op(eng, lambda v: v.tensor_tensor(out_ap, a, b, opk),
                       reads=reads, writes=writes, deps=deps)

    def ts(self, out_ap, in_ap, s1, s2, op0, op1=None, reads=(), writes=(),
           deps=(), eng="vector"):
        if op1 is None:
            return self.op(eng, lambda v: v.tensor_scalar(out_ap, in_ap, s1, None, op0),
                           reads=reads, writes=writes, deps=deps)
        return self.op(eng,
                       lambda v: v.tensor_scalar(out_ap, in_ap, s1, s2, op0, op1),
                       reads=reads, writes=writes, deps=deps)

    def stt(self, out_ap, in0, scalar, in1, op0, op1, reads=(), writes=(),
            deps=(), eng="vector"):
        return self.op(eng,
                       lambda v: v.scalar_tensor_tensor(out_ap, in0, scalar, in1, op0, op1),
                       reads=reads, writes=writes, deps=deps)

    def act(self, out_ap, in_ap, func, bias=None, scale=None, reads=(),
            writes=(), deps=()):
        kw = {}
        if bias is not None:
            kw["bias"] = bias
        if scale is not None:
            kw["scale"] = scale
        return self.op("scalar",
                       lambda s: s.activation(out_ap, in_ap, func, **kw),
                       reads=reads, writes=writes, deps=deps)

    def scan(self, out_ap, d0, d1, init, reads=(), writes=(), deps=()):
        return self.op("vector",
                       lambda v: v.tensor_tensor_scan(out_ap, d0, d1, init,
                                                      OP.mult, OP.add),
                       reads=reads, writes=writes, deps=deps)

    def red_max(self, out_ap, in_ap, reads=(), writes=(), deps=()):
        return self.op("vector",
                       lambda v: v.reduce_max(out_ap, in_ap,
                                              axis=mybir.AxisListType.X),
                       reads=reads, writes=writes, deps=deps)

    def recip(self, out_ap, in_ap, reads=(), writes=(), deps=()):
        return self.op("vector", lambda v: v.reciprocal(out_ap, in_ap),
                       reads=reads, writes=writes, deps=deps)

    def memset(self, ap, val, reads=(), writes=(), deps=(), eng="vector"):
        return self.op(eng, lambda v: v.memset(ap, val),
                       reads=reads, writes=writes, deps=deps)

    def dma(self, queue, out_ap, in_ap, reads=(), writes=(), deps=()):
        return self.op(queue, lambda q: q.dma_start(out=out_ap, in_=in_ap),
                       reads=reads, writes=writes, deps=deps, dma=True)


class Arena:

    def __init__(self, ap, words):
        self.ap = ap
        self.words = words
        self.off = 0
        self.marks = []

    def alloc(self, words, dtype=F32, parts=128, name=""):
        w32 = words if dtype == F32 else (words + 1) // 2
        w32 = (w32 + 1) & ~1
        assert self.off + w32 <= self.words, \
            f"arena overflow: {self.off}+{w32} > {self.words} ({name})"
        sl = self.ap[0:parts, self.off:self.off + w32]
        self.off += w32
        if dtype != F32:
            sl = sl.bitcast(dtype)
        return Buf(sl, name)

    def push(self):
        self.marks.append(self.off)

    def pop(self):
        self.off = self.marks.pop()


def prep_inputs(ws):
    f32 = lambda a: np.ascontiguousarray(a, dtype=np.float32)
    sh = {}
    sh["c_eye128"] = np.eye(128, dtype=np.float32)
    sh["c_eye64"] = np.eye(64, dtype=np.float32)
    sh["c_eye16"] = np.eye(16, dtype=np.float32)
    sh["c_eye32"] = np.eye(32, dtype=np.float32)
    sh["c_eye8"] = np.eye(8, dtype=np.float32)
    sh["c_ones_col"] = np.ones((128, 1), dtype=np.float32)
    sh["c_ones_row"] = np.ones((1, 128), dtype=np.float32)
    q = np.arange(128)
    sh["c_utri"] = np.where(q[:, None] <= q[None, :], 0.0, -1e5).astype(np.float32)
    sh["c_strict"] = (q[:, None] < q[None, :]).astype(np.float32)
    sh["w_l0_inproj"] = f32(ws["L0_attn_in_proj_w"])
    sh["w_l0_outproj"] = f32(ws["L0_attn_out_proj_w"])
    sh["w_l0_theta"] = f32(ws["L0_attn_theta_w"])
    sh["v_l0_ln_s"] = f32(ws["L0_attn_norm_scale"]).reshape(8, 128).T.copy()
    sh["v_l0_ln_b"] = f32(ws["L0_attn_norm_bias"]).reshape(8, 128).T.copy()
    sh["v_l0_alog"] = f32(ws["L0_attn_A_log"]).reshape(16, 1)
    sh["v_l0_dtb"] = f32(ws["L0_attn_dt_bias"]).reshape(16, 1)
    sh["v_l0_bcB"] = f32(ws["L0_attn_bcB_scale"]).reshape(64, 1)
    sh["v_l0_bcC"] = f32(ws["L0_attn_bcC_scale"]).reshape(64, 1)
    sh["v_l0_Bb"] = f32(ws["L0_attn_B_bias"]).reshape(64, 1)
    sh["v_l0_Cb"] = f32(ws["L0_attn_C_bias"]).reshape(64, 1)
    sh["v_l0_D"] = f32(ws["L0_attn_D"]).reshape(1, 16)
    sh["w_l0f_gate"] = f32(ws["L0_ffn_w_gate"])
    sh["w_l0f_Wd"] = f32(ws["L0_ffn_Wd"])
    sh["w_l0f_Wu"] = f32(ws["L0_ffn_Wu"])
    sh["v_l0f_bd"] = f32(ws["L0_ffn_bd"]).T.copy()
    sh["w_l0f_bu"] = f32(ws["L0_ffn_bu"])
    sh["w_l1_inproj"] = f32(ws["L1_attn_in_proj_w"].T)
    sh["w_l1_xproj"] = f32(ws["L1_attn_x_proj_w"].T)
    sh["w_l1_dtproj"] = f32(ws["L1_attn_dt_proj_w"].T)
    sh["w_l1_outproj"] = f32(ws["L1_attn_out_proj_w"].T)
    cw = f32(ws["L1_attn_conv_w"]).reshape(2048, 4)
    sh["v_l1_convw"] = cw.reshape(16, 128, 4).transpose(1, 0, 2).reshape(128, 64).copy()
    sh["v_l1_convb"] = f32(ws["L1_attn_conv_b"]).reshape(16, 128).T.copy()
    sh["v_l1_dtpb"] = f32(ws["L1_attn_dt_proj_b"]).reshape(16, 128).T.copy()
    sh["w_l1_alog"] = f32(ws["L1_attn_A_log"])
    sh["w_l1f_rw1"] = f32(ws["L1_ffn_rw1"])
    sh["w_l1f_rw2"] = f32(ws["L1_ffn_rw2"])
    sh["v_l1f_rb1"] = f32(ws["L1_ffn_rb1"]).reshape(32, 128).T.copy()
    sh["v_l1f_rb2"] = f32(ws["L1_ffn_rb2"]).reshape(1, 8)
    sh["w_l1f_w1"] = f32(ws["L1_ffn_w1"])
    sh["w_l1f_w2"] = f32(ws["L1_ffn_w2"])
    sh["v_l1f_b1"] = f32(ws["L1_ffn_b1"]).T.reshape(32, 128, 8).transpose(1, 0, 2).reshape(128, 256).copy()
    sh["w_l1f_b2"] = f32(ws["L1_ffn_b2"])

    u = np.asarray(ws["L0_attn_u"], dtype=np.float32)

    def core_map(b):
        m = dict(sh)
        m["u_fm"] = np.ascontiguousarray(u[b].T)
        return m

    return core_map


def build_program():
    import contextlib
    nc = bass.Bass("TRN2", target_bir_lowering=False)
    stack = contextlib.ExitStack()

    def ext_in(name, shape, dt=F32):
        return nc.dram_tensor(name, shape, dt, kind="ExternalInput").ap()

    def scratch(name, shape, dt=F32):
        kind = "ExternalOutput" if name in DEBUG_DUMP else "Internal"
        return nc.dram_tensor(name, shape, dt, kind=kind).ap()

    u_fm = ext_in("u_fm", [D, T])
    out_fm = nc.dram_tensor("out_fm", [D, T], F32, kind="ExternalOutput").ap()

    g = {}
    for name, shape in [
        ("c_eye128", [128, 128]), ("c_eye64", [64, 64]), ("c_eye16", [16, 16]),
        ("c_eye32", [32, 32]), ("c_eye8", [8, 8]), ("c_ones_col", [128, 1]), ("c_ones_row", [1, 128]),
        ("c_utri", [128, 128]), ("c_strict", [128, 128]),
        ("w_l0_inproj", [1024, 4256]), ("w_l0_outproj", [2048, 1024]),
        ("w_l0_theta", [1024, 32]),
        ("v_l0_ln_s", [128, 8]), ("v_l0_ln_b", [128, 8]),
        ("v_l0_alog", [16, 1]), ("v_l0_dtb", [16, 1]),
        ("v_l0_bcB", [64, 1]), ("v_l0_bcC", [64, 1]),
        ("v_l0_Bb", [64, 1]), ("v_l0_Cb", [64, 1]), ("v_l0_D", [1, 16]),
        ("w_l0f_gate", [1024, 8]), ("w_l0f_Wd", [8, 1024, 64]),
        ("w_l0f_Wu", [8, 64, 1024]), ("v_l0f_bd", [64, 8]), ("w_l0f_bu", [8, 1024]),
        ("w_l1_inproj", [1024, 4096]), ("w_l1_xproj", [2048, 128]),
        ("w_l1_dtproj", [2048, 2048]), ("w_l1_outproj", [2048, 1024]),
        ("v_l1_convw", [128, 64]), ("v_l1_convb", [128, 16]),
        ("v_l1_dtpb", [128, 16]), ("w_l1_alog", [2048, 64]),
        ("w_l1f_rw1", [1024, 4096]), ("w_l1f_rw2", [4096, 8]),
        ("v_l1f_rb1", [128, 32]), ("v_l1f_rb2", [1, 8]),
        ("w_l1f_w1", [8, 1024, 4096]), ("w_l1f_w2", [8, 4096, 1024]),
        ("v_l1f_b1", [128, 256]), ("w_l1f_b2", [8, 1024]),
    ]:
        g[name] = ext_in(name, shape)

    x1_fm = scratch("x1_fm", [D, T])
    x2_fm = scratch("x2_fm", [D, T])
    x3_fm = scratch("x3_fm", [D, T])
    xin_bf = scratch("xin_bf", [D, T], BF16)
    xu_fm = scratch("xu_fm", [D, T])
    y0_bf = scratch("y0_bf", [2048, T], BF16)
    xs_fm = scratch("xs_fm", [2048, T])
    res_fm = scratch("res_fm", [2048, T])
    delta_fm = scratch("delta_fm", [2048, T])
    dxd_fm = scratch("dxd_fm", [2048, T])
    yg_bf = scratch("yg_bf", [2048, T], BF16)
    xf_bfs = scratch("xf_bfs", [D, T], BF16)
    oparts = scratch("oparts", [8, D, T], BF16)
    we_sc = scratch("we_sc", [8, T], BF16)

    ARENA_WORDS = 47104
    arena_t = stack.enter_context(nc.sbuf_tensor("arena", [128, ARENA_WORDS], F32))
    psum = [stack.enter_context(nc.psum_tensor(f"pb{i}", [128, 512], F32))
            for i in range(8)]
    pb = [Buf(p[:], f"pb{i}") for i, p in enumerate(psum)]

    P = Prog(nc, stack)
    A = Arena(arena_t[:], ARENA_WORDS)

    C = {}
    CONST_SHAPES = {
        "c_eye128": [128, 128], "c_eye64": [64, 64], "c_eye16": [16, 16],
        "c_eye32": [32, 32], "c_eye8": [8, 8],
        "c_ones_col": [128, 1], "c_ones_row": [1, 128],
        "c_utri": [128, 128], "c_strict": [128, 128],
        "v_l0_ln_s": [128, 8], "v_l0_ln_b": [128, 8],
        "v_l0_alog": [16, 1], "v_l0_dtb": [16, 1],
        "v_l0_bcB": [64, 1], "v_l0_bcC": [64, 1],
        "v_l0_Bb": [64, 1], "v_l0_Cb": [64, 1], "v_l0_D": [1, 16],
        "v_l0f_bd": [64, 8],
        "v_l1_convw": [128, 64], "v_l1_convb": [128, 16],
        "v_l1_dtpb": [128, 16],
        "v_l1f_rb1": [128, 32], "v_l1f_rb2": [1, 8],
        "v_l1f_b1": [128, 256],
    }

    def ld_consts(names):
        for name in names:
            shape = CONST_SHAPES[name]
            b = A.alloc(shape[1], F32, name=name)
            C[name] = Buf(b.t[0:shape[0], 0:shape[1]], name)
            P.dma("sync", C[name].t, g[name], writes=[C[name]])

    ld_consts(["c_eye128", "c_eye64", "c_eye16", "c_eye32", "c_eye8",
               "c_ones_col", "c_ones_row"])
    ln_state = {}

    def mk_ln_rows():
        ln_state["buf"] = A.alloc(1024, name="ln_rows")

    EYE = C["c_eye128"]
    EYE64 = C["c_eye64"]
    EYE16 = C["c_eye16"]
    EYE32 = C["c_eye32"]
    EYE8 = C["c_eye8"]
    ONESC = C["c_ones_col"]
    ONESR = C["c_ones_row"]

    def ln_feature_major(xbufs, xaps, tc, eps, tmp, rows2, out_cb,
                         nchunks=8, dsize=1024.0):
        ln_rows = ln_state["buf"]
        bs, br = rows2
        for i in range(nchunks):
            P.mm(bs.t[0:1, 0:tc], ONESC.t, xaps[i], start=(i == 0),
                 stop=(i == nchunks - 1), reads=[xbufs[i], ONESC], writes=[bs])
        for i in range(nchunks):
            P.act(tmp.t[:, 0:tc], xaps[i], AF.Square, reads=[xbufs[i]],
                  writes=[tmp])
            P.mm(br.t[0:1, 0:tc], ONESC.t, tmp.t[:, 0:tc], start=(i == 0),
                 stop=(i == nchunks - 1), reads=[tmp, ONESC], writes=[br])
        mrow = ln_rows.t[0:1, 0:tc]
        rrow = ln_rows.t[0:1, 512:512 + tc]
        t1 = pb[2].t[0:1, 0:tc]
        P.ts(mrow, bs.t[0:1, 0:tc], 1.0 / dsize, None, OP.mult,
             reads=[bs], writes=[ln_rows])
        P.ts(rrow, br.t[0:1, 0:tc], 1.0 / dsize, None, OP.mult,
             reads=[br], writes=[ln_rows])
        P.tt(t1, mrow, mrow, OP.mult, reads=[ln_rows], writes=[pb[2]])
        P.tt(rrow, rrow, t1, OP.subtract, reads=[ln_rows, pb[2]],
             writes=[ln_rows])
        P.ts(rrow, rrow, eps, None, OP.add, reads=[ln_rows], writes=[ln_rows])
        P.recip(rrow, rrow, reads=[ln_rows], writes=[ln_rows])
        P.act(rrow, rrow, AF.Sqrt, reads=[ln_rows], writes=[ln_rows])
        P.mm(bs.t[:, 0:tc], ONESR.t, mrow, start=True, stop=True,
             reads=[ln_rows, ONESR], writes=[bs])
        P.mm(br.t[:, 0:tc], ONESR.t, rrow, start=True, stop=True,
             reads=[ln_rows, ONESR], writes=[br])
        for i in range(nchunks):
            out_cb(i, bs, br)

    NCH = T // 128

    A.push()
    ld_consts(["c_utri", "c_strict", "v_l0_ln_s", "v_l0_ln_b", "v_l0_alog",
               "v_l0_dtb", "v_l0_bcB", "v_l0_bcC", "v_l0_Bb", "v_l0_Cb",
               "v_l0_D"])
    UTRI = C["c_utri"]
    STRICT = C["c_strict"]
    A16 = A.alloc(2, F32, name="A16")
    A16 = Buf(A16.t[0:16, 0:1], "A16")
    P.act(A16.t, C["v_l0_alog"].t, AF.Exp, reads=[C["v_l0_alog"]], writes=[A16])
    P.ts(A16.t, A16.t, -1.0, None, OP.mult, reads=[A16], writes=[A16])
    w0 = A.alloc(T, name="cc_fm")
    ap_cc = w0.t[0:16, :]
    wex = A.alloc(T, name="expcc_fm")
    ap_expcc = wex.t[0:16, :]
    wB1 = A.alloc(T, name="B1_fm")
    ap_B1 = wB1.t[0:64, :]
    w1 = A.alloc(T, name="C1_fm")
    ap_C1 = w1.t[0:64, :]
    expic0 = A.alloc(32, name="expic0")
    ap_expic0 = expic0.t[0:16, 0:32]
    colsT = {nm: A.alloc(NCH * 16, name="colsT_" + nm)
             for nm in ("negcc", "u", "w", "exprc")}
    B_tm = A.alloc(NCH * 64, name="B_tm")
    DI = A.alloc(16 * 128, BF16, name="DI")
    phic = A.alloc(2, name="phic")
    ap_phic = phic.t[0:32, 0:1]
    P.memset(ap_phic, 0.0, writes=[phic])

    for h in range(16):
        P.mm(pb[6].t[:, 0:1], ONESR.t, C["v_l0_D"].t[0:1, h:h + 1], start=True,
             stop=True, reads=[C["v_l0_D"], ONESR], writes=[pb[6]])
        P.ts(DI.t[:, h * 128:(h + 1) * 128], EYE.t, pb[6].t[:, 0:1],
             None, OP.mult, reads=[EYE, pb[6]], writes=[DI])

    if PHASE_LIMIT <= 0:
        return nc, P, dict(locals())
    TSL = 256
    NSL = T // TSL
    A.push()
    mk_ln_rows()
    x8 = [A.alloc(TSL, name=f"x8_{i}") for i in range(8)]
    u8 = [A.alloc(TSL, name=f"u8_{i}") for i in range(8)]
    tmpc = A.alloc(TSL, name="tmpc")
    xin8 = [A.alloc(TSL, BF16, name=f"xin_{i}") for i in range(8)]
    wsm = [A.alloc(160, BF16, name=f"wsm{k}") for k in range(8)]
    wth = [A.alloc(32, name=f"wth{k}") for k in range(8)]
    for k in range(8):
        P.dma("gpsimd", wsm[k].t, g["w_l0_inproj"][k * 128:(k + 1) * 128, 4096:4256],
              writes=[wsm[k]])
        P.dma("sync", wth[k].t[:, 0:32], g["w_l0_theta"][k * 128:(k + 1) * 128, :],
              writes=[wth[k]])
    theta_sl = A.alloc(TSL, name="theta_sl")
    ap_theta = theta_sl.t[0:32, :]
    braw = A.alloc(TSL, name="braw"); ap_braw = braw.t[0:64, :]
    craw = A.alloc(TSL, name="craw"); ap_craw = craw.t[0:64, :]
    bB0 = A.alloc(TSL, name="B0"); ap_B0 = bB0.t[0:64, :]
    bC0 = A.alloc(TSL, name="C0"); ap_C0 = bC0.t[0:64, :]
    sqbc = A.alloc(TSL, name="sqbc"); ap_sq = sqbc.t[0:64, :]
    bdt = [A.alloc(TSL, name=f"dt{j}") for j in range(2)]
    ap_dt = [bdt[0].t[0:16, :], bdt[1].t[0:16, :]]
    bdtl1 = [A.alloc(TSL, name=f"dtl1{j}") for j in range(2)]
    ap_dtl1 = [bdtl1[0].t[0:16, :], bdtl1[1].t[0:16, :]]
    blam = A.alloc(TSL, name="lam"); ap_lam = blam.t[0:16, :]
    buu = A.alloc(TSL, name="uu"); ap_uu = buu.t[0:16, :]
    bww = A.alloc(TSL, name="ww"); ap_ww = bww.t[0:16, :]
    badt = A.alloc(TSL, name="adt"); ap_adt = badt.t[0:16, :]
    bsp = [A.alloc(TSL, name=f"sp{j}") for j in range(3)]
    ap_sp = [b.t[0:16, :] for b in bsp]
    bphi = A.alloc(TSL, name="phi"); ap_phi = bphi.t[0:32, :]
    bq = A.alloc(TSL, name="q"); ap_q = bq.t[0:32, :]
    trg = A.alloc(192, name="trg")
    tg_phi = trg.t[:, 0:32]
    tg_q = trg.t[:, 32:64]
    tg_r = trg.t[:, 64:96]
    tg_sin = trg.t[:, 96:128]
    tg_cos = trg.t[:, 128:160]
    btm = A.alloc(64, name="btm")
    ctm = A.alloc(64, name="ctm")
    c1tm = A.alloc(64, name="c1tm")
    tp1 = A.alloc(32, name="tp1")
    tp2 = A.alloc(32, name="tp2")
    bch1 = A.alloc(128, name="ch1"); ap_ch1 = bch1.t[0:16, 0:128]
    bch2 = A.alloc(128, name="ch2"); ap_ch2 = bch2.t[0:16, 0:128]
    ones16 = A.alloc(128, name="ones16")
    ap_o16 = ones16.t[0:16, 0:128]
    P.memset(ap_o16, 1.0, writes=[ones16])
    ones32 = A.alloc(TSL, name="ones32")
    ap_o32 = ones32.t[0:32, :]
    P.memset(ap_o32, 1.0, writes=[ones32])
    c116 = A.alloc(2, name="c116")
    ap_c116 = c116.t[0:16, 0:1]
    P.memset(ap_c116, 1.0 / 16.0, writes=[c116])
    o64c = A.alloc(2, name="o64c")
    ap_o64 = o64c.t[0:64, 0:1]
    P.memset(ap_o64, 1.0, writes=[o64c])
    NCPS = TSL // 128

    def emit_ww(src_buf, src_ap, nxt, sl_of):
        P.cp(ap_ww[:, 0:TSL - 1], src_ap[:, 1:TSL], reads=[src_buf], writes=[bww])
        if nxt is None:
            P.memset(ap_ww[:, TSL - 1:TSL], 0.0, writes=[bww])
        else:
            nb_, na_ = nxt
            P.cp(ap_ww[:, TSL - 1:TSL], na_[:, 0:1], reads=[nb_], writes=[bww])
        for c in range(NCPS):
            ch = sl_of * NCPS + c
            P.transpose(pb[4].t[0:128, 0:16], ap_ww[:, c * 128:(c + 1) * 128],
                        EYE16.t, reads=[bww, EYE16], writes=[pb[4]])
            P.cp(colsT["w"].t[:, ch * 16:(ch + 1) * 16], pb[4].t[0:128, 0:16],
                 reads=[pb[4]], writes=[colsT["w"]])

    prev = {"buf": None, "ap": None, "idx": None}
    for sl in range(NSL):
        t0 = sl * TSL
        cur = sl % 2
        for i in range(8):
            P.dma("sync", x8[i].t, u_fm[i * 128:(i + 1) * 128, t0:t0 + TSL],
                  writes=[x8[i]])

        def out1(i, mB, rB):
            P.tt(u8[i].t, x8[i].t, mB.t[:, 0:TSL], OP.subtract,
                 reads=[x8[i], mB], writes=[u8[i]])
            P.tt(u8[i].t, u8[i].t, rB.t[:, 0:TSL], OP.mult,
                 reads=[u8[i], rB], writes=[u8[i]])
        ln_feature_major(x8, [b.t for b in x8], TSL, 1e-5, tmpc,
                         (pb[0], pb[1]), out1)
        for i in range(8):
            P.tt(tmpc.t, x8[i].t, u8[i].t, OP.add, reads=[x8[i], u8[i]],
                 writes=[tmpc])
            P.dma("scalar", xu_fm[i * 128:(i + 1) * 128, t0:t0 + TSL], tmpc.t,
                  reads=[tmpc])

        def out2(i, mB, rB):
            P.tt(x8[i].t, u8[i].t, mB.t[:, 0:TSL], OP.subtract,
                 reads=[u8[i], mB], writes=[x8[i]])
            P.tt(x8[i].t, x8[i].t, rB.t[:, 0:TSL], OP.mult,
                 reads=[x8[i], rB], writes=[x8[i]])
            P.stt(xin8[i].t, x8[i].t, C["v_l0_ln_s"].t[:, i:i + 1],
                  C["v_l0_ln_b"].t[:, i:i + 1].broadcast_to([128, TSL]),
                  OP.mult, OP.add, reads=[x8[i], C["v_l0_ln_s"], C["v_l0_ln_b"]],
                  writes=[xin8[i]])
            P.dma("scalar", xin_bf[i * 128:(i + 1) * 128, t0:t0 + TSL],
                  xin8[i].t, reads=[xin8[i]])
        ln_feature_major(u8, [b.t for b in u8], TSL, 1e-6, tmpc,
                         (pb[0], pb[1]), out2)
        for k in range(8):
            P.mm(pb[2].t[0:32, 0:TSL], wth[k].t[:, 0:32], u8[k].t,
                 start=(k == 0), stop=(k == 7),
                 reads=[wth[k], u8[k]], writes=[pb[2]])
        P.cp(ap_theta, pb[2].t[0:32, 0:TSL], reads=[pb[2]], writes=[theta_sl])
        for (mo, msz) in [(0, 64), (64, 64), (128, 16), (144, 16)]:
            bk = pb[3]
            for k in range(8):
                P.mm(bk.t[0:msz, 0:TSL], wsm[k].t[:, mo:mo + msz], xin8[k].t,
                     start=(k == 0), stop=(k == 7),
                     reads=[wsm[k], xin8[k]], writes=[bk])
            if mo == 0:
                P.cp(ap_braw, bk.t[0:64, 0:TSL], reads=[bk], writes=[braw])
            elif mo == 64:
                P.cp(ap_craw, bk.t[0:64, 0:TSL], reads=[bk], writes=[craw])
            elif mo == 128:
                P.stt(ap_dt[cur], bk.t[0:16, 0:TSL], 1.0,
                      C["v_l0_dtb"].t.broadcast_to([16, TSL]), OP.mult, OP.add,
                      reads=[bk, C["v_l0_dtb"]], writes=[bdt[cur]])
                P.act(ap_sp[0], ap_dt[cur], AF.Abs, reads=[bdt[cur]],
                      writes=[bsp[0]])
                P.act(ap_sp[1], ap_sp[0], AF.Exp, scale=-1.0,
                      reads=[bsp[0]], writes=[bsp[1]])
                P.act(ap_sp[1], ap_sp[1], AF.Ln, bias=1.0,
                      reads=[bsp[1]], writes=[bsp[1]])
                P.op("vector", lambda v, o=ap_sp[2], a=ap_dt[cur]:
                     v.tensor_scalar_max(o, a, 0.0),
                     reads=[bdt[cur]], writes=[bsp[2]])
                P.tt(ap_dt[cur], ap_sp[1], ap_sp[2], OP.add,
                     reads=[bsp[1], bsp[2]], writes=[bdt[cur]])
            else:
                P.act(ap_lam, bk.t[0:16, 0:TSL], AF.Sigmoid,
                      reads=[bk], writes=[blam])
        P.tt(ap_uu, ap_dt[cur], ap_lam, OP.mult, reads=[bdt[cur], blam],
             writes=[buu])
        P.ts(ap_adt, ap_dt[cur], A16.t, None, OP.mult, reads=[bdt[cur], A16],
             writes=[badt])
        P.ts(ap_sp[0], ap_lam, -1.0, 1.0, OP.mult, OP.add,
             reads=[blam], writes=[bsp[0]])
        P.tt(ap_dtl1[cur], ap_dt[cur], ap_sp[0], OP.mult,
             reads=[bdt[cur], bsp[0]], writes=[bdtl1[cur]])
        if prev["buf"] is not None:
            emit_ww(prev["buf"], prev["ap"], (bdtl1[cur], ap_dtl1[cur]),
                    prev["idx"])
        prev["buf"] = bdtl1[cur]
        prev["ap"] = ap_dtl1[cur]
        prev["idx"] = sl

        for c in range(NCPS):
            ch = sl * NCPS + c
            csl = slice(c * 128, (c + 1) * 128)
            P.scan(ap_cc[:, ch * 128:(ch + 1) * 128], ap_o16,
                   ap_adt[:, csl], 0.0, reads=[ones16, badt], writes=[w0])
            P.scan(ap_ch1[:, ::-1], ap_o16, ap_adt[:, csl][:, ::-1], 0.0,
                   reads=[ones16, badt], writes=[bch1])
            P.tt(ap_ch2, ap_ch1, ap_adt[:, csl], OP.subtract,
                 reads=[bch1, badt], writes=[bch2])
            P.act(ap_expic0[:, ch:ch + 1], ap_ch1[:, 0:1], AF.Exp,
                  reads=[bch1], writes=[expic0])
            P.act(ap_expcc[:, ch * 128:(ch + 1) * 128],
                  ap_cc[:, ch * 128:(ch + 1) * 128], AF.Exp,
                  reads=[w0], writes=[wex])
            P.act(ap_ch1, ap_ch2, AF.Exp, reads=[bch2], writes=[bch1])
            P.transpose(pb[4].t[0:128, 0:16], ap_ch1, EYE16.t,
                        reads=[bch1, EYE16], writes=[pb[4]])
            P.cp(colsT["exprc"].t[:, ch * 16:(ch + 1) * 16],
                 pb[4].t[0:128, 0:16], reads=[pb[4]], writes=[colsT["exprc"]])
            P.ts(ap_ch1, ap_cc[:, ch * 128:(ch + 1) * 128], -1.0, None,
                 OP.mult, reads=[w0], writes=[bch1])
            P.transpose(pb[4].t[0:128, 0:16], ap_ch1, EYE16.t,
                        reads=[bch1, EYE16], writes=[pb[4]])
            P.cp(colsT["negcc"].t[:, ch * 16:(ch + 1) * 16],
                 pb[4].t[0:128, 0:16], reads=[pb[4]], writes=[colsT["negcc"]])
            P.transpose(pb[4].t[0:128, 0:16], ap_uu[:, csl], EYE16.t,
                        reads=[buu, EYE16], writes=[pb[4]])
            P.cp(colsT["u"].t[:, ch * 16:(ch + 1) * 16], pb[4].t[0:128, 0:16],
                 reads=[pb[4]], writes=[colsT["u"]])

        def rms(raw_ap, raw_buf, scale_c, bias_c, out_ap, out_buf):
            P.act(ap_sq, raw_ap, AF.Square, reads=[raw_buf], writes=[sqbc])
            P.mm(pb[5].t[0:1, 0:TSL], ap_o64, ap_sq, start=True, stop=True,
                 reads=[sqbc, o64c], writes=[pb[5]])
            rr = tmpc.t[0:1, 0:TSL]
            P.ts(rr, pb[5].t[0:1, 0:TSL], 1.0 / 64.0, 1e-6, OP.mult, OP.add,
                 reads=[pb[5]], writes=[tmpc])
            P.recip(rr, rr, reads=[tmpc], writes=[tmpc])
            P.act(rr, rr, AF.Sqrt, reads=[tmpc], writes=[tmpc])
            P.mm(pb[5].t[0:64, 0:TSL], ONESR.t[:, 0:64], rr, start=True,
                 stop=True, reads=[tmpc, ONESR], writes=[pb[5]])
            P.tt(out_ap, raw_ap, pb[5].t[0:64, 0:TSL], OP.mult,
                 reads=[raw_buf, pb[5]], writes=[out_buf])
            P.stt(out_ap, out_ap, scale_c.t, bias_c.t.broadcast_to([64, TSL]),
                  OP.mult, OP.add, reads=[out_buf, scale_c, bias_c],
                  writes=[out_buf])
        rms(ap_braw, braw, C["v_l0_bcB"], C["v_l0_Bb"], ap_B0, bB0)
        rms(ap_craw, craw, C["v_l0_bcC"], C["v_l0_Cb"], ap_C0, bC0)
        P.mm(pb[5].t[0:1, 0:TSL], ap_c116, ap_dt[cur], start=True, stop=True,
             reads=[bdt[cur], c116], writes=[pb[5]])
        P.cp(tmpc.t[0:1, 0:TSL], pb[5].t[0:1, 0:TSL], reads=[pb[5]],
             writes=[tmpc])
        P.mm(pb[5].t[0:32, 0:TSL], ONESR.t[:, 0:32], tmpc.t[0:1, 0:TSL],
             start=True, stop=True, reads=[tmpc, ONESR], writes=[pb[5]])
        P.tt(ap_q, ap_theta, pb[5].t[0:32, 0:TSL], OP.mult,
             reads=[theta_sl, pb[5]], writes=[bq])
        P.scan(ap_phi, ap_o32, ap_q, ap_phic,
               reads=[ones32, bq, phic], writes=[bphi])
        P.cp(ap_phic, ap_phi[:, TSL - 1:TSL], reads=[bphi], writes=[phic])
        for c in range(NCPS):
            ch = sl * NCPS + c
            csl = slice(c * 128, (c + 1) * 128)
            chsl = slice(ch * 128, (ch + 1) * 128)
            P.transpose(pb[4].t[0:128, 0:32], ap_phi[:, csl], EYE32.t,
                        reads=[bphi, EYE32], writes=[pb[4]])
            P.cp(tg_phi, pb[4].t[0:128, 0:32], reads=[pb[4]], writes=[trg])
            P.ts(tg_q, tg_phi, 1.0 / (2 * PI), MAGIC, OP.mult, OP.add,
                 reads=[trg], writes=[trg])
            P.ts(tg_q, tg_q, -MAGIC, None, OP.add, reads=[trg], writes=[trg])
            P.stt(tg_r, tg_q, -2 * PI, tg_phi, OP.mult, OP.add,
                  reads=[trg], writes=[trg])
            P.act(tg_sin, tg_r, AF.Sin, reads=[trg], writes=[trg])
            P.ts(tg_q, tg_phi, 1.0 / (2 * PI), MAGIC + 0.25, OP.mult, OP.add,
                 reads=[trg], writes=[trg])
            P.ts(tg_q, tg_q, -MAGIC, None, OP.add, reads=[trg], writes=[trg])
            P.stt(tg_r, tg_q, -2 * PI, tg_phi, OP.mult, OP.add,
                  reads=[trg], writes=[trg])
            P.ts(tg_r, tg_r, PI / 2, None, OP.add, reads=[trg], writes=[trg])
            P.act(tg_cos, tg_r, AF.Sin, reads=[trg], writes=[trg])
            P.transpose(pb[4].t[0:128, 0:64], ap_B0[:, csl], EYE64.t,
                        reads=[bB0, EYE64], writes=[pb[4]])
            P.cp(btm.t, pb[4].t[0:128, 0:64], reads=[pb[4]], writes=[btm])
            P.transpose(pb[4].t[0:128, 0:64], ap_C0[:, csl], EYE64.t,
                        reads=[bC0, EYE64], writes=[pb[4]])
            P.cp(ctm.t, pb[4].t[0:128, 0:64], reads=[pb[4]], writes=[ctm])

            def rope_tm(src_b, dst_ap, dst_b):
                P.tt(tp1.t[:, 0:32], src_b.t[:, 0:32], tg_cos, OP.mult,
                     reads=[src_b, trg], writes=[tp1])
                P.tt(tp2.t[:, 0:32], src_b.t[:, 32:64], tg_sin, OP.mult,
                     reads=[src_b, trg], writes=[tp2])
                P.tt(dst_ap[:, 0:32], tp1.t[:, 0:32], tp2.t[:, 0:32],
                     OP.subtract, reads=[tp1, tp2], writes=[dst_b])
                P.tt(tp1.t[:, 0:32], src_b.t[:, 0:32], tg_sin, OP.mult,
                     reads=[src_b, trg], writes=[tp1])
                P.tt(tp2.t[:, 0:32], src_b.t[:, 32:64], tg_cos, OP.mult,
                     reads=[src_b, trg], writes=[tp2])
                P.tt(dst_ap[:, 32:64], tp1.t[:, 0:32], tp2.t[:, 0:32],
                     OP.add, reads=[tp1, tp2], writes=[dst_b])
            rope_tm(btm, B_tm.t[:, ch * 64:(ch + 1) * 64], B_tm)
            rope_tm(ctm, c1tm.t, c1tm)
            P.transpose(pb[4].t[0:64, 0:128], B_tm.t[:, ch * 64:(ch + 1) * 64],
                        EYE.t, reads=[B_tm, EYE], writes=[pb[4]])
            P.cp(ap_B1[:, chsl], pb[4].t[0:64, 0:128], reads=[pb[4]],
                 writes=[wB1])
            P.transpose(pb[4].t[0:64, 0:128], c1tm.t, EYE.t,
                        reads=[c1tm, EYE], writes=[pb[4]])
            P.cp(ap_C1[:, chsl], pb[4].t[0:64, 0:128], reads=[pb[4]],
                 writes=[w1])
    emit_ww(prev["buf"], prev["ap"], None, prev["idx"])
    A.pop()
    P.barrier()

    if PHASE_LIMIT <= 1:
        return nc, P, dict(locals())
    TSB = 256
    NSB = T // TSB
    A.push()
    w_x = [A.alloc(2048, BF16, name=f"wx{k}") for k in range(8)]
    for k in range(8):
        P.dma("gpsimd", w_x[k].t, g["w_l0_inproj"][k * 128:(k + 1) * 128, 0:2048],
              writes=[w_x[k]])
    xin_s = [A.alloc(TSB, BF16, name=f"xins{i}") for i in range(8)]
    x_tm = [A.alloc(2048, BF16, name=f"xtm{i}") for i in range(2)]
    ybuf = [A.alloc(128, BF16, name=f"yb{i}") for i in range(2)]
    d0t = A.alloc(128, name="d0t")
    lt = A.alloc(128, name="lt")
    smt = [A.alloc(128, name=f"smt{j}") for j in range(2)]
    smst = [A.alloc(128, name=f"smst{j}") for j in range(2)]
    vut = [A.alloc(128, name=f"vut{j}") for j in range(2)]
    vwt = [A.alloc(128, name=f"vwt{j}") for j in range(2)]
    vft = [A.alloc(128, name=f"vft{j}") for j in range(2)]
    bhat = [A.alloc(64, name=f"bhat{j}") for j in range(2)]
    cdec = [A.alloc(128, name=f"cdec{j}") for j in range(2)]
    eit = [A.alloc(64, name=f"eit{j}") for j in range(2)]
    Hb = [A.alloc(16 * 128, name=f"H{j}") for j in range(2)]
    ap_H = [Hb[0].t[0:64, :], Hb[1].t[0:64, :]]
    P.memset(ap_H[0], 0.0, writes=[Hb[0]])

    for sb in range(NSB):
        t0 = sb * TSB
        for k in range(8):
            P.dma("sync", xin_s[k].t, xin_bf[k * 128:(k + 1) * 128, t0:t0 + TSB],
                  writes=[xin_s[k]])
        for tc in range(2):
            tokap = slice(tc * 128, (tc + 1) * 128)
            for fs in range(4):
                bk = pb[fs % 2]
                for k in range(8):
                    P.mm(bk.t[:, 0:512], xin_s[k].t[:, tokap],
                         w_x[k].t[:, fs * 512:(fs + 1) * 512],
                         start=(k == 0), stop=(k == 7),
                         reads=[xin_s[k], w_x[k]], writes=[bk])
                P.act(x_tm[tc].t[:, fs * 512:(fs + 1) * 512], bk.t[:, 0:512],
                      AF.Silu, reads=[bk], writes=[x_tm[tc]])
        for c in range(TSB // 128):
            ch = sb * (TSB // 128) + c
            curH = ap_H[ch % 2]
            nxtH = ap_H[(ch + 1) % 2]
            curHb = Hb[ch % 2]
            nxtHb = Hb[(ch + 1) % 2]
            chsl = slice(ch * 128, (ch + 1) * 128)
            P.mm(pb[4].t[:, 0:128], ap_B1[:, chsl], ap_C1[:, chsl],
                 start=True, stop=True, reads=[wB1, w1], writes=[pb[4]])

            def S0(h):
                p = h % 2
                yb = pb[5 + p]
                sbk = pb[3] if p == 0 else pb[7]
                P.mm(yb.t[:, 0:128],
                     EYE16.t[:, h:h + 1].broadcast_to([16, 128]),
                     ap_cc[:, chsl], start=True, stop=True,
                     reads=[EYE16, w0], writes=[yb])
                P.mm(sbk.t[0:64, 0:128],
                     EYE16.t[:, h:h + 1].broadcast_to([16, 64]),
                     ap_expcc[:, chsl], start=True, stop=True,
                     reads=[EYE16, wex], writes=[sbk])
                P.mm(sbk.t[0:64, 256:257],
                     EYE16.t[:, h:h + 1].broadcast_to([16, 64]),
                     ap_expic0[:, ch:ch + 1], start=True, stop=True,
                     reads=[EYE16, expic0], writes=[sbk])

            def S1(h):
                p = h % 2
                yb = pb[5 + p]
                sbk = pb[3] if p == 0 else pb[7]
                hcol = slice(ch * 16 + h, ch * 16 + h + 1)
                P.tt(d0t.t[:, 0:128], yb.t[:, 0:128], UTRI.t, OP.add,
                     reads=[yb, UTRI], writes=[d0t])
                P.act(lt.t[:, 0:128], d0t.t[:, 0:128], AF.Exp,
                      bias=colsT["negcc"].t[:, hcol],
                      reads=[d0t, colsT["negcc"]], writes=[lt])
                P.tt(smt[p].t[:, 0:128], lt.t[:, 0:128], pb[4].t[:, 0:128],
                     OP.mult, reads=[lt, pb[4]], writes=[smt[p]])
                P.tt(smst[p].t[:, 0:128], smt[p].t[:, 0:128], STRICT.t,
                     OP.mult, reads=[smt[p], STRICT], writes=[smst[p]])
                xs_ap = x_tm[c].t[:, h * 128:(h + 1) * 128]
                P.ts(vut[p].t[:, 0:128], xs_ap, colsT["u"].t[:, hcol], None,
                     OP.mult, reads=[x_tm[c], colsT["u"]], writes=[vut[p]])
                P.ts(vwt[p].t[:, 0:128], xs_ap, colsT["w"].t[:, hcol], None,
                     OP.mult, reads=[x_tm[c], colsT["w"]], writes=[vwt[p]])
                P.tt(vft[p].t[:, 0:128], vut[p].t[:, 0:128],
                     vwt[p].t[:, 0:128], OP.add,
                     reads=[vut[p], vwt[p]], writes=[vft[p]])
                P.tt(cdec[p].t[0:64, 0:128], ap_C1[:, chsl],
                     sbk.t[0:64, 0:128], OP.mult,
                     reads=[w1, sbk], writes=[cdec[p]])
                P.ts(eit[p].t[0:64, 0:64], EYE64.t, sbk.t[0:64, 256:257],
                     None, OP.mult, reads=[EYE64, sbk], writes=[eit[p]])
                P.ts(bhat[p].t[:, 0:64], B_tm.t[:, ch * 64:(ch + 1) * 64],
                     colsT["exprc"].t[:, hcol], None, OP.mult,
                     reads=[B_tm, colsT["exprc"]], writes=[bhat[p]])

            def S2(h):
                p = h % 2
                yb = pb[5 + p]
                sbk = pb[3] if p == 0 else pb[7]
                xs_ap = x_tm[c].t[:, h * 128:(h + 1) * 128]
                P.mm(yb.t[:, 128:256], vut[p].t[:, 0:128],
                     smt[p].t[:, 0:128], start=True, stop=False,
                     reads=[vut[p], smt[p]], writes=[yb])
                P.mm(yb.t[:, 128:256], vwt[p].t[:, 0:128],
                     smst[p].t[:, 0:128], start=False, stop=False,
                     reads=[vwt[p], smst[p]], writes=[yb])
                P.mm(yb.t[:, 128:256], xs_ap, DI.t[:, h * 128:(h + 1) * 128],
                     start=False, stop=False, reads=[x_tm[c], DI], writes=[yb])
                P.mm(yb.t[:, 128:256], curH[:, h * 128:(h + 1) * 128],
                     cdec[p].t[0:64, 0:128], start=False, stop=True,
                     reads=[curHb, cdec[p]], writes=[yb])
                P.mm(sbk.t[0:64, 128:256], bhat[p].t[:, 0:64],
                     vft[p].t[:, 0:128], start=True, stop=False,
                     reads=[bhat[p], vft[p]], writes=[sbk])
                P.mm(sbk.t[0:64, 128:256], eit[p].t[0:64, 0:64],
                     curH[:, h * 128:(h + 1) * 128], start=False, stop=True,
                     reads=[eit[p], curHb], writes=[sbk])

            def S3(h):
                p = h % 2
                yb = pb[5 + p]
                sbk = pb[3] if p == 0 else pb[7]
                P.cp(nxtH[:, h * 128:(h + 1) * 128], sbk.t[0:64, 128:256],
                     reads=[sbk], writes=[nxtHb], eng="scalar")
                yo = ybuf[p]
                P.cp(yo.t, yb.t[:, 128:256], reads=[yb], writes=[yo])
                P.dma("scalar",
                      y0_bf[h * 128:(h + 1) * 128, (ch * 128):(ch * 128) + 128],
                      yo.t, reads=[yo])

            S0(0)
            for h in range(16):
                S1(h)
                if h > 0:
                    S3(h - 1)
                if h + 1 < 16:
                    S0(h + 1)
                S2(h)
            S3(15)
    A.pop()
    A.pop()
    P.barrier()

    if PHASE_LIMIT <= 2:
        return nc, P, dict(locals())
    TSC = 512
    A.push()
    w_z = [A.alloc(2048, BF16, name=f"wz{k}") for k in range(8)]
    w_o0 = [A.alloc(1024, BF16, name=f"wo0_{k}") for k in range(16)]
    for k in range(8):
        P.dma("gpsimd", w_z[k].t,
              g["w_l0_inproj"][k * 128:(k + 1) * 128, 2048:4096], writes=[w_z[k]])
    for k in range(16):
        P.dma("gpsimd", w_o0[k].t, g["w_l0_outproj"][k * 128:(k + 1) * 128, :],
              writes=[w_o0[k]])
    xinc = [A.alloc(TSC, BF16, name=f"xinc{i}") for i in range(8)]
    xuc = [A.alloc(TSC, name=f"xuc{i}") for i in range(8)]
    y0c = [A.alloc(TSC, BF16, name=f"y0c{i}") for i in range(16)]
    ygc = [A.alloc(TSC, BF16, name=f"ygc{i}") for i in range(16)]
    zsl = A.alloc(TSC, name="zsl")
    for sc in range(T // TSC):
        t0 = sc * TSC
        for k in range(8):
            P.dma("sync", xinc[k].t, xin_bf[k * 128:(k + 1) * 128, t0:t0 + TSC],
                  writes=[xinc[k]])
            P.dma("sync", xuc[k].t, xu_fm[k * 128:(k + 1) * 128, t0:t0 + TSC],
                  writes=[xuc[k]])
        for m in range(16):
            P.dma("scalar", y0c[m].t, y0_bf[m * 128:(m + 1) * 128, t0:t0 + TSC],
                  writes=[y0c[m]])
        for m in range(16):
            bk = pb[m % 2]
            for k in range(8):
                P.mm(bk.t[:, 0:TSC], w_z[k].t[:, m * 128:(m + 1) * 128],
                     xinc[k].t, start=(k == 0), stop=(k == 7),
                     reads=[w_z[k], xinc[k]], writes=[bk])
            P.act(zsl.t, bk.t[:, 0:TSC], AF.Silu, reads=[bk], writes=[zsl])
            P.tt(ygc[m].t, y0c[m].t, zsl.t, OP.mult,
                 reads=[y0c[m], zsl], writes=[ygc[m]])
        for m in range(8):
            bk = pb[2 + (m % 2)]
            for k in range(16):
                P.mm(bk.t[:, 0:TSC], w_o0[k].t[:, m * 128:(m + 1) * 128],
                     ygc[k].t, start=(k == 0), stop=(k == 15),
                     reads=[w_o0[k], ygc[k]], writes=[bk])
            P.tt(xuc[m].t, bk.t[:, 0:TSC], xuc[m].t, OP.add,
                 reads=[bk, xuc[m]], writes=[xuc[m]])
            P.dma("scalar", x1_fm[m * 128:(m + 1) * 128, t0:t0 + TSC],
                  xuc[m].t, reads=[xuc[m]])
    A.pop()
    P.barrier()

    if PHASE_LIMIT <= 3:
        return nc, P, dict(locals())
    TS1 = 256
    A.push()
    mk_ln_rows()
    ld_consts(["v_l0f_bd"])
    wgate = [A.alloc(8, name=f"wg{k}") for k in range(8)]
    wd = [[A.alloc(64, BF16, name=f"wd{e}_{k}") for k in range(8)]
          for e in range(8)]
    wu = [A.alloc(1024, BF16, name=f"wu{e}") for e in range(8)]
    bu_l = [A.alloc(1024, name=f"bul{j}") for j in range(1)]
    bu_l = bu_l[0]
    ap_bu = bu_l.t[0:8, 0:1024]
    for k in range(8):
        P.dma("sync", wgate[k].t, g["w_l0f_gate"][k * 128:(k + 1) * 128, :],
              writes=[wgate[k]])
    for e in range(8):
        for k in range(8):
            P.dma("gpsimd", wd[e][k].t, g["w_l0f_Wd"][e, k * 128:(k + 1) * 128, :],
                  writes=[wd[e][k]])
        P.dma("gpsimd", wu[e].t[0:64, :], g["w_l0f_Wu"][e], writes=[wu[e]])
    P.dma("sync", ap_bu, g["w_l0f_bu"], writes=[bu_l])
    x1c = [A.alloc(TS1, name=f"x1c{i}") for i in range(8)]
    xfc = [A.alloc(TS1, name=f"xfc{i}") for i in range(8)]
    xfb = [A.alloc(TS1, BF16, name=f"xfb{i}") for i in range(8)]
    tmp1 = A.alloc(TS1, name="tmp1")
    weT8 = A.alloc(TS1, name="weT8")
    ap_weT = weT8.t[0:8, :]
    tp = A.alloc(64, name="tp")
    ap_lg = tp.t[:, 0:8]
    ap_msk1 = tp.t[:, 8:16]
    ap_msk2 = tp.t[:, 16:24]
    ap_d = tp.t[:, 24:32]
    ap_m1 = tp.t[:, 32:33]
    ap_m2 = tp.t[:, 33:34]
    ap_g1 = tp.t[:, 34:35]
    ap_g2 = tp.t[:, 35:36]
    hbf = A.alloc(TS1, BF16, name="hbf")
    ap_hbf = hbf.t[0:64, :]
    hsc = A.alloc(TS1, BF16, name="hsc")
    ap_hsc = hsc.t[0:64, :]
    wcb = A.alloc(TS1, BF16, name="wcb")
    ap_wcb = wcb.t[0:64, :]

    def top2_we(lg_psum_ap, tokchunk_ap):
        P.cp(ap_lg, lg_psum_ap, reads=[pb[6]], writes=[tp])
        P.red_max(ap_m1, ap_lg, reads=[tp], writes=[tp])
        P.stt(ap_d, ap_lg, 1.0, tp.t[:, 32:33].broadcast_to([128, 8]),
              OP.mult, OP.subtract, reads=[tp], writes=[tp])
        P.act(ap_d, ap_d, AF.Sign, reads=[tp], writes=[tp])
        P.tt(ap_msk1, ap_d, ap_d, OP.mult, reads=[tp], writes=[tp])
        P.ts(ap_msk1, ap_msk1, -1.0, 1.0, OP.mult, OP.add, reads=[tp], writes=[tp])
        P.stt(ap_d, ap_msk1, -1e30, ap_lg, OP.mult, OP.add, reads=[tp], writes=[tp])
        P.red_max(ap_m2, ap_d, reads=[tp], writes=[tp])
        P.stt(ap_d, ap_d, 1.0, tp.t[:, 33:34].broadcast_to([128, 8]),
              OP.mult, OP.subtract, reads=[tp], writes=[tp])
        P.act(ap_d, ap_d, AF.Sign, reads=[tp], writes=[tp])
        P.tt(ap_msk2, ap_d, ap_d, OP.mult, reads=[tp], writes=[tp])
        P.ts(ap_msk2, ap_msk2, -1.0, 1.0, OP.mult, OP.add, reads=[tp], writes=[tp])
        P.tt(ap_d[:, 0:1], ap_m1, ap_m2, OP.subtract, reads=[tp], writes=[tp])
        P.act(ap_g1, ap_d[:, 0:1], AF.Sigmoid, reads=[tp], writes=[tp])
        P.ts(ap_g2, ap_g1, -1.0, 1.0, OP.mult, OP.add, reads=[tp], writes=[tp])
        P.ts(ap_msk1, ap_msk1, tp.t[:, 34:35], None, OP.mult, reads=[tp],
             writes=[tp])
        P.stt(ap_msk2, ap_msk2, tp.t[:, 35:36], ap_msk1, OP.mult, OP.add,
              reads=[tp], writes=[tp])
        P.transpose(pb[7].t[0:8, 0:128], ap_msk2, EYE.t, reads=[tp, EYE],
                    writes=[pb[7]])
        P.cp(ap_weT[:, tokchunk_ap], pb[7].t[0:8, 0:128], reads=[pb[7]],
             writes=[weT8])

    for s1 in range(T // TS1):
        t0 = s1 * TS1
        for i in range(8):
            P.dma("sync", x1c[i].t, x1_fm[i * 128:(i + 1) * 128, t0:t0 + TS1],
                  writes=[x1c[i]])

        def outf(i, mB, rB):
            P.tt(xfc[i].t, x1c[i].t, mB.t[:, 0:TS1], OP.subtract,
                 reads=[x1c[i], mB], writes=[xfc[i]])
            P.tt(xfc[i].t, xfc[i].t, rB.t[:, 0:TS1], OP.mult,
                 reads=[xfc[i], rB], writes=[xfc[i]])
            P.cp(xfb[i].t, xfc[i].t, reads=[xfc[i]], writes=[xfb[i]])
        ln_feature_major(x1c, [b.t for b in x1c], TS1, 1e-5, tmp1,
                         (pb[0], pb[1]), outf)
        for tc in range(2):
            tokap = slice(tc * 128, (tc + 1) * 128)
            for k in range(8):
                P.mm(pb[6].t[:, 0:8], xfc[k].t[:, tokap], wgate[k].t,
                     start=(k == 0), stop=(k == 7),
                     reads=[xfc[k], wgate[k]], writes=[pb[6]])
            top2_we(pb[6].t[:, 0:8], slice(tc * 128, (tc + 1) * 128))
        for e in range(8):
            P.mm(pb[1].t[0:64, 0:TS1],
                 EYE8.t[:, e:e + 1].broadcast_to([8, 64]), ap_weT,
                 start=True, stop=True, reads=[EYE8, weT8], writes=[pb[1]])
            P.cp(ap_wcb, pb[1].t[0:64, 0:TS1], reads=[pb[1]], writes=[wcb])
            bk = pb[0]
            for k in range(8):
                P.mm(bk.t[0:64, 0:TS1], wd[e][k].t, xfb[k].t,
                     start=(k == 0), stop=(k == 7),
                     reads=[wd[e][k], xfb[k]], writes=[bk])
            P.act(ap_hbf, bk.t[0:64, 0:TS1], AF.Relu,
                  bias=C["v_l0f_bd"].t[:, e:e + 1],
                  reads=[bk, C["v_l0f_bd"]], writes=[hbf])
            P.tt(ap_hsc, ap_hbf, ap_wcb, OP.mult, reads=[hbf, wcb], writes=[hsc])
            for mp in range(8):
                ob = pb[4 + mp // 2]
                oap = ob.t[:, (mp % 2) * 256:(mp % 2) * 256 + TS1]
                P.mm(oap, wu[e].t[0:64, mp * 128:(mp + 1) * 128], ap_hsc,
                     start=(e == 0), stop=False, reads=[wu[e], hsc], writes=[ob])
        for mp in range(8):
            ob = pb[4 + mp // 2]
            oap = ob.t[:, (mp % 2) * 256:(mp % 2) * 256 + TS1]
            P.mm(oap, ap_bu[:, mp * 128:(mp + 1) * 128], ap_weT,
                 start=False, stop=True, reads=[bu_l, weT8], writes=[ob])
            P.stt(x1c[mp].t, oap, 0.5, x1c[mp].t, OP.mult, OP.add,
                  reads=[ob, x1c[mp]], writes=[x1c[mp]])
            P.dma("scalar", x2_fm[mp * 128:(mp + 1) * 128, t0:t0 + TS1],
                  x1c[mp].t, reads=[x1c[mp]])
    A.pop()
    P.barrier()

    if PHASE_LIMIT <= 4:
        return nc, P, dict(locals())
    TS2 = 512
    A.push()
    mk_ln_rows()
    ld_consts(["v_l1_convw", "v_l1_convb"])
    w_in1 = [A.alloc(4096, BF16, name=f"wi1_{k}") for k in range(8)]
    for k in range(8):
        P.dma("gpsimd", w_in1[k].t, g["w_l1_inproj"][k * 128:(k + 1) * 128, :],
              writes=[w_in1[k]])
    x2c = [A.alloc(TS2, name=f"x2c{i}") for i in range(8)]
    hsb = [A.alloc(TS2, BF16, name=f"hsb{i}") for i in range(8)]
    tmp2 = A.alloc(TS2, name="tmp2")
    xcat = A.alloc(TS2 + 4, name="xcat")
    carry3 = A.alloc(48, name="carry3")
    P.memset(carry3.t, 0.0, writes=[carry3])
    acc2 = [A.alloc(TS2, name=f"acc2_{j}") for j in range(2)]
    resb = A.alloc(TS2, name="resb")
    for s2 in range(T // TS2):
        t0 = s2 * TS2
        for i in range(8):
            P.dma("sync", x2c[i].t, x2_fm[i * 128:(i + 1) * 128, t0:t0 + TS2],
                  writes=[x2c[i]])

        def outh(i, mB, rB):
            P.tt(tmp2.t, x2c[i].t, mB.t[:, 0:TS2], OP.subtract,
                 reads=[x2c[i], mB], writes=[tmp2])
            P.tt(hsb[i].t, tmp2.t, rB.t[:, 0:TS2], OP.mult,
                 reads=[tmp2, rB], writes=[hsb[i]])
        ln_feature_major(x2c, [b.t for b in x2c], TS2, 1e-5, tmp2,
                         (pb[0], pb[1]), outh)
        for m in range(32):
            bk = pb[2 + (m % 2)]
            for k in range(8):
                P.mm(bk.t[:, 0:TS2], w_in1[k].t[:, m * 128:(m + 1) * 128],
                     hsb[k].t, start=(k == 0), stop=(k == 7),
                     reads=[w_in1[k], hsb[k]], writes=[bk])
            if m < 16:
                P.cp(xcat.t[:, 0:3], carry3.t[:, m * 3:m * 3 + 3],
                     reads=[carry3], writes=[xcat])
                P.cp(xcat.t[:, 3:3 + TS2], bk.t[:, 0:TS2], reads=[bk],
                     writes=[xcat])
                P.cp(carry3.t[:, m * 3:m * 3 + 3],
                     xcat.t[:, TS2:TS2 + 3], reads=[xcat], writes=[carry3])
                P.ts(acc2[0].t, xcat.t[:, 0:TS2], C["v_l1_convw"].t[:, m * 4:m * 4 + 1],
                     None, OP.mult, reads=[xcat, C["v_l1_convw"]], writes=[acc2[0]])
                for kk in range(1, 4):
                    P.stt(acc2[kk % 2].t, xcat.t[:, kk:kk + TS2],
                          C["v_l1_convw"].t[:, m * 4 + kk:m * 4 + kk + 1],
                          acc2[(kk + 1) % 2].t, OP.mult, OP.add,
                          reads=[xcat, C["v_l1_convw"], acc2[(kk + 1) % 2]],
                          writes=[acc2[kk % 2]])
                P.act(acc2[0].t, acc2[1].t, AF.Silu,
                      bias=C["v_l1_convb"].t[:, m:m + 1],
                      reads=[acc2[1], C["v_l1_convb"]], writes=[acc2[0]])
                P.dma("scalar", xs_fm[m * 128:(m + 1) * 128, t0:t0 + TS2],
                      acc2[0].t, reads=[acc2[0]])
            else:
                mm_ = m - 16
                P.cp(resb.t, bk.t[:, 0:TS2], reads=[bk], writes=[resb])
                P.dma("scalar", res_fm[mm_ * 128:(mm_ + 1) * 128, t0:t0 + TS2],
                      resb.t, reads=[resb])
    A.pop()
    P.barrier()

    if PHASE_LIMIT <= 5:
        return nc, P, dict(locals())
    A.push()
    bBl1 = A.alloc(T, BF16, name="Bl1")
    ap_Bl1 = bBl1.t[0:64, :]
    bCl1 = A.alloc(T, BF16, name="Cl1")
    ap_Cl1 = bCl1.t[0:64, :]
    eye64bf = A.alloc(64, BF16, name="eye64bf")
    ap_e64b = eye64bf.t[0:64, 0:64]
    P.cp(ap_e64b, EYE64.t, reads=[EYE64], writes=[eye64bf])
    eye128bf = A.alloc(128, BF16, name="eye128bf")
    ap_e128b = eye128bf.t[:, 0:128]
    P.cp(ap_e128b, EYE.t, reads=[EYE], writes=[eye128bf])
    a_sb = A.alloc(1024, name="a_sb")
    st_sb = A.alloc(1024, name="st_sb")
    P.memset(st_sb.t, 0.0, writes=[st_sb])
    for dc in range(16):
        P.dma("sync", a_sb.t[:, dc * 64:(dc + 1) * 64],
              g["w_l1_alog"][dc * 128:(dc + 1) * 128, :], writes=[a_sb])
    P.act(a_sb.t, a_sb.t, AF.Exp, reads=[a_sb], writes=[a_sb])
    P.ts(a_sb.t, a_sb.t, -1.0, None, OP.mult, reads=[a_sb], writes=[a_sb])

    A.push()
    ld_consts(["v_l1_dtpb"])
    w_dt = [A.alloc(2048, BF16, name=f"wdt{k}") for k in range(16)]
    w_xp = [A.alloc(128, BF16, name=f"wxp{k}") for k in range(16)]
    for k in range(16):
        P.dma("gpsimd", w_dt[k].t, g["w_l1_dtproj"][k * 128:(k + 1) * 128, :],
              writes=[w_dt[k]])
        P.dma("gpsimd", w_xp[k].t, g["w_l1_xproj"][k * 128:(k + 1) * 128, :],
              writes=[w_xp[k]])
    xsb = [A.alloc(TS2, BF16, name=f"xsb{i}") for i in range(16)]
    xs32 = A.alloc(TS2, name="xs32")
    dl32 = A.alloc(TS2, name="dl32")
    sp1 = A.alloc(TS2, name="sp1")
    sp2 = A.alloc(TS2, name="sp2")
    for s2 in range(T // TS2):
        t0 = s2 * TS2
        for k in range(16):
            P.dma("gpsimd", xsb[k].t, xs_fm[k * 128:(k + 1) * 128, t0:t0 + TS2],
                  writes=[xsb[k]])
        bcw_of = {0: bBl1, 64: bCl1}
        for (mo, rows) in [(0, ap_Bl1), (64, ap_Cl1)]:
            bk = pb[5]
            for k in range(16):
                P.mm(bk.t[0:64, 0:TS2], w_xp[k].t[:, mo:mo + 64], xsb[k].t,
                     start=(k == 0), stop=(k == 15),
                     reads=[w_xp[k], xsb[k]], writes=[bk])
            P.cp(rows[:, t0:t0 + TS2], bk.t[0:64, 0:TS2], reads=[bk],
                 writes=[bcw_of[mo]])
        for m in range(16):
            bk = pb[2 + (m % 2)]
            for k in range(16):
                P.mm(bk.t[:, 0:TS2], w_dt[k].t[:, m * 128:(m + 1) * 128],
                     xsb[k].t, start=(k == 0), stop=(k == 15),
                     reads=[w_dt[k], xsb[k]], writes=[bk])
            P.stt(dl32.t, bk.t[:, 0:TS2], 1.0,
                  C["v_l1_dtpb"].t[:, m:m + 1].broadcast_to([128, TS2]),
                  OP.mult, OP.add, reads=[bk, C["v_l1_dtpb"]], writes=[dl32])
            P.act(sp1.t, dl32.t, AF.Abs, reads=[dl32], writes=[sp1])
            P.act(sp1.t, sp1.t, AF.Exp, scale=-1.0, reads=[sp1], writes=[sp1])
            P.act(sp1.t, sp1.t, AF.Ln, bias=1.0, reads=[sp1], writes=[sp1])
            P.op("vector", lambda v, o=sp2.t, a=dl32.t:
                 v.tensor_scalar_max(o, a, 0.0), reads=[dl32], writes=[sp2])
            P.tt(dl32.t, sp1.t, sp2.t, OP.add, reads=[sp1, sp2], writes=[dl32])
            P.dma("scalar", delta_fm[m * 128:(m + 1) * 128, t0:t0 + TS2],
                  dl32.t, reads=[dl32])
            P.dma("sync", xs32.t, xs_fm[m * 128:(m + 1) * 128, t0:t0 + TS2],
                  writes=[xs32])
            P.tt(sp2.t, dl32.t, xs32.t, OP.mult, reads=[dl32, xs32], writes=[sp2])
            P.dma("scalar", dxd_fm[m * 128:(m + 1) * 128, t0:t0 + TS2],
                  sp2.t, reads=[sp2])
    A.pop()
    P.barrier()

    if PHASE_LIMIT <= 6:
        return nc, P, dict(locals())
    TS3 = 512
    A.push()
    bbcb = [A.alloc(TS3, BF16, name=f"bbcb{j}") for j in range(64)]
    y_sb = [A.alloc(TS3, name=f"ysb{d}") for d in range(16)]
    dlc = [A.alloc(TS3, name=f"dlc{j}") for j in range(2)]
    dxc = [A.alloc(TS3, BF16, name=f"dxc{j}") for j in range(2)]
    da2 = [A.alloc(TS3, BF16, name=f"da{j}") for j in range(2)]
    dbx2 = [A.alloc(TS3, BF16, name=f"dbx{j}") for j in range(2)]
    dbxP = [A.alloc(TS3, BF16, name=f"dbxP{j}") for j in range(2)]
    hti2 = [A.alloc(TS3, BF16, name=f"hti{j}") for j in range(2)]
    ytm = [A.alloc(TS3, BF16, name=f"ytm{j}") for j in range(2)]
    ytmP = [A.alloc(TS3, BF16, name=f"ytmP{j}") for j in range(2)]
    gsc = A.alloc(TS3, name="gsc")

    for s3 in range(T // TS3):
        t0 = s3 * TS3
        for nb in range(2):
            for j in range(32):
                n = nb * 32 + j
                bk = pb[j % 2]
                P.mm(bk.t[:, 0:TS3],
                     ap_e64b[:, n:n + 1].broadcast_to([64, 128]),
                     ap_Bl1[:, t0:t0 + TS3], start=True, stop=True,
                     reads=[eye64bf, bBl1], writes=[bk])
                P.cp(bbcb[j].t, bk.t[:, 0:TS3], reads=[bk], writes=[bbcb[j]],
                     eng="scalar")
                bk = pb[j % 2]
                P.mm(bk.t[:, 0:TS3],
                     ap_e64b[:, n:n + 1].broadcast_to([64, 128]),
                     ap_Cl1[:, t0:t0 + TS3], start=True, stop=True,
                     reads=[eye64bf, bCl1], writes=[bk])
                P.cp(bbcb[32 + j].t, bk.t[:, 0:TS3], reads=[bk],
                     writes=[bbcb[32 + j]], eng="scalar")
            for dc in range(16):
                dbuf = dc % 2
                P.dma("sync", dlc[dbuf].t,
                      delta_fm[dc * 128:(dc + 1) * 128, t0:t0 + TS3],
                      writes=[dlc[dbuf]])
                P.dma("gpsimd", dxc[dbuf].t,
                      dxd_fm[dc * 128:(dc + 1) * 128, t0:t0 + TS3],
                      writes=[dxc[dbuf]])
                yb = pb[2 + (dc % 2)]
                def on_pool(j):
                    return False

                def emit_dbx(j):
                    if on_pool(j):
                        dbb = dbxP[j % 2]
                        P.tt(dbb.t, dxc[dbuf].t, bbcb[j].t, OP.mult,
                             reads=[dxc[dbuf], bbcb[j]], writes=[dbb],
                             eng="gpsimd")
                    else:
                        dbb = dbx2[j % 2]
                        P.tt(dbb.t, dxc[dbuf].t, bbcb[j].t, OP.mult,
                             reads=[dxc[dbuf], bbcb[j]], writes=[dbb])
                    return dbb

                def emit_ytmp(j, ph):
                    if on_pool(j):
                        yt = ytmP[j % 2]
                        P.tt(yt.t, ph.t, bbcb[32 + j].t, OP.mult,
                             reads=[ph, bbcb[32 + j]], writes=[yt],
                             eng="gpsimd")
                    else:
                        yt = ytm[j % 2]
                        P.tt(yt.t, ph.t, bbcb[32 + j].t, OP.mult,
                             reads=[ph, bbcb[32 + j]], writes=[yt])
                    return yt

                dpend = {}
                for j in range(32):
                    n = nb * 32 + j
                    dab = da2[j % 2]
                    P.act(dab.t, dlc[dbuf].t, AF.Exp,
                          scale=a_sb.t[:, dc * 64 + n:dc * 64 + n + 1],
                          reads=[dlc[dbuf], a_sb], writes=[dab])
                    dpend[j] = emit_dbx(j)
                    if j > 0:
                        pn = nb * 32 + j - 1
                        pstcol = st_sb.t[:, dc * 64 + pn:dc * 64 + pn + 1]
                        ph = hti2[(j - 1) % 2]
                        P.cp(pstcol, ph.t[:, TS3 - 1:TS3], reads=[ph],
                             writes=[st_sb], eng="scalar")
                        yt = emit_ytmp(j - 1, ph)
                        P.mm(yb.t[:, 0:TS3], ap_e128b, yt.t, start=(j == 1),
                             stop=False, reads=[eye128bf, yt], writes=[yb])
                    stcol = st_sb.t[:, dc * 64 + n:dc * 64 + n + 1]
                    P.scan(hti2[j % 2].t, dab.t, dpend[j].t, stcol,
                           reads=[dab, dpend[j], st_sb], writes=[hti2[j % 2]])
                pn = nb * 32 + 31
                pstcol = st_sb.t[:, dc * 64 + pn:dc * 64 + pn + 1]
                ph = hti2[31 % 2]
                P.cp(pstcol, ph.t[:, TS3 - 1:TS3], reads=[ph], writes=[st_sb],
                     eng="scalar")
                yt = emit_ytmp(31, ph)
                P.mm(yb.t[:, 0:TS3], ap_e128b, yt.t, start=False, stop=True,
                     reads=[eye128bf, yt], writes=[yb])
                if nb == 0:
                    P.cp(y_sb[dc].t, yb.t[:, 0:TS3], reads=[yb],
                         writes=[y_sb[dc]], eng="scalar")
                else:
                    P.tt(y_sb[dc].t, y_sb[dc].t, yb.t[:, 0:TS3], OP.add,
                         reads=[yb, y_sb[dc]], writes=[y_sb[dc]])
        silr = dlc[0]
        ygt_ap = gsc.t.bitcast(BF16)[:, 0:TS3]
        for dc in range(16):
            P.dma("sync", silr.t, res_fm[dc * 128:(dc + 1) * 128, t0:t0 + TS3],
                  writes=[silr])
            P.act(silr.t, silr.t, AF.Silu, reads=[silr], writes=[silr])
            P.tt(ygt_ap, y_sb[dc].t, silr.t, OP.mult,
                 reads=[y_sb[dc], silr], writes=[gsc])
            P.dma("scalar", yg_bf[dc * 128:(dc + 1) * 128, t0:t0 + TS3],
                  ygt_ap, reads=[gsc])
    A.pop()
    A.pop()
    P.barrier()

    if PHASE_LIMIT <= 7:
        return nc, P, dict(locals())
    TS4 = 512
    A.push()
    w_o1 = [A.alloc(1024, BF16, name=f"wo1_{k}") for k in range(16)]
    for k in range(16):
        P.dma("gpsimd", w_o1[k].t, g["w_l1_outproj"][k * 128:(k + 1) * 128, :],
              writes=[w_o1[k]])
    ygc4 = [A.alloc(TS4, BF16, name=f"yg4_{k}") for k in range(16)]
    x2c4 = [A.alloc(TS4, name=f"x2c4_{m}") for m in range(8)]
    for s4 in range(T // TS4):
        t0 = s4 * TS4
        for k in range(16):
            P.dma("sync", ygc4[k].t, yg_bf[k * 128:(k + 1) * 128, t0:t0 + TS4],
                  writes=[ygc4[k]])
        for m in range(8):
            P.dma("sync", x2c4[m].t, x2_fm[m * 128:(m + 1) * 128, t0:t0 + TS4],
                  writes=[x2c4[m]])
            bk = pb[m % 2]
            for k in range(16):
                P.mm(bk.t[:, 0:TS4], w_o1[k].t[:, m * 128:(m + 1) * 128],
                     ygc4[k].t, start=(k == 0), stop=(k == 15),
                     reads=[w_o1[k], ygc4[k]], writes=[bk])
            P.tt(x2c4[m].t, bk.t[:, 0:TS4], x2c4[m].t, OP.add,
                 reads=[bk, x2c4[m]], writes=[x2c4[m]])
            P.dma("scalar", x3_fm[m * 128:(m + 1) * 128, t0:t0 + TS4],
                  x2c4[m].t, reads=[x2c4[m]])
    A.pop()
    P.barrier()

    if PHASE_LIMIT <= 8:
        return nc, P, dict(locals())
    A.push()
    ld_consts(["v_l1f_b1"])
    weL = A.alloc(T, BF16, name="weL")
    ap_weL = weL.t[0:8, :]
    eye8bf = A.alloc(8, BF16, name="eye8bf")
    ap_e8b = eye8bf.t[0:8, 0:8]
    P.cp(ap_e8b, EYE8.t, reads=[EYE8], writes=[eye8bf])
    A.push()
    mk_ln_rows()
    ld_consts(["v_l1f_rb1", "v_l1f_rb2"])
    rb2B = A.alloc(8, name="rb2B")
    rb2B = Buf(rb2B.t[:, 0:8], "rb2B")
    P.mm(pb[0].t[:, 0:8], ONESR.t, C["v_l1f_rb2"].t, start=True,
         stop=True, reads=[ONESR, C["v_l1f_rb2"]], writes=[pb[0]])
    P.cp(rb2B.t, pb[0].t[:, 0:8], reads=[pb[0]], writes=[rb2B])
    w_r1 = [A.alloc(4096, BF16, name=f"wr1_{k}") for k in range(8)]
    w_r2 = [A.alloc(8, BF16, name=f"wr2_{k}") for k in range(32)]
    for k in range(8):
        P.dma("gpsimd", w_r1[k].t, g["w_l1f_rw1"][k * 128:(k + 1) * 128, :],
              writes=[w_r1[k]])
    for k in range(32):
        P.dma("gpsimd", w_r2[k].t, g["w_l1f_rw2"][k * 128:(k + 1) * 128, :],
              writes=[w_r2[k]])
    TS5 = 512
    x3c = [A.alloc(TS5, name=f"x3c{i}") for i in range(8)]
    xfc5 = [A.alloc(TS5, BF16, name=f"xf5_{i}") for i in range(8)]
    hmid = [A.alloc(TS5, BF16, name=f"hm{m}") for m in range(32)]
    tmp5 = A.alloc(TS5, name="tmp5")
    tp5 = A.alloc(64, name="tp5")
    ap_lg = tp5.t[:, 0:8]
    ap_msk1 = tp5.t[:, 8:16]
    ap_msk2 = tp5.t[:, 16:24]
    ap_d = tp5.t[:, 24:32]
    ap_m1 = tp5.t[:, 32:33]
    ap_m2 = tp5.t[:, 33:34]
    ap_g1 = tp5.t[:, 34:35]
    ap_g2 = tp5.t[:, 35:36]

    def top2_we5(lg_psum_buf, lg_psum_ap, wecols):
        P.stt(ap_lg, lg_psum_ap, 1.0, rb2B.t, OP.mult, OP.add,
              reads=[lg_psum_buf, rb2B], writes=[tp5])
        P.red_max(ap_m1, ap_lg, reads=[tp5], writes=[tp5])
        P.stt(ap_d, ap_lg, 1.0, tp5.t[:, 32:33].broadcast_to([128, 8]),
              OP.mult, OP.subtract, reads=[tp5], writes=[tp5])
        P.act(ap_d, ap_d, AF.Sign, reads=[tp5], writes=[tp5])
        P.tt(ap_msk1, ap_d, ap_d, OP.mult, reads=[tp5], writes=[tp5])
        P.ts(ap_msk1, ap_msk1, -1.0, 1.0, OP.mult, OP.add, reads=[tp5],
             writes=[tp5])
        P.stt(ap_d, ap_msk1, -1e30, ap_lg, OP.mult, OP.add, reads=[tp5],
              writes=[tp5])
        P.red_max(ap_m2, ap_d, reads=[tp5], writes=[tp5])
        P.stt(ap_d, ap_d, 1.0, tp5.t[:, 33:34].broadcast_to([128, 8]),
              OP.mult, OP.subtract, reads=[tp5], writes=[tp5])
        P.act(ap_d, ap_d, AF.Sign, reads=[tp5], writes=[tp5])
        P.tt(ap_msk2, ap_d, ap_d, OP.mult, reads=[tp5], writes=[tp5])
        P.ts(ap_msk2, ap_msk2, -1.0, 1.0, OP.mult, OP.add, reads=[tp5],
             writes=[tp5])
        P.tt(ap_d[:, 0:1], ap_m1, ap_m2, OP.subtract, reads=[tp5], writes=[tp5])
        P.act(ap_g1, ap_d[:, 0:1], AF.Sigmoid, reads=[tp5], writes=[tp5])
        P.ts(ap_g2, ap_g1, -1.0, 1.0, OP.mult, OP.add, reads=[tp5], writes=[tp5])
        P.ts(ap_msk1, ap_msk1, tp5.t[:, 34:35], None, OP.mult, reads=[tp5],
             writes=[tp5])
        P.stt(ap_msk2, ap_msk2, tp5.t[:, 35:36], ap_msk1, OP.mult, OP.add,
              reads=[tp5], writes=[tp5])
        P.transpose(pb[7].t[0:8, 0:128], ap_msk2, EYE.t, reads=[tp5, EYE],
                    writes=[pb[7]])
        P.cp(ap_weL[:, wecols], pb[7].t[0:8, 0:128], reads=[pb[7]],
             writes=[weL])

    for s5 in range(T // TS5):
        t0 = s5 * TS5
        for i in range(8):
            P.dma("sync", x3c[i].t, x3_fm[i * 128:(i + 1) * 128, t0:t0 + TS5],
                  writes=[x3c[i]])

        def outr(i, mB, rB):
            P.tt(tmp5.t, x3c[i].t, mB.t[:, 0:TS5], OP.subtract,
                 reads=[x3c[i], mB], writes=[tmp5])
            P.tt(tmp5.t, tmp5.t, rB.t[:, 0:TS5], OP.mult,
                 reads=[tmp5, rB], writes=[tmp5])
            P.cp(xfc5[i].t, tmp5.t, reads=[tmp5], writes=[xfc5[i]])
            P.dma("scalar", xf_bfs[i * 128:(i + 1) * 128, t0:t0 + TS5],
                  xfc5[i].t, reads=[xfc5[i]])
        ln_feature_major(x3c, [b.t for b in x3c], TS5, 1e-5, tmp5,
                         (pb[0], pb[1]), outr)
        for m in range(32):
            bk = pb[2 + (m % 2)]
            for k in range(8):
                P.mm(bk.t[:, 0:TS5], w_r1[k].t[:, m * 128:(m + 1) * 128],
                     xfc5[k].t, start=(k == 0), stop=(k == 7),
                     reads=[w_r1[k], xfc5[k]], writes=[bk])
            P.act(hmid[m].t, bk.t[:, 0:TS5], AF.Silu,
                  bias=C["v_l1f_rb1"].t[:, m:m + 1],
                  reads=[bk, C["v_l1f_rb1"]], writes=[hmid[m]])
        for tc in range(4):
            tokap = slice(tc * 128, (tc + 1) * 128)
            for k in range(32):
                P.mm(pb[6].t[:, 0:8], hmid[k].t[:, tokap], w_r2[k].t,
                     start=(k == 0), stop=(k == 31),
                     reads=[hmid[k], w_r2[k]], writes=[pb[6]])
            top2_we5(pb[6], pb[6].t[:, 0:8],
                     slice(t0 + tc * 128, t0 + (tc + 1) * 128))
    P.dma("scalar", we_sc, ap_weL, reads=[weL])
    A.pop()
    P.barrier()

    if PHASE_LIMIT <= 9:
        return nc, P, dict(locals())
    A.push()
    w1e = [A.alloc(4096, BF16, name=f"w1e{k}") for k in range(8)]
    w2e = [A.alloc(1024, BF16, name=f"w2e{k}") for k in range(32)]
    TSE = 512
    xfe = [A.alloc(TSE, BF16, name=f"xfe{k}") for k in range(8)]
    hstore = [A.alloc(TSE, BF16, name=f"hst{m}") for m in range(32)]
    wcb5 = A.alloc(TSE, BF16, name="wcb5")
    oev = [A.alloc(TSE, BF16, name=f"oev{j}") for j in range(2)]
    for e in range(8):
        for k in range(8):
            P.dma("gpsimd", w1e[k].t, g["w_l1f_w1"][e, k * 128:(k + 1) * 128, :],
                  writes=[w1e[k]])
        for k in range(32):
            P.dma("gpsimd", w2e[k].t, g["w_l1f_w2"][e, k * 128:(k + 1) * 128, :],
                  writes=[w2e[k]])
        for se in range(T // TSE):
            t0 = se * TSE
            for k in range(8):
                P.dma("sync", xfe[k].t,
                      xf_bfs[k * 128:(k + 1) * 128, t0:t0 + TSE],
                      writes=[xfe[k]])
            P.mm(pb[0].t[:, 0:TSE],
                 ap_e8b[:, e:e + 1].broadcast_to([8, 128]),
                 ap_weL[:, t0:t0 + TSE], start=True, stop=True,
                 reads=[eye8bf, weL], writes=[pb[0]])
            P.cp(wcb5.t, pb[0].t[:, 0:TSE], reads=[pb[0]], writes=[wcb5])

            def h_mm(m):
                bk = pb[2 + (m % 2)]
                for k in range(8):
                    last = (k == 7)
                    P.mm(bk.t[:, 0:TSE], w1e[k].t[:, m * 128:(m + 1) * 128],
                         xfe[k].t, start=(k == 0), stop=last,
                         reads=(w1e + xfe) if last else [w1e[k], xfe[k]],
                         writes=[bk], track=last)
            h_mm(0)
            for m in range(32):
                bk = pb[2 + (m % 2)]
                P.act(hstore[m].t, bk.t[:, 0:TSE], AF.Silu,
                      bias=g_b1_col(C, m, e),
                      reads=[bk, C["v_l1f_b1"]], writes=[hstore[m]])
                if m + 1 < 32:
                    h_mm(m + 1)
            for half in range(2):
                for m in range(32):
                    for mp4 in range(4):
                        mp = half * 4 + mp4
                        ob = pb[4 + mp4]
                        last = (m == 31)
                        full = last and (mp4 == 3)
                        P.mm(ob.t[:, 0:TSE],
                             w2e[m].t[:, mp * 128:(mp + 1) * 128],
                             hstore[m].t, start=(m == 0), stop=last,
                             reads=(w2e + hstore) if full
                             else [w2e[m], hstore[m]],
                             writes=[ob], track=last)
                for mp4 in range(4):
                    mp = half * 4 + mp4
                    ob = pb[4 + mp4]
                    ov = oev[mp4 % 2]
                    P.tt(ov.t, ob.t[:, 0:TSE], wcb5.t, OP.mult,
                         reads=[ob, wcb5], writes=[ov])
                    P.dma("scalar",
                          oparts[e, mp * 128:(mp + 1) * 128, t0:t0 + TSE],
                          ov.t, reads=[ov])
    A.pop()
    A.pop()
    P.barrier()

    if PHASE_LIMIT <= 10:
        return nc, P, dict(locals())
    A.push()
    TSR = 512
    accp = [A.alloc(TSR, name=f"accp{j}") for j in range(8)]
    x3r = A.alloc(TSR, name="x3r")
    b2l = A.alloc(1024, name="b2l")
    ap_b2 = b2l.t[0:8, 0:1024]
    P.dma("sync", ap_b2, g["w_l1f_b2"], writes=[b2l])
    wer = A.alloc(TSR, name="wer")
    ap_wer = wer.t[0:8, :]
    for sr in range(T // TSR):
        t0 = sr * TSR
        P.dma("gpsimd", ap_wer, we_sc[:, t0:t0 + TSR], writes=[wer])
        for mp in range(8):
            bk = pb[mp % 2]
            P.mm(bk.t[:, 0:TSR], ap_b2[:, mp * 128:(mp + 1) * 128], ap_wer,
                 start=True, stop=True, reads=[b2l, wer], writes=[bk])
            P.dma("sync", x3r.t, x3_fm[mp * 128:(mp + 1) * 128, t0:t0 + TSR],
                  writes=[x3r])
            for e in range(8):
                P.dma("gpsimd", accp[e].t,
                      oparts[e, mp * 128:(mp + 1) * 128, t0:t0 + TSR],
                      writes=[accp[e]])
            for e in range(1, 8):
                P.tt(accp[0].t, accp[0].t, accp[e].t, OP.add,
                     reads=[accp[0], accp[e]], writes=[accp[0]])
            P.tt(x3r.t, x3r.t, accp[0].t, OP.add, reads=[x3r, accp[0]],
                 writes=[x3r])
            P.tt(x3r.t, x3r.t, bk.t[:, 0:TSR], OP.add, reads=[x3r, bk],
                 writes=[x3r])
            P.dma("scalar", out_fm[mp * 128:(mp + 1) * 128, t0:t0 + TSR],
                  x3r.t, reads=[x3r])
    A.pop()
    P.barrier()
    return nc, P, dict(locals())


def g_b1_col(C, m, e):
    return C["v_l1f_b1"].t[:, m * 8 + e:m * 8 + e + 1]


def finish_program(nc, P, env):
    with nc.Block() as block:
        P.finalize(block)
    return nc


_CACHE = {}


def kernel(**inputs):
    core_map = prep_inputs(inputs)
    key = "prog"
    if key not in _CACHE:
        nc, P, env = build_program()
        finish_program(nc, P, env)
        _CACHE[key] = (nc, env)
    nc, env = _CACHE[key]
    in_maps = [core_map(b) for b in range(NCORES)]
    res = run_bass_kernel_spmd(nc, in_maps, core_ids=list(range(NCORES)))
    out = np.stack([np.ascontiguousarray(np.asarray(r["out_fm"]).T)
                    for r in res.results])
    return out.astype(np.float32)


# version 05 (final)
# speedup vs baseline: 2.0823x; 1.0901x over previous
import numpy as np
import ml_dtypes
import concourse.bass as bass
import concourse.mybir as mybir
from concourse.bass_utils import run_bass_kernel_spmd
from concourse.alu_op_type import AluOpType as OP

F32 = mybir.dt.float32
BF16 = mybir.dt.bfloat16
AF = mybir.ActivationFunctionType
BFNP = ml_dtypes.bfloat16

T = 4096
D = 1024
NCORES = 8
MAGIC = float(1.5 * 2 ** 23)
PI = float(np.pi)

DEBUG_DUMP = set()
PHASE_LIMIT = 99


class Buf:
    __slots__ = ("t", "wr", "rd", "name")

    def __init__(self, ap, name=""):
        self.t = ap
        self.wr = None
        self.rd = []
        self.name = name

    def __getitem__(self, key):
        return self.t[key]


class Prog:
    ENG = ("tensor", "vector", "scalar", "gpsimd", "sync")

    def __init__(self, nc, stack):
        self.nc = nc
        self.q = {e: [] for e in self.ENG}
        self.sems = {}
        for name in ("T", "V", "A", "G", "DS", "DG", "DA",
                     "FS", "FG", "FA"):
            self.sems[name] = stack.enter_context(nc.semaphore("sem" + name))
        self.cnt = {k: 0 for k in self.sems}
        self.opidx = {e: 0 for e in self.ENG}
        self.val2idx = {k: {} for k in self.sems}
        self.waited = {}
        self.csem = {"tensor": "T", "vector": "V", "scalar": "A",
                     "gpsimd": "G", "sync": None}
        self.dsem = {"sync": "DS", "gpsimd": "DG", "scalar": "DA"}
        self.fsem = {"sync": "FS", "gpsimd": "FG", "scalar": "FA"}
        self.sem_eng = {"T": "tensor", "V": "vector", "A": "scalar",
                        "G": "gpsimd", "DS": "sync", "DG": "gpsimd",
                        "DA": "scalar", "FS": "sync", "FG": "gpsimd",
                        "FA": "scalar"}
        self.dma_issue = {"sync": 0, "gpsimd": 0, "scalar": 0}
        self.flushed_issue = {"sync": 0, "gpsimd": 0, "scalar": 0}

    def _flag_for(self, queue, idx):
        fname = self.fsem[queue]
        if self.flushed_issue[queue] < idx:
            dname = self.dsem[queue]
            need = 16 * self.dma_issue[queue]
            self.cnt[fname] += 1
            fval = self.cnt[fname]
            sems = self.sems

            def runner(eng, dn=dname, nv=need, fs=fname, fv=fval):
                eng.wait_ge(sems[dn], nv)
                eng.nop().then_inc(sems[fs], 1)

            self.q[queue].append(runner)
            self.flushed_issue[queue] = self.dma_issue[queue]
            self.opidx[queue] += 1
            self.val2idx[fname][fval] = self.opidx[queue] - 1
        return (fname, self.cnt[fname])

    def _emit(self, engine, fn, deps, dma):
        sem_name = self.dsem[engine] if dma else self.csem[engine]
        assert sem_name is not None, f"{engine} cannot emit dma={dma}"
        step = 16 if dma else 1
        deps2 = []
        for (dsname, dval) in deps:
            if dval is None:
                continue
            if dsname.startswith("Q"):
                queue = dsname[1:]
                if engine == queue:
                    deps2.append(self._flag_for(queue, dval))
                else:
                    deps2.append(self._flag_for(queue, dval))
            else:
                deps2.append((dsname, dval))
        deps = deps2
        myidx = self.opidx[engine]
        waits = []
        need_drain = False
        for (dsname, dval) in deps:
            if dval is None:
                continue
            if dsname == sem_name and not dma and self.sem_eng[dsname] == engine:
                pidx = self.val2idx[dsname].get(dval, -10)
                if pidx >= myidx - 1:
                    need_drain = True
                continue
            if self.sem_eng[dsname] == engine and not (dsname in ("DS", "DG", "DA")):
                pidx = self.val2idx[dsname].get(dval, -10)
                if pidx >= myidx - 1:
                    need_drain = True
                continue
            key = (engine, dsname)
            if self.waited.get(key, -1) >= dval:
                continue
            self.waited[key] = dval
            waits.append((dsname, dval))
        self.cnt[sem_name] += step
        if dma:
            self.dma_issue[engine] += 1
            token = ("Q" + engine, self.dma_issue[engine])
        else:
            token = (sem_name, self.cnt[sem_name])
        self.val2idx[sem_name][self.cnt[sem_name]] = myidx
        self.opidx[engine] += 1
        sems = self.sems

        def runner(eng, waits=waits, need_drain=need_drain, fn=fn,
                   sem=sems[sem_name], step=step):
            for (wn, wv) in waits:
                eng.wait_ge(sems[wn], wv)
            if need_drain and hasattr(eng, "drain"):
                eng.drain()
            ins = fn(eng)
            ins.then_inc(sem, step)

        self.q[engine].append(runner)
        return token

    def op(self, engine, fn, reads=(), writes=(), deps=(), dma=False,
           track=True):
        dl = list(deps)
        for b in reads:
            if b.wr is not None:
                dl.append(b.wr)
        for b in writes:
            if b.wr is not None:
                dl.append(b.wr)
            dl.extend(b.rd)
        if not track:
            self._emit_notrack(engine, fn, dl)
            return None
        tok = self._emit(engine, fn, dl, dma)
        for b in reads:
            b.rd.append(tok)
        for b in writes:
            b.wr = tok
            b.rd = []
        return tok

    def _emit_notrack(self, engine, fn, deps):
        deps2 = []
        for (dsname, dval) in deps:
            if dval is None:
                continue
            if dsname.startswith("Q"):
                deps2.append(self._flag_for(dsname[1:], dval))
            else:
                deps2.append((dsname, dval))
        waits = []
        for (dsname, dval) in deps2:
            if self.sem_eng[dsname] == engine and dsname not in ("DS", "DG", "DA"):
                continue
            key = (engine, dsname)
            if self.waited.get(key, -1) >= dval:
                continue
            self.waited[key] = dval
            waits.append((dsname, dval))
        self.opidx[engine] += 1
        sems = self.sems

        def runner(eng, waits=waits, fn=fn):
            for (wn, wv) in waits:
                eng.wait_ge(sems[wn], wv)
            fn(eng)

        self.q[engine].append(runner)

    def barrier(self):
        snap = dict(self.cnt)
        for e in self.ENG:
            waits = []
            for sn, sv in snap.items():
                if sv == 0:
                    continue
                if self.sem_eng[sn] == e and sn not in ("DS", "DG", "DA"):
                    continue
                key = (e, sn)
                if self.waited.get(key, -1) >= sv:
                    continue
                self.waited[key] = sv
                waits.append((sn, sv))
            if waits:
                sems = self.sems

                def runner(eng, waits=waits):
                    for (wn, wv) in waits:
                        eng.wait_ge(sems[wn], wv)

                self.q[e].append(runner)

    def finalize(self, block):
        for e in self.ENG:
            ops = self.q[e]

            def body(eng, ops=ops):
                for r in ops:
                    r(eng)

            getattr(block, e)(body)

    def mm(self, out_ap, lhsT, rhs, start, stop, reads=(), writes=(), deps=(),
           track=True):
        return self.op("tensor",
                       lambda t: t.matmul(out_ap, lhsT, rhs, start=start, stop=stop),
                       reads=reads, writes=writes, deps=deps, track=track)

    def mm_group(self, items, writes, accum=False, stop=True, start=True):
        allreads = []
        for (_, _, _, rb) in items:
            for b in rb:
                if b not in allreads:
                    allreads.append(b)
        n = len(items)
        for i, (out_ap, lhsT, rhs, rb) in enumerate(items):
            last = (i == n - 1)
            self.mm(out_ap, lhsT, rhs,
                    start=(start and i == 0), stop=(stop and last),
                    reads=allreads if last else rb,
                    writes=writes, track=last)

    def transpose(self, out_ap, in_ap, ident_ap, reads=(), writes=(), deps=()):
        return self.op("tensor",
                       lambda t: t.transpose(out_ap, in_ap, ident_ap),
                       reads=reads, writes=writes, deps=deps)

    def cp(self, out_ap, in_ap, reads=(), writes=(), deps=(), eng="vector"):
        if eng == "scalar":
            return self.op(eng, lambda s: s.copy(out_ap, in_ap),
                           reads=reads, writes=writes, deps=deps)
        return self.op(eng, lambda v: v.tensor_copy(out_ap, in_ap),
                       reads=reads, writes=writes, deps=deps)

    def tt(self, out_ap, a, b, opk, reads=(), writes=(), deps=(), eng="vector"):
        return self.op(eng, lambda v: v.tensor_tensor(out_ap, a, b, opk),
                       reads=reads, writes=writes, deps=deps)

    def ts(self, out_ap, in_ap, s1, s2, op0, op1=None, reads=(), writes=(),
           deps=(), eng="vector"):
        if op1 is None:
            return self.op(eng, lambda v: v.tensor_scalar(out_ap, in_ap, s1, None, op0),
                           reads=reads, writes=writes, deps=deps)
        return self.op(eng,
                       lambda v: v.tensor_scalar(out_ap, in_ap, s1, s2, op0, op1),
                       reads=reads, writes=writes, deps=deps)

    def stt(self, out_ap, in0, scalar, in1, op0, op1, reads=(), writes=(),
            deps=(), eng="vector"):
        return self.op(eng,
                       lambda v: v.scalar_tensor_tensor(out_ap, in0, scalar, in1, op0, op1),
                       reads=reads, writes=writes, deps=deps)

    def act(self, out_ap, in_ap, func, bias=None, scale=None, reads=(),
            writes=(), deps=()):
        kw = {}
        if bias is not None:
            kw["bias"] = bias
        if scale is not None:
            kw["scale"] = scale
        return self.op("scalar",
                       lambda s: s.activation(out_ap, in_ap, func, **kw),
                       reads=reads, writes=writes, deps=deps)

    def scan(self, out_ap, d0, d1, init, reads=(), writes=(), deps=()):
        return self.op("vector",
                       lambda v: v.tensor_tensor_scan(out_ap, d0, d1, init,
                                                      OP.mult, OP.add),
                       reads=reads, writes=writes, deps=deps)

    def red_max(self, out_ap, in_ap, reads=(), writes=(), deps=()):
        return self.op("vector",
                       lambda v: v.reduce_max(out_ap, in_ap,
                                              axis=mybir.AxisListType.X),
                       reads=reads, writes=writes, deps=deps)

    def recip(self, out_ap, in_ap, reads=(), writes=(), deps=()):
        return self.op("vector", lambda v: v.reciprocal(out_ap, in_ap),
                       reads=reads, writes=writes, deps=deps)

    def memset(self, ap, val, reads=(), writes=(), deps=(), eng="vector"):
        return self.op(eng, lambda v: v.memset(ap, val),
                       reads=reads, writes=writes, deps=deps)

    def dma(self, queue, out_ap, in_ap, reads=(), writes=(), deps=()):
        return self.op(queue, lambda q: q.dma_start(out=out_ap, in_=in_ap),
                       reads=reads, writes=writes, deps=deps, dma=True)


class Arena:

    def __init__(self, ap, words):
        self.ap = ap
        self.words = words
        self.off = 0
        self.marks = []

    def alloc(self, words, dtype=F32, parts=128, name=""):
        w32 = words if dtype == F32 else (words + 1) // 2
        w32 = (w32 + 1) & ~1
        assert self.off + w32 <= self.words, \
            f"arena overflow: {self.off}+{w32} > {self.words} ({name})"
        sl = self.ap[0:parts, self.off:self.off + w32]
        self.off += w32
        if dtype != F32:
            sl = sl.bitcast(dtype)
        return Buf(sl, name)

    def push(self):
        self.marks.append(self.off)

    def pop(self):
        self.off = self.marks.pop()


def prep_inputs(ws):
    f32 = lambda a: np.ascontiguousarray(a, dtype=np.float32)
    sh = {}
    sh["c_eye128"] = np.eye(128, dtype=np.float32)
    sh["c_eye64"] = np.eye(64, dtype=np.float32)
    sh["c_eye16"] = np.eye(16, dtype=np.float32)
    sh["c_eye32"] = np.eye(32, dtype=np.float32)
    sh["c_eye8"] = np.eye(8, dtype=np.float32)
    sh["c_ones_col"] = np.ones((128, 1), dtype=np.float32)
    sh["c_ones_row"] = np.ones((1, 128), dtype=np.float32)
    q = np.arange(128)
    sh["c_utri"] = np.where(q[:, None] <= q[None, :], 0.0, -1e5).astype(np.float32)
    sh["c_strict"] = (q[:, None] < q[None, :]).astype(np.float32)
    sh["w_l0_inproj"] = f32(ws["L0_attn_in_proj_w"])
    sh["w_l0_outproj"] = f32(ws["L0_attn_out_proj_w"])
    sh["w_l0_theta"] = f32(ws["L0_attn_theta_w"])
    sh["v_l0_ln_s"] = f32(ws["L0_attn_norm_scale"]).reshape(8, 128).T.copy()
    sh["v_l0_ln_b"] = f32(ws["L0_attn_norm_bias"]).reshape(8, 128).T.copy()
    sh["v_l0_alog"] = f32(ws["L0_attn_A_log"]).reshape(16, 1)
    sh["v_l0_dtb"] = f32(ws["L0_attn_dt_bias"]).reshape(16, 1)
    sh["v_l0_bcB"] = f32(ws["L0_attn_bcB_scale"]).reshape(64, 1)
    sh["v_l0_bcC"] = f32(ws["L0_attn_bcC_scale"]).reshape(64, 1)
    sh["v_l0_Bb"] = f32(ws["L0_attn_B_bias"]).reshape(64, 1)
    sh["v_l0_Cb"] = f32(ws["L0_attn_C_bias"]).reshape(64, 1)
    sh["v_l0_D"] = f32(ws["L0_attn_D"]).reshape(1, 16)
    sh["w_l0f_gate"] = f32(ws["L0_ffn_w_gate"])
    sh["w_l0f_Wd"] = f32(ws["L0_ffn_Wd"])
    sh["w_l0f_Wu"] = f32(ws["L0_ffn_Wu"])
    sh["v_l0f_bd"] = f32(ws["L0_ffn_bd"]).T.copy()
    sh["w_l0f_bu"] = f32(ws["L0_ffn_bu"])
    sh["w_l1_inproj"] = f32(ws["L1_attn_in_proj_w"].T)
    sh["w_l1_xproj"] = f32(ws["L1_attn_x_proj_w"].T)
    sh["w_l1_dtproj"] = f32(ws["L1_attn_dt_proj_w"].T)
    sh["w_l1_outproj"] = f32(ws["L1_attn_out_proj_w"].T)
    cw = f32(ws["L1_attn_conv_w"]).reshape(2048, 4)
    sh["v_l1_convw"] = cw.reshape(16, 128, 4).transpose(1, 0, 2).reshape(128, 64).copy()
    sh["v_l1_convb"] = f32(ws["L1_attn_conv_b"]).reshape(16, 128).T.copy()
    sh["v_l1_dtpb"] = f32(ws["L1_attn_dt_proj_b"]).reshape(16, 128).T.copy()
    sh["w_l1_alog"] = f32(ws["L1_attn_A_log"])
    sh["w_l1f_rw1"] = f32(ws["L1_ffn_rw1"])
    sh["w_l1f_rw2"] = f32(ws["L1_ffn_rw2"])
    sh["v_l1f_rb1"] = f32(ws["L1_ffn_rb1"]).reshape(32, 128).T.copy()
    sh["v_l1f_rb2"] = f32(ws["L1_ffn_rb2"]).reshape(1, 8)
    sh["w_l1f_w1"] = f32(ws["L1_ffn_w1"])
    sh["w_l1f_w2"] = f32(ws["L1_ffn_w2"])
    sh["v_l1f_b1"] = f32(ws["L1_ffn_b1"]).T.reshape(32, 128, 8).transpose(1, 0, 2).reshape(128, 256).copy()
    sh["w_l1f_b2"] = f32(ws["L1_ffn_b2"])

    u = np.asarray(ws["L0_attn_u"], dtype=np.float32)

    def core_map(b):
        m = dict(sh)
        m["u_fm"] = np.ascontiguousarray(u[b].T)
        return m

    return core_map


def build_program():
    import contextlib
    nc = bass.Bass("TRN2", target_bir_lowering=False)
    stack = contextlib.ExitStack()

    def ext_in(name, shape, dt=F32):
        return nc.dram_tensor(name, shape, dt, kind="ExternalInput").ap()

    def scratch(name, shape, dt=F32):
        kind = "ExternalOutput" if name in DEBUG_DUMP else "Internal"
        return nc.dram_tensor(name, shape, dt, kind=kind).ap()

    u_fm = ext_in("u_fm", [D, T])
    out_fm = nc.dram_tensor("out_fm", [D, T], F32, kind="ExternalOutput").ap()

    g = {}
    for name, shape in [
        ("c_eye128", [128, 128]), ("c_eye64", [64, 64]), ("c_eye16", [16, 16]),
        ("c_eye32", [32, 32]), ("c_eye8", [8, 8]), ("c_ones_col", [128, 1]), ("c_ones_row", [1, 128]),
        ("c_utri", [128, 128]), ("c_strict", [128, 128]),
        ("w_l0_inproj", [1024, 4256]), ("w_l0_outproj", [2048, 1024]),
        ("w_l0_theta", [1024, 32]),
        ("v_l0_ln_s", [128, 8]), ("v_l0_ln_b", [128, 8]),
        ("v_l0_alog", [16, 1]), ("v_l0_dtb", [16, 1]),
        ("v_l0_bcB", [64, 1]), ("v_l0_bcC", [64, 1]),
        ("v_l0_Bb", [64, 1]), ("v_l0_Cb", [64, 1]), ("v_l0_D", [1, 16]),
        ("w_l0f_gate", [1024, 8]), ("w_l0f_Wd", [8, 1024, 64]),
        ("w_l0f_Wu", [8, 64, 1024]), ("v_l0f_bd", [64, 8]), ("w_l0f_bu", [8, 1024]),
        ("w_l1_inproj", [1024, 4096]), ("w_l1_xproj", [2048, 128]),
        ("w_l1_dtproj", [2048, 2048]), ("w_l1_outproj", [2048, 1024]),
        ("v_l1_convw", [128, 64]), ("v_l1_convb", [128, 16]),
        ("v_l1_dtpb", [128, 16]), ("w_l1_alog", [2048, 64]),
        ("w_l1f_rw1", [1024, 4096]), ("w_l1f_rw2", [4096, 8]),
        ("v_l1f_rb1", [128, 32]), ("v_l1f_rb2", [1, 8]),
        ("w_l1f_w1", [8, 1024, 4096]), ("w_l1f_w2", [8, 4096, 1024]),
        ("v_l1f_b1", [128, 256]), ("w_l1f_b2", [8, 1024]),
    ]:
        g[name] = ext_in(name, shape)

    x1_fm = scratch("x1_fm", [D, T])
    x2_fm = scratch("x2_fm", [D, T])
    x3_fm = scratch("x3_fm", [D, T])
    xin_bf = scratch("xin_bf", [D, T], BF16)
    xu_fm = scratch("xu_fm", [D, T])
    y0_bf = scratch("y0_bf", [2048, T], BF16)
    xs_fm = scratch("xs_fm", [2048, T])
    res_fm = scratch("res_fm", [2048, T])
    delta_fm = scratch("delta_fm", [2048, T])
    dxd_fm = scratch("dxd_fm", [2048, T], BF16)
    yg_bf = scratch("yg_bf", [2048, T], BF16)
    xf_bfs = scratch("xf_bfs", [D, T], BF16)
    oparts = scratch("oparts", [8, D, T], BF16)
    we_sc = scratch("we_sc", [8, T], BF16)

    ARENA_WORDS = 47104
    arena_t = stack.enter_context(nc.sbuf_tensor("arena", [128, ARENA_WORDS], F32))
    psum = [stack.enter_context(nc.psum_tensor(f"pb{i}", [128, 512], F32))
            for i in range(8)]
    pb = [Buf(p[:], f"pb{i}") for i, p in enumerate(psum)]

    P = Prog(nc, stack)
    A = Arena(arena_t[:], ARENA_WORDS)

    C = {}
    CONST_SHAPES = {
        "c_eye128": [128, 128], "c_eye64": [64, 64], "c_eye16": [16, 16],
        "c_eye32": [32, 32], "c_eye8": [8, 8],
        "c_ones_col": [128, 1], "c_ones_row": [1, 128],
        "c_utri": [128, 128], "c_strict": [128, 128],
        "v_l0_ln_s": [128, 8], "v_l0_ln_b": [128, 8],
        "v_l0_alog": [16, 1], "v_l0_dtb": [16, 1],
        "v_l0_bcB": [64, 1], "v_l0_bcC": [64, 1],
        "v_l0_Bb": [64, 1], "v_l0_Cb": [64, 1], "v_l0_D": [1, 16],
        "v_l0f_bd": [64, 8],
        "v_l1_convw": [128, 64], "v_l1_convb": [128, 16],
        "v_l1_dtpb": [128, 16],
        "v_l1f_rb1": [128, 32], "v_l1f_rb2": [1, 8],
        "v_l1f_b1": [128, 256],
    }

    def ld_consts(names):
        for name in names:
            shape = CONST_SHAPES[name]
            b = A.alloc(shape[1], F32, name=name)
            C[name] = Buf(b.t[0:shape[0], 0:shape[1]], name)
            P.dma("sync", C[name].t, g[name], writes=[C[name]])

    ld_consts(["c_eye128", "c_eye64", "c_eye16", "c_eye32", "c_eye8",
               "c_ones_col", "c_ones_row"])
    ln_state = {}

    def mk_ln_rows():
        ln_state["buf"] = A.alloc(1024, name="ln_rows")

    EYE = C["c_eye128"]
    EYE64 = C["c_eye64"]
    EYE16 = C["c_eye16"]
    EYE32 = C["c_eye32"]
    EYE8 = C["c_eye8"]
    ONESC = C["c_ones_col"]
    ONESR = C["c_ones_row"]

    def ln_feature_major(xbufs, xaps, tc, eps, tmp, rows2, out_cb,
                         nchunks=8, dsize=1024.0):
        ln_rows = ln_state["buf"]
        bs, br = rows2
        for i in range(nchunks):
            P.mm(bs.t[0:1, 0:tc], ONESC.t, xaps[i], start=(i == 0),
                 stop=(i == nchunks - 1), reads=[xbufs[i], ONESC], writes=[bs])
        for i in range(nchunks):
            P.act(tmp.t[:, 0:tc], xaps[i], AF.Square, reads=[xbufs[i]],
                  writes=[tmp])
            P.mm(br.t[0:1, 0:tc], ONESC.t, tmp.t[:, 0:tc], start=(i == 0),
                 stop=(i == nchunks - 1), reads=[tmp, ONESC], writes=[br])
        mrow = ln_rows.t[0:1, 0:tc]
        rrow = ln_rows.t[0:1, 512:512 + tc]
        t1 = pb[2].t[0:1, 0:tc]
        P.ts(mrow, bs.t[0:1, 0:tc], 1.0 / dsize, None, OP.mult,
             reads=[bs], writes=[ln_rows])
        P.ts(rrow, br.t[0:1, 0:tc], 1.0 / dsize, None, OP.mult,
             reads=[br], writes=[ln_rows])
        P.tt(t1, mrow, mrow, OP.mult, reads=[ln_rows], writes=[pb[2]])
        P.tt(rrow, rrow, t1, OP.subtract, reads=[ln_rows, pb[2]],
             writes=[ln_rows])
        P.ts(rrow, rrow, eps, None, OP.add, reads=[ln_rows], writes=[ln_rows])
        P.recip(rrow, rrow, reads=[ln_rows], writes=[ln_rows])
        P.act(rrow, rrow, AF.Sqrt, reads=[ln_rows], writes=[ln_rows])
        P.mm(bs.t[:, 0:tc], ONESR.t, mrow, start=True, stop=True,
             reads=[ln_rows, ONESR], writes=[bs])
        P.mm(br.t[:, 0:tc], ONESR.t, rrow, start=True, stop=True,
             reads=[ln_rows, ONESR], writes=[br])
        for i in range(nchunks):
            out_cb(i, bs, br)

    NCH = T // 128

    A.push()
    ld_consts(["c_utri", "c_strict", "v_l0_ln_s", "v_l0_ln_b", "v_l0_alog",
               "v_l0_dtb", "v_l0_bcB", "v_l0_bcC", "v_l0_Bb", "v_l0_Cb",
               "v_l0_D"])
    UTRI = C["c_utri"]
    STRICT = C["c_strict"]
    A16 = A.alloc(2, F32, name="A16")
    A16 = Buf(A16.t[0:16, 0:1], "A16")
    P.act(A16.t, C["v_l0_alog"].t, AF.Exp, reads=[C["v_l0_alog"]], writes=[A16])
    P.ts(A16.t, A16.t, -1.0, None, OP.mult, reads=[A16], writes=[A16])
    w0 = A.alloc(T, name="cc_fm")
    ap_cc = w0.t[0:16, :]
    wex = A.alloc(T, name="expcc_fm")
    ap_expcc = wex.t[0:16, :]
    wB1 = A.alloc(T, name="B1_fm")
    ap_B1 = wB1.t[0:64, :]
    w1 = A.alloc(T, name="C1_fm")
    ap_C1 = w1.t[0:64, :]
    expic0 = A.alloc(32, name="expic0")
    ap_expic0 = expic0.t[0:16, 0:32]
    colsT = {nm: A.alloc(NCH * 16, name="colsT_" + nm)
             for nm in ("negcc", "u", "w", "exprc")}
    B_tm = A.alloc(NCH * 64, name="B_tm")
    DI = A.alloc(16 * 128, BF16, name="DI")
    phic = A.alloc(2, name="phic")
    ap_phic = phic.t[0:32, 0:1]
    P.memset(ap_phic, 0.0, writes=[phic])

    for h in range(16):
        P.mm(pb[6].t[:, 0:1], ONESR.t, C["v_l0_D"].t[0:1, h:h + 1], start=True,
             stop=True, reads=[C["v_l0_D"], ONESR], writes=[pb[6]])
        P.ts(DI.t[:, h * 128:(h + 1) * 128], EYE.t, pb[6].t[:, 0:1],
             None, OP.mult, reads=[EYE, pb[6]], writes=[DI])

    if PHASE_LIMIT <= 0:
        return nc, P, dict(locals())
    TSL = 256
    NSL = T // TSL
    A.push()
    mk_ln_rows()
    x8 = [A.alloc(TSL, name=f"x8_{i}") for i in range(8)]
    u8 = [A.alloc(TSL, name=f"u8_{i}") for i in range(8)]
    tmpc = A.alloc(TSL, name="tmpc")
    xin8 = [A.alloc(TSL, BF16, name=f"xin_{i}") for i in range(8)]
    wsm = [A.alloc(160, BF16, name=f"wsm{k}") for k in range(8)]
    wth = [A.alloc(32, name=f"wth{k}") for k in range(8)]
    for k in range(8):
        P.dma("gpsimd", wsm[k].t, g["w_l0_inproj"][k * 128:(k + 1) * 128, 4096:4256],
              writes=[wsm[k]])
        P.dma("sync", wth[k].t[:, 0:32], g["w_l0_theta"][k * 128:(k + 1) * 128, :],
              writes=[wth[k]])
    theta_sl = A.alloc(TSL, name="theta_sl")
    ap_theta = theta_sl.t[0:32, :]
    braw = A.alloc(TSL, name="braw"); ap_braw = braw.t[0:64, :]
    craw = A.alloc(TSL, name="craw"); ap_craw = craw.t[0:64, :]
    bB0 = A.alloc(TSL, name="B0"); ap_B0 = bB0.t[0:64, :]
    bC0 = A.alloc(TSL, name="C0"); ap_C0 = bC0.t[0:64, :]
    sqbc = A.alloc(TSL, name="sqbc"); ap_sq = sqbc.t[0:64, :]
    bdt = [A.alloc(TSL, name=f"dt{j}") for j in range(2)]
    ap_dt = [bdt[0].t[0:16, :], bdt[1].t[0:16, :]]
    bdtl1 = [A.alloc(TSL, name=f"dtl1{j}") for j in range(2)]
    ap_dtl1 = [bdtl1[0].t[0:16, :], bdtl1[1].t[0:16, :]]
    blam = A.alloc(TSL, name="lam"); ap_lam = blam.t[0:16, :]
    buu = A.alloc(TSL, name="uu"); ap_uu = buu.t[0:16, :]
    bww = A.alloc(TSL, name="ww"); ap_ww = bww.t[0:16, :]
    badt = A.alloc(TSL, name="adt"); ap_adt = badt.t[0:16, :]
    bsp = [A.alloc(TSL, name=f"sp{j}") for j in range(3)]
    ap_sp = [b.t[0:16, :] for b in bsp]
    bphi = A.alloc(TSL, name="phi"); ap_phi = bphi.t[0:32, :]
    bq = A.alloc(TSL, name="q"); ap_q = bq.t[0:32, :]
    trg = A.alloc(192, name="trg")
    tg_phi = trg.t[:, 0:32]
    tg_q = trg.t[:, 32:64]
    tg_r = trg.t[:, 64:96]
    tg_sin = trg.t[:, 96:128]
    tg_cos = trg.t[:, 128:160]
    btm = A.alloc(64, name="btm")
    ctm = A.alloc(64, name="ctm")
    c1tm = A.alloc(64, name="c1tm")
    tp1 = A.alloc(32, name="tp1")
    tp2 = A.alloc(32, name="tp2")
    bch1 = A.alloc(128, name="ch1"); ap_ch1 = bch1.t[0:16, 0:128]
    bch2 = A.alloc(128, name="ch2"); ap_ch2 = bch2.t[0:16, 0:128]
    ones16 = A.alloc(128, name="ones16")
    ap_o16 = ones16.t[0:16, 0:128]
    P.memset(ap_o16, 1.0, writes=[ones16])
    ones32 = A.alloc(TSL, name="ones32")
    ap_o32 = ones32.t[0:32, :]
    P.memset(ap_o32, 1.0, writes=[ones32])
    c116 = A.alloc(2, name="c116")
    ap_c116 = c116.t[0:16, 0:1]
    P.memset(ap_c116, 1.0 / 16.0, writes=[c116])
    o64c = A.alloc(2, name="o64c")
    ap_o64 = o64c.t[0:64, 0:1]
    P.memset(ap_o64, 1.0, writes=[o64c])
    NCPS = TSL // 128

    def emit_ww(src_buf, src_ap, nxt, sl_of):
        P.cp(ap_ww[:, 0:TSL - 1], src_ap[:, 1:TSL], reads=[src_buf], writes=[bww])
        if nxt is None:
            P.memset(ap_ww[:, TSL - 1:TSL], 0.0, writes=[bww])
        else:
            nb_, na_ = nxt
            P.cp(ap_ww[:, TSL - 1:TSL], na_[:, 0:1], reads=[nb_], writes=[bww])
        for c in range(NCPS):
            ch = sl_of * NCPS + c
            P.transpose(pb[4].t[0:128, 0:16], ap_ww[:, c * 128:(c + 1) * 128],
                        EYE16.t, reads=[bww, EYE16], writes=[pb[4]])
            P.cp(colsT["w"].t[:, ch * 16:(ch + 1) * 16], pb[4].t[0:128, 0:16],
                 reads=[pb[4]], writes=[colsT["w"]])

    prev = {"buf": None, "ap": None, "idx": None}
    for sl in range(NSL):
        t0 = sl * TSL
        cur = sl % 2
        for i in range(8):
            P.dma("sync", x8[i].t, u_fm[i * 128:(i + 1) * 128, t0:t0 + TSL],
                  writes=[x8[i]])

        def out1(i, mB, rB):
            P.tt(u8[i].t, x8[i].t, mB.t[:, 0:TSL], OP.subtract,
                 reads=[x8[i], mB], writes=[u8[i]])
            P.tt(u8[i].t, u8[i].t, rB.t[:, 0:TSL], OP.mult,
                 reads=[u8[i], rB], writes=[u8[i]])
        ln_feature_major(x8, [b.t for b in x8], TSL, 1e-5, tmpc,
                         (pb[0], pb[1]), out1)
        for i in range(8):
            P.tt(tmpc.t, x8[i].t, u8[i].t, OP.add, reads=[x8[i], u8[i]],
                 writes=[tmpc])
            P.dma("scalar", xu_fm[i * 128:(i + 1) * 128, t0:t0 + TSL], tmpc.t,
                  reads=[tmpc])

        def out2(i, mB, rB):
            P.tt(x8[i].t, u8[i].t, mB.t[:, 0:TSL], OP.subtract,
                 reads=[u8[i], mB], writes=[x8[i]])
            P.tt(x8[i].t, x8[i].t, rB.t[:, 0:TSL], OP.mult,
                 reads=[x8[i], rB], writes=[x8[i]])
            P.stt(xin8[i].t, x8[i].t, C["v_l0_ln_s"].t[:, i:i + 1],
                  C["v_l0_ln_b"].t[:, i:i + 1].broadcast_to([128, TSL]),
                  OP.mult, OP.add, reads=[x8[i], C["v_l0_ln_s"], C["v_l0_ln_b"]],
                  writes=[xin8[i]])
            P.dma("scalar", xin_bf[i * 128:(i + 1) * 128, t0:t0 + TSL],
                  xin8[i].t, reads=[xin8[i]])
        ln_feature_major(u8, [b.t for b in u8], TSL, 1e-6, tmpc,
                         (pb[0], pb[1]), out2)
        for k in range(8):
            P.mm(pb[2].t[0:32, 0:TSL], wth[k].t[:, 0:32], u8[k].t,
                 start=(k == 0), stop=(k == 7),
                 reads=[wth[k], u8[k]], writes=[pb[2]])
        P.cp(ap_theta, pb[2].t[0:32, 0:TSL], reads=[pb[2]], writes=[theta_sl])
        for (mo, msz) in [(0, 64), (64, 64), (128, 16), (144, 16)]:
            bk = pb[3]
            for k in range(8):
                P.mm(bk.t[0:msz, 0:TSL], wsm[k].t[:, mo:mo + msz], xin8[k].t,
                     start=(k == 0), stop=(k == 7),
                     reads=[wsm[k], xin8[k]], writes=[bk])
            if mo == 0:
                P.cp(ap_braw, bk.t[0:64, 0:TSL], reads=[bk], writes=[braw])
            elif mo == 64:
                P.cp(ap_craw, bk.t[0:64, 0:TSL], reads=[bk], writes=[craw])
            elif mo == 128:
                P.stt(ap_dt[cur], bk.t[0:16, 0:TSL], 1.0,
                      C["v_l0_dtb"].t.broadcast_to([16, TSL]), OP.mult, OP.add,
                      reads=[bk, C["v_l0_dtb"]], writes=[bdt[cur]])
                P.act(ap_sp[0], ap_dt[cur], AF.Abs, reads=[bdt[cur]],
                      writes=[bsp[0]])
                P.act(ap_sp[1], ap_sp[0], AF.Exp, scale=-1.0,
                      reads=[bsp[0]], writes=[bsp[1]])
                P.act(ap_sp[1], ap_sp[1], AF.Ln, bias=1.0,
                      reads=[bsp[1]], writes=[bsp[1]])
                P.op("vector", lambda v, o=ap_sp[2], a=ap_dt[cur]:
                     v.tensor_scalar_max(o, a, 0.0),
                     reads=[bdt[cur]], writes=[bsp[2]])
                P.tt(ap_dt[cur], ap_sp[1], ap_sp[2], OP.add,
                     reads=[bsp[1], bsp[2]], writes=[bdt[cur]])
            else:
                P.act(ap_lam, bk.t[0:16, 0:TSL], AF.Sigmoid,
                      reads=[bk], writes=[blam])
        P.tt(ap_uu, ap_dt[cur], ap_lam, OP.mult, reads=[bdt[cur], blam],
             writes=[buu])
        P.ts(ap_adt, ap_dt[cur], A16.t, None, OP.mult, reads=[bdt[cur], A16],
             writes=[badt])
        P.ts(ap_sp[0], ap_lam, -1.0, 1.0, OP.mult, OP.add,
             reads=[blam], writes=[bsp[0]])
        P.tt(ap_dtl1[cur], ap_dt[cur], ap_sp[0], OP.mult,
             reads=[bdt[cur], bsp[0]], writes=[bdtl1[cur]])
        if prev["buf"] is not None:
            emit_ww(prev["buf"], prev["ap"], (bdtl1[cur], ap_dtl1[cur]),
                    prev["idx"])
        prev["buf"] = bdtl1[cur]
        prev["ap"] = ap_dtl1[cur]
        prev["idx"] = sl

        for c in range(NCPS):
            ch = sl * NCPS + c
            csl = slice(c * 128, (c + 1) * 128)
            P.scan(ap_cc[:, ch * 128:(ch + 1) * 128], ap_o16,
                   ap_adt[:, csl], 0.0, reads=[ones16, badt], writes=[w0])
            P.scan(ap_ch1[:, ::-1], ap_o16, ap_adt[:, csl][:, ::-1], 0.0,
                   reads=[ones16, badt], writes=[bch1])
            P.tt(ap_ch2, ap_ch1, ap_adt[:, csl], OP.subtract,
                 reads=[bch1, badt], writes=[bch2])
            P.act(ap_expic0[:, ch:ch + 1], ap_ch1[:, 0:1], AF.Exp,
                  reads=[bch1], writes=[expic0])
            P.act(ap_expcc[:, ch * 128:(ch + 1) * 128],
                  ap_cc[:, ch * 128:(ch + 1) * 128], AF.Exp,
                  reads=[w0], writes=[wex])
            P.act(ap_ch1, ap_ch2, AF.Exp, reads=[bch2], writes=[bch1])
            P.transpose(pb[4].t[0:128, 0:16], ap_ch1, EYE16.t,
                        reads=[bch1, EYE16], writes=[pb[4]])
            P.cp(colsT["exprc"].t[:, ch * 16:(ch + 1) * 16],
                 pb[4].t[0:128, 0:16], reads=[pb[4]], writes=[colsT["exprc"]])
            P.ts(ap_ch1, ap_cc[:, ch * 128:(ch + 1) * 128], -1.0, None,
                 OP.mult, reads=[w0], writes=[bch1])
            P.transpose(pb[4].t[0:128, 0:16], ap_ch1, EYE16.t,
                        reads=[bch1, EYE16], writes=[pb[4]])
            P.cp(colsT["negcc"].t[:, ch * 16:(ch + 1) * 16],
                 pb[4].t[0:128, 0:16], reads=[pb[4]], writes=[colsT["negcc"]])
            P.transpose(pb[4].t[0:128, 0:16], ap_uu[:, csl], EYE16.t,
                        reads=[buu, EYE16], writes=[pb[4]])
            P.cp(colsT["u"].t[:, ch * 16:(ch + 1) * 16], pb[4].t[0:128, 0:16],
                 reads=[pb[4]], writes=[colsT["u"]])

        def rms(raw_ap, raw_buf, scale_c, bias_c, out_ap, out_buf):
            P.act(ap_sq, raw_ap, AF.Square, reads=[raw_buf], writes=[sqbc])
            P.mm(pb[5].t[0:1, 0:TSL], ap_o64, ap_sq, start=True, stop=True,
                 reads=[sqbc, o64c], writes=[pb[5]])
            rr = tmpc.t[0:1, 0:TSL]
            P.ts(rr, pb[5].t[0:1, 0:TSL], 1.0 / 64.0, 1e-6, OP.mult, OP.add,
                 reads=[pb[5]], writes=[tmpc])
            P.recip(rr, rr, reads=[tmpc], writes=[tmpc])
            P.act(rr, rr, AF.Sqrt, reads=[tmpc], writes=[tmpc])
            P.mm(pb[5].t[0:64, 0:TSL], ONESR.t[:, 0:64], rr, start=True,
                 stop=True, reads=[tmpc, ONESR], writes=[pb[5]])
            P.tt(out_ap, raw_ap, pb[5].t[0:64, 0:TSL], OP.mult,
                 reads=[raw_buf, pb[5]], writes=[out_buf])
            P.stt(out_ap, out_ap, scale_c.t, bias_c.t.broadcast_to([64, TSL]),
                  OP.mult, OP.add, reads=[out_buf, scale_c, bias_c],
                  writes=[out_buf])
        rms(ap_braw, braw, C["v_l0_bcB"], C["v_l0_Bb"], ap_B0, bB0)
        rms(ap_craw, craw, C["v_l0_bcC"], C["v_l0_Cb"], ap_C0, bC0)
        P.mm(pb[5].t[0:1, 0:TSL], ap_c116, ap_dt[cur], start=True, stop=True,
             reads=[bdt[cur], c116], writes=[pb[5]])
        P.cp(tmpc.t[0:1, 0:TSL], pb[5].t[0:1, 0:TSL], reads=[pb[5]],
             writes=[tmpc])
        P.mm(pb[5].t[0:32, 0:TSL], ONESR.t[:, 0:32], tmpc.t[0:1, 0:TSL],
             start=True, stop=True, reads=[tmpc, ONESR], writes=[pb[5]])
        P.tt(ap_q, ap_theta, pb[5].t[0:32, 0:TSL], OP.mult,
             reads=[theta_sl, pb[5]], writes=[bq])
        P.scan(ap_phi, ap_o32, ap_q, ap_phic,
               reads=[ones32, bq, phic], writes=[bphi])
        P.cp(ap_phic, ap_phi[:, TSL - 1:TSL], reads=[bphi], writes=[phic])
        for c in range(NCPS):
            ch = sl * NCPS + c
            csl = slice(c * 128, (c + 1) * 128)
            chsl = slice(ch * 128, (ch + 1) * 128)
            P.transpose(pb[4].t[0:128, 0:32], ap_phi[:, csl], EYE32.t,
                        reads=[bphi, EYE32], writes=[pb[4]])
            P.cp(tg_phi, pb[4].t[0:128, 0:32], reads=[pb[4]], writes=[trg])
            P.ts(tg_q, tg_phi, 1.0 / (2 * PI), MAGIC, OP.mult, OP.add,
                 reads=[trg], writes=[trg])
            P.ts(tg_q, tg_q, -MAGIC, None, OP.add, reads=[trg], writes=[trg])
            P.stt(tg_r, tg_q, -2 * PI, tg_phi, OP.mult, OP.add,
                  reads=[trg], writes=[trg])
            P.act(tg_sin, tg_r, AF.Sin, reads=[trg], writes=[trg])
            P.ts(tg_q, tg_phi, 1.0 / (2 * PI), MAGIC + 0.25, OP.mult, OP.add,
                 reads=[trg], writes=[trg])
            P.ts(tg_q, tg_q, -MAGIC, None, OP.add, reads=[trg], writes=[trg])
            P.stt(tg_r, tg_q, -2 * PI, tg_phi, OP.mult, OP.add,
                  reads=[trg], writes=[trg])
            P.ts(tg_r, tg_r, PI / 2, None, OP.add, reads=[trg], writes=[trg])
            P.act(tg_cos, tg_r, AF.Sin, reads=[trg], writes=[trg])
            P.transpose(pb[4].t[0:128, 0:64], ap_B0[:, csl], EYE64.t,
                        reads=[bB0, EYE64], writes=[pb[4]])
            P.cp(btm.t, pb[4].t[0:128, 0:64], reads=[pb[4]], writes=[btm])
            P.transpose(pb[4].t[0:128, 0:64], ap_C0[:, csl], EYE64.t,
                        reads=[bC0, EYE64], writes=[pb[4]])
            P.cp(ctm.t, pb[4].t[0:128, 0:64], reads=[pb[4]], writes=[ctm])

            def rope_tm(src_b, dst_ap, dst_b):
                P.tt(tp1.t[:, 0:32], src_b.t[:, 0:32], tg_cos, OP.mult,
                     reads=[src_b, trg], writes=[tp1])
                P.tt(tp2.t[:, 0:32], src_b.t[:, 32:64], tg_sin, OP.mult,
                     reads=[src_b, trg], writes=[tp2])
                P.tt(dst_ap[:, 0:32], tp1.t[:, 0:32], tp2.t[:, 0:32],
                     OP.subtract, reads=[tp1, tp2], writes=[dst_b])
                P.tt(tp1.t[:, 0:32], src_b.t[:, 0:32], tg_sin, OP.mult,
                     reads=[src_b, trg], writes=[tp1])
                P.tt(tp2.t[:, 0:32], src_b.t[:, 32:64], tg_cos, OP.mult,
                     reads=[src_b, trg], writes=[tp2])
                P.tt(dst_ap[:, 32:64], tp1.t[:, 0:32], tp2.t[:, 0:32],
                     OP.add, reads=[tp1, tp2], writes=[dst_b])
            rope_tm(btm, B_tm.t[:, ch * 64:(ch + 1) * 64], B_tm)
            rope_tm(ctm, c1tm.t, c1tm)
            P.transpose(pb[4].t[0:64, 0:128], B_tm.t[:, ch * 64:(ch + 1) * 64],
                        EYE.t, reads=[B_tm, EYE], writes=[pb[4]])
            P.cp(ap_B1[:, chsl], pb[4].t[0:64, 0:128], reads=[pb[4]],
                 writes=[wB1])
            P.transpose(pb[4].t[0:64, 0:128], c1tm.t, EYE.t,
                        reads=[c1tm, EYE], writes=[pb[4]])
            P.cp(ap_C1[:, chsl], pb[4].t[0:64, 0:128], reads=[pb[4]],
                 writes=[w1])
    emit_ww(prev["buf"], prev["ap"], None, prev["idx"])
    A.pop()
    P.barrier()

    if PHASE_LIMIT <= 1:
        return nc, P, dict(locals())
    TSB = 256
    NSB = T // TSB
    A.push()
    w_x = [A.alloc(2048, BF16, name=f"wx{k}") for k in range(8)]
    for k in range(8):
        P.dma("gpsimd", w_x[k].t, g["w_l0_inproj"][k * 128:(k + 1) * 128, 0:2048],
              writes=[w_x[k]])
    xin_s = [A.alloc(TSB, BF16, name=f"xins{i}") for i in range(8)]
    x_tm = [A.alloc(2048, BF16, name=f"xtm{i}") for i in range(2)]
    ybuf = [A.alloc(128, BF16, name=f"yb{i}") for i in range(2)]
    d0t = A.alloc(128, name="d0t")
    lt = A.alloc(128, name="lt")
    smt = [A.alloc(128, name=f"smt{j}") for j in range(2)]
    smst = [A.alloc(128, name=f"smst{j}") for j in range(2)]
    vut = [A.alloc(128, name=f"vut{j}") for j in range(2)]
    vwt = [A.alloc(128, name=f"vwt{j}") for j in range(2)]
    vft = [A.alloc(128, name=f"vft{j}") for j in range(2)]
    bhat = [A.alloc(64, name=f"bhat{j}") for j in range(2)]
    cdec = [A.alloc(128, name=f"cdec{j}") for j in range(2)]
    eit = [A.alloc(64, name=f"eit{j}") for j in range(2)]
    Hb = [A.alloc(16 * 128, name=f"H{j}") for j in range(2)]
    ap_H = [Hb[0].t[0:64, :], Hb[1].t[0:64, :]]
    P.memset(ap_H[0], 0.0, writes=[Hb[0]])

    for sb in range(NSB):
        t0 = sb * TSB
        for k in range(8):
            P.dma("sync", xin_s[k].t, xin_bf[k * 128:(k + 1) * 128, t0:t0 + TSB],
                  writes=[xin_s[k]])
        for tc in range(2):
            tokap = slice(tc * 128, (tc + 1) * 128)
            for fs in range(4):
                bk = pb[fs % 2]
                P.mm_group([(bk.t[:, 0:512], xin_s[k].t[:, tokap],
                             w_x[k].t[:, fs * 512:(fs + 1) * 512],
                             [xin_s[k], w_x[k]]) for k in range(8)],
                           writes=[bk])
                P.act(x_tm[tc].t[:, fs * 512:(fs + 1) * 512], bk.t[:, 0:512],
                      AF.Silu, reads=[bk], writes=[x_tm[tc]])
        for c in range(TSB // 128):
            ch = sb * (TSB // 128) + c
            curH = ap_H[ch % 2]
            nxtH = ap_H[(ch + 1) % 2]
            curHb = Hb[ch % 2]
            nxtHb = Hb[(ch + 1) % 2]
            chsl = slice(ch * 128, (ch + 1) * 128)
            P.mm(pb[4].t[:, 0:128], ap_B1[:, chsl], ap_C1[:, chsl],
                 start=True, stop=True, reads=[wB1, w1], writes=[pb[4]])

            def S0(h):
                p = h % 2
                yb = pb[5 + p]
                sbk = pb[3] if p == 0 else pb[7]
                P.mm(yb.t[:, 0:128],
                     EYE16.t[:, h:h + 1].broadcast_to([16, 128]),
                     ap_cc[:, chsl], start=True, stop=True,
                     reads=[EYE16, w0], writes=[yb])
                P.mm(sbk.t[0:64, 0:128],
                     EYE16.t[:, h:h + 1].broadcast_to([16, 64]),
                     ap_expcc[:, chsl], start=True, stop=True,
                     reads=[EYE16, wex], writes=[sbk])
                P.mm(sbk.t[0:64, 256:257],
                     EYE16.t[:, h:h + 1].broadcast_to([16, 64]),
                     ap_expic0[:, ch:ch + 1], start=True, stop=True,
                     reads=[EYE16, expic0], writes=[sbk])

            def S1(h):
                p = h % 2
                yb = pb[5 + p]
                sbk = pb[3] if p == 0 else pb[7]
                hcol = slice(ch * 16 + h, ch * 16 + h + 1)
                P.tt(d0t.t[:, 0:128], yb.t[:, 0:128], UTRI.t, OP.add,
                     reads=[yb, UTRI], writes=[d0t])
                P.act(lt.t[:, 0:128], d0t.t[:, 0:128], AF.Exp,
                      bias=colsT["negcc"].t[:, hcol],
                      reads=[d0t, colsT["negcc"]], writes=[lt])
                P.tt(smt[p].t[:, 0:128], lt.t[:, 0:128], pb[4].t[:, 0:128],
                     OP.mult, reads=[lt, pb[4]], writes=[smt[p]])
                P.tt(smst[p].t[:, 0:128], smt[p].t[:, 0:128], STRICT.t,
                     OP.mult, reads=[smt[p], STRICT], writes=[smst[p]])
                xs_ap = x_tm[c].t[:, h * 128:(h + 1) * 128]
                P.ts(vut[p].t[:, 0:128], xs_ap, colsT["u"].t[:, hcol], None,
                     OP.mult, reads=[x_tm[c], colsT["u"]], writes=[vut[p]])
                P.ts(vwt[p].t[:, 0:128], xs_ap, colsT["w"].t[:, hcol], None,
                     OP.mult, reads=[x_tm[c], colsT["w"]], writes=[vwt[p]])
                P.tt(vft[p].t[:, 0:128], vut[p].t[:, 0:128],
                     vwt[p].t[:, 0:128], OP.add,
                     reads=[vut[p], vwt[p]], writes=[vft[p]])
                P.tt(cdec[p].t[0:64, 0:128], ap_C1[:, chsl],
                     sbk.t[0:64, 0:128], OP.mult,
                     reads=[w1, sbk], writes=[cdec[p]])
                P.ts(eit[p].t[0:64, 0:64], EYE64.t, sbk.t[0:64, 256:257],
                     None, OP.mult, reads=[EYE64, sbk], writes=[eit[p]])
                P.ts(bhat[p].t[:, 0:64], B_tm.t[:, ch * 64:(ch + 1) * 64],
                     colsT["exprc"].t[:, hcol], None, OP.mult,
                     reads=[B_tm, colsT["exprc"]], writes=[bhat[p]])

            def S2(h):
                p = h % 2
                yb = pb[5 + p]
                sbk = pb[3] if p == 0 else pb[7]
                xs_ap = x_tm[c].t[:, h * 128:(h + 1) * 128]
                P.mm(yb.t[:, 128:256], vut[p].t[:, 0:128],
                     smt[p].t[:, 0:128], start=True, stop=False,
                     reads=[vut[p], smt[p]], writes=[yb])
                P.mm(yb.t[:, 128:256], vwt[p].t[:, 0:128],
                     smst[p].t[:, 0:128], start=False, stop=False,
                     reads=[vwt[p], smst[p]], writes=[yb])
                P.mm(yb.t[:, 128:256], xs_ap, DI.t[:, h * 128:(h + 1) * 128],
                     start=False, stop=False, reads=[x_tm[c], DI], writes=[yb])
                P.mm(yb.t[:, 128:256], curH[:, h * 128:(h + 1) * 128],
                     cdec[p].t[0:64, 0:128], start=False, stop=True,
                     reads=[curHb, cdec[p]], writes=[yb])
                P.mm(sbk.t[0:64, 128:256], bhat[p].t[:, 0:64],
                     vft[p].t[:, 0:128], start=True, stop=False,
                     reads=[bhat[p], vft[p]], writes=[sbk])
                P.mm(sbk.t[0:64, 128:256], eit[p].t[0:64, 0:64],
                     curH[:, h * 128:(h + 1) * 128], start=False, stop=True,
                     reads=[eit[p], curHb], writes=[sbk])

            def S3(h):
                p = h % 2
                yb = pb[5 + p]
                sbk = pb[3] if p == 0 else pb[7]
                P.cp(nxtH[:, h * 128:(h + 1) * 128], sbk.t[0:64, 128:256],
                     reads=[sbk], writes=[nxtHb], eng="scalar")
                yo = ybuf[p]
                P.cp(yo.t, yb.t[:, 128:256], reads=[yb], writes=[yo])
                P.dma("scalar",
                      y0_bf[h * 128:(h + 1) * 128, (ch * 128):(ch * 128) + 128],
                      yo.t, reads=[yo])

            S0(0)
            for h in range(16):
                S1(h)
                if h > 0:
                    S3(h - 1)
                if h + 1 < 16:
                    S0(h + 1)
                S2(h)
            S3(15)
    A.pop()
    A.pop()
    P.barrier()

    if PHASE_LIMIT <= 2:
        return nc, P, dict(locals())
    TSC = 512
    A.push()
    w_z = [A.alloc(2048, BF16, name=f"wz{k}") for k in range(8)]
    w_o0 = [A.alloc(1024, BF16, name=f"wo0_{k}") for k in range(16)]
    for k in range(8):
        P.dma("gpsimd", w_z[k].t,
              g["w_l0_inproj"][k * 128:(k + 1) * 128, 2048:4096], writes=[w_z[k]])
    for k in range(16):
        P.dma("gpsimd", w_o0[k].t, g["w_l0_outproj"][k * 128:(k + 1) * 128, :],
              writes=[w_o0[k]])
    xinc = [A.alloc(TSC, BF16, name=f"xinc{i}") for i in range(8)]
    xuc = [A.alloc(TSC, name=f"xuc{i}") for i in range(8)]
    y0c = [A.alloc(TSC, BF16, name=f"y0c{i}") for i in range(16)]
    ygc = [A.alloc(TSC, BF16, name=f"ygc{i}") for i in range(16)]
    zsl = A.alloc(TSC, name="zsl")
    for sc in range(T // TSC):
        t0 = sc * TSC
        for k in range(8):
            P.dma("sync", xinc[k].t, xin_bf[k * 128:(k + 1) * 128, t0:t0 + TSC],
                  writes=[xinc[k]])
            P.dma("sync", xuc[k].t, xu_fm[k * 128:(k + 1) * 128, t0:t0 + TSC],
                  writes=[xuc[k]])
        for m in range(16):
            P.dma("scalar", y0c[m].t, y0_bf[m * 128:(m + 1) * 128, t0:t0 + TSC],
                  writes=[y0c[m]])
        for m in range(16):
            bk = pb[m % 2]
            P.mm_group([(bk.t[:, 0:TSC], w_z[k].t[:, m * 128:(m + 1) * 128],
                         xinc[k].t, [w_z[k], xinc[k]]) for k in range(8)],
                       writes=[bk])
            P.act(zsl.t, bk.t[:, 0:TSC], AF.Silu, reads=[bk], writes=[zsl])
            P.tt(ygc[m].t, y0c[m].t, zsl.t, OP.mult,
                 reads=[y0c[m], zsl], writes=[ygc[m]])
        for m in range(8):
            bk = pb[2 + (m % 2)]
            P.mm_group([(bk.t[:, 0:TSC], w_o0[k].t[:, m * 128:(m + 1) * 128],
                         ygc[k].t, [w_o0[k], ygc[k]]) for k in range(16)],
                       writes=[bk])
            P.tt(xuc[m].t, bk.t[:, 0:TSC], xuc[m].t, OP.add,
                 reads=[bk, xuc[m]], writes=[xuc[m]])
            P.dma("scalar", x1_fm[m * 128:(m + 1) * 128, t0:t0 + TSC],
                  xuc[m].t, reads=[xuc[m]])
    A.pop()
    P.barrier()

    if PHASE_LIMIT <= 3:
        return nc, P, dict(locals())
    TS1 = 256
    A.push()
    mk_ln_rows()
    ld_consts(["v_l0f_bd"])
    wgate = [A.alloc(8, name=f"wg{k}") for k in range(8)]
    wd = [[A.alloc(64, BF16, name=f"wd{e}_{k}") for k in range(8)]
          for e in range(8)]
    wu = [A.alloc(1024, BF16, name=f"wu{e}") for e in range(8)]
    bu_l = [A.alloc(1024, name=f"bul{j}") for j in range(1)]
    bu_l = bu_l[0]
    ap_bu = bu_l.t[0:8, 0:1024]
    for k in range(8):
        P.dma("sync", wgate[k].t, g["w_l0f_gate"][k * 128:(k + 1) * 128, :],
              writes=[wgate[k]])
    for e in range(8):
        for k in range(8):
            P.dma("gpsimd", wd[e][k].t, g["w_l0f_Wd"][e, k * 128:(k + 1) * 128, :],
                  writes=[wd[e][k]])
        P.dma("gpsimd", wu[e].t[0:64, :], g["w_l0f_Wu"][e], writes=[wu[e]])
    P.dma("sync", ap_bu, g["w_l0f_bu"], writes=[bu_l])
    x1c = [A.alloc(TS1, name=f"x1c{i}") for i in range(8)]
    xfc = [A.alloc(TS1, name=f"xfc{i}") for i in range(8)]
    xfb = [A.alloc(TS1, BF16, name=f"xfb{i}") for i in range(8)]
    tmp1 = A.alloc(TS1, name="tmp1")
    weT8 = A.alloc(TS1, name="weT8")
    ap_weT = weT8.t[0:8, :]
    tp = A.alloc(64, name="tp")
    ap_lg = tp.t[:, 0:8]
    ap_msk1 = tp.t[:, 8:16]
    ap_msk2 = tp.t[:, 16:24]
    ap_d = tp.t[:, 24:32]
    ap_m1 = tp.t[:, 32:33]
    ap_m2 = tp.t[:, 33:34]
    ap_g1 = tp.t[:, 34:35]
    ap_g2 = tp.t[:, 35:36]
    hbf = A.alloc(TS1, BF16, name="hbf")
    ap_hbf = hbf.t[0:64, :]
    hsc = A.alloc(TS1, BF16, name="hsc")
    ap_hsc = hsc.t[0:64, :]
    wcb = A.alloc(TS1, BF16, name="wcb")
    ap_wcb = wcb.t[0:64, :]

    def top2_we(lg_psum_ap, tokchunk_ap):
        P.cp(ap_lg, lg_psum_ap, reads=[pb[6]], writes=[tp])
        P.red_max(ap_m1, ap_lg, reads=[tp], writes=[tp])
        P.stt(ap_d, ap_lg, 1.0, tp.t[:, 32:33].broadcast_to([128, 8]),
              OP.mult, OP.subtract, reads=[tp], writes=[tp])
        P.act(ap_d, ap_d, AF.Sign, reads=[tp], writes=[tp])
        P.tt(ap_msk1, ap_d, ap_d, OP.mult, reads=[tp], writes=[tp])
        P.ts(ap_msk1, ap_msk1, -1.0, 1.0, OP.mult, OP.add, reads=[tp], writes=[tp])
        P.stt(ap_d, ap_msk1, -1e30, ap_lg, OP.mult, OP.add, reads=[tp], writes=[tp])
        P.red_max(ap_m2, ap_d, reads=[tp], writes=[tp])
        P.stt(ap_d, ap_d, 1.0, tp.t[:, 33:34].broadcast_to([128, 8]),
              OP.mult, OP.subtract, reads=[tp], writes=[tp])
        P.act(ap_d, ap_d, AF.Sign, reads=[tp], writes=[tp])
        P.tt(ap_msk2, ap_d, ap_d, OP.mult, reads=[tp], writes=[tp])
        P.ts(ap_msk2, ap_msk2, -1.0, 1.0, OP.mult, OP.add, reads=[tp], writes=[tp])
        P.tt(ap_d[:, 0:1], ap_m1, ap_m2, OP.subtract, reads=[tp], writes=[tp])
        P.act(ap_g1, ap_d[:, 0:1], AF.Sigmoid, reads=[tp], writes=[tp])
        P.ts(ap_g2, ap_g1, -1.0, 1.0, OP.mult, OP.add, reads=[tp], writes=[tp])
        P.ts(ap_msk1, ap_msk1, tp.t[:, 34:35], None, OP.mult, reads=[tp],
             writes=[tp])
        P.stt(ap_msk2, ap_msk2, tp.t[:, 35:36], ap_msk1, OP.mult, OP.add,
              reads=[tp], writes=[tp])
        P.transpose(pb[7].t[0:8, 0:128], ap_msk2, EYE.t, reads=[tp, EYE],
                    writes=[pb[7]])
        P.cp(ap_weT[:, tokchunk_ap], pb[7].t[0:8, 0:128], reads=[pb[7]],
             writes=[weT8])

    for s1 in range(T // TS1):
        t0 = s1 * TS1
        for i in range(8):
            P.dma("sync", x1c[i].t, x1_fm[i * 128:(i + 1) * 128, t0:t0 + TS1],
                  writes=[x1c[i]])

        def outf(i, mB, rB):
            P.tt(xfc[i].t, x1c[i].t, mB.t[:, 0:TS1], OP.subtract,
                 reads=[x1c[i], mB], writes=[xfc[i]])
            P.tt(xfc[i].t, xfc[i].t, rB.t[:, 0:TS1], OP.mult,
                 reads=[xfc[i], rB], writes=[xfc[i]])
            P.cp(xfb[i].t, xfc[i].t, reads=[xfc[i]], writes=[xfb[i]])
        ln_feature_major(x1c, [b.t for b in x1c], TS1, 1e-5, tmp1,
                         (pb[0], pb[1]), outf)
        for tc in range(2):
            tokap = slice(tc * 128, (tc + 1) * 128)
            for k in range(8):
                P.mm(pb[6].t[:, 0:8], xfc[k].t[:, tokap], wgate[k].t,
                     start=(k == 0), stop=(k == 7),
                     reads=[xfc[k], wgate[k]], writes=[pb[6]])
            top2_we(pb[6].t[:, 0:8], slice(tc * 128, (tc + 1) * 128))
        for e in range(8):
            P.mm(pb[1].t[0:64, 0:TS1],
                 EYE8.t[:, e:e + 1].broadcast_to([8, 64]), ap_weT,
                 start=True, stop=True, reads=[EYE8, weT8], writes=[pb[1]])
            P.cp(ap_wcb, pb[1].t[0:64, 0:TS1], reads=[pb[1]], writes=[wcb])
            bk = pb[0]
            P.mm_group([(bk.t[0:64, 0:TS1], wd[e][k].t, xfb[k].t,
                         [wd[e][k], xfb[k]]) for k in range(8)], writes=[bk])
            P.act(ap_hbf, bk.t[0:64, 0:TS1], AF.Relu,
                  bias=C["v_l0f_bd"].t[:, e:e + 1],
                  reads=[bk, C["v_l0f_bd"]], writes=[hbf])
            P.tt(ap_hsc, ap_hbf, ap_wcb, OP.mult, reads=[hbf, wcb], writes=[hsc])
            for mp in range(8):
                ob = pb[4 + mp // 2]
                oap = ob.t[:, (mp % 2) * 256:(mp % 2) * 256 + TS1]
                P.mm(oap, wu[e].t[0:64, mp * 128:(mp + 1) * 128], ap_hsc,
                     start=(e == 0), stop=False, reads=[wu[e], hsc], writes=[ob])
        for mp in range(8):
            ob = pb[4 + mp // 2]
            oap = ob.t[:, (mp % 2) * 256:(mp % 2) * 256 + TS1]
            P.mm(oap, ap_bu[:, mp * 128:(mp + 1) * 128], ap_weT,
                 start=False, stop=True, reads=[bu_l, weT8], writes=[ob])
            P.stt(x1c[mp].t, oap, 0.5, x1c[mp].t, OP.mult, OP.add,
                  reads=[ob, x1c[mp]], writes=[x1c[mp]])
            P.dma("scalar", x2_fm[mp * 128:(mp + 1) * 128, t0:t0 + TS1],
                  x1c[mp].t, reads=[x1c[mp]])
    A.pop()
    P.barrier()

    if PHASE_LIMIT <= 4:
        return nc, P, dict(locals())
    TS2 = 512
    A.push()
    mk_ln_rows()
    ld_consts(["v_l1_convw", "v_l1_convb"])
    w_in1 = [A.alloc(4096, BF16, name=f"wi1_{k}") for k in range(8)]
    for k in range(8):
        P.dma("gpsimd", w_in1[k].t, g["w_l1_inproj"][k * 128:(k + 1) * 128, :],
              writes=[w_in1[k]])
    x2c = [A.alloc(TS2, name=f"x2c{i}") for i in range(8)]
    hsb = [A.alloc(TS2, BF16, name=f"hsb{i}") for i in range(8)]
    tmp2 = A.alloc(TS2, name="tmp2")
    xcat = A.alloc(TS2 + 4, name="xcat")
    carry3 = A.alloc(48, name="carry3")
    P.memset(carry3.t, 0.0, writes=[carry3])
    acc2 = [A.alloc(TS2, name=f"acc2_{j}") for j in range(2)]
    resb = A.alloc(TS2, name="resb")
    for s2 in range(T // TS2):
        t0 = s2 * TS2
        for i in range(8):
            P.dma("sync", x2c[i].t, x2_fm[i * 128:(i + 1) * 128, t0:t0 + TS2],
                  writes=[x2c[i]])

        def outh(i, mB, rB):
            P.tt(tmp2.t, x2c[i].t, mB.t[:, 0:TS2], OP.subtract,
                 reads=[x2c[i], mB], writes=[tmp2])
            P.tt(hsb[i].t, tmp2.t, rB.t[:, 0:TS2], OP.mult,
                 reads=[tmp2, rB], writes=[hsb[i]])
        ln_feature_major(x2c, [b.t for b in x2c], TS2, 1e-5, tmp2,
                         (pb[0], pb[1]), outh)
        for m in range(32):
            bk = pb[2 + (m % 2)]
            P.mm_group([(bk.t[:, 0:TS2], w_in1[k].t[:, m * 128:(m + 1) * 128],
                         hsb[k].t, [w_in1[k], hsb[k]]) for k in range(8)],
                       writes=[bk])
            if m < 16:
                P.cp(xcat.t[:, 0:3], carry3.t[:, m * 3:m * 3 + 3],
                     reads=[carry3], writes=[xcat])
                P.cp(xcat.t[:, 3:3 + TS2], bk.t[:, 0:TS2], reads=[bk],
                     writes=[xcat])
                P.cp(carry3.t[:, m * 3:m * 3 + 3],
                     xcat.t[:, TS2:TS2 + 3], reads=[xcat], writes=[carry3])
                P.ts(acc2[0].t, xcat.t[:, 0:TS2], C["v_l1_convw"].t[:, m * 4:m * 4 + 1],
                     None, OP.mult, reads=[xcat, C["v_l1_convw"]], writes=[acc2[0]])
                for kk in range(1, 4):
                    P.stt(acc2[kk % 2].t, xcat.t[:, kk:kk + TS2],
                          C["v_l1_convw"].t[:, m * 4 + kk:m * 4 + kk + 1],
                          acc2[(kk + 1) % 2].t, OP.mult, OP.add,
                          reads=[xcat, C["v_l1_convw"], acc2[(kk + 1) % 2]],
                          writes=[acc2[kk % 2]])
                P.act(acc2[0].t, acc2[1].t, AF.Silu,
                      bias=C["v_l1_convb"].t[:, m:m + 1],
                      reads=[acc2[1], C["v_l1_convb"]], writes=[acc2[0]])
                P.dma("scalar", xs_fm[m * 128:(m + 1) * 128, t0:t0 + TS2],
                      acc2[0].t, reads=[acc2[0]])
            else:
                mm_ = m - 16
                P.cp(resb.t, bk.t[:, 0:TS2], reads=[bk], writes=[resb])
                P.dma("scalar", res_fm[mm_ * 128:(mm_ + 1) * 128, t0:t0 + TS2],
                      resb.t, reads=[resb])
    A.pop()
    P.barrier()

    if PHASE_LIMIT <= 5:
        return nc, P, dict(locals())
    A.push()
    bBl1 = A.alloc(T, BF16, name="Bl1")
    ap_Bl1 = bBl1.t[0:64, :]
    bCl1 = A.alloc(T, BF16, name="Cl1")
    ap_Cl1 = bCl1.t[0:64, :]
    eye64bf = A.alloc(64, BF16, name="eye64bf")
    ap_e64b = eye64bf.t[0:64, 0:64]
    P.cp(ap_e64b, EYE64.t, reads=[EYE64], writes=[eye64bf])
    eye128bf = A.alloc(128, BF16, name="eye128bf")
    ap_e128b = eye128bf.t[:, 0:128]
    P.cp(ap_e128b, EYE.t, reads=[EYE], writes=[eye128bf])
    a_sb = A.alloc(1024, name="a_sb")
    st_sb = A.alloc(1024, name="st_sb")
    P.memset(st_sb.t, 0.0, writes=[st_sb])
    for dc in range(16):
        P.dma("sync", a_sb.t[:, dc * 64:(dc + 1) * 64],
              g["w_l1_alog"][dc * 128:(dc + 1) * 128, :], writes=[a_sb])
    P.act(a_sb.t, a_sb.t, AF.Exp, reads=[a_sb], writes=[a_sb])
    P.ts(a_sb.t, a_sb.t, -1.0, None, OP.mult, reads=[a_sb], writes=[a_sb])

    A.push()
    ld_consts(["v_l1_dtpb"])
    w_dt = [A.alloc(2048, BF16, name=f"wdt{k}") for k in range(16)]
    w_xp = [A.alloc(128, BF16, name=f"wxp{k}") for k in range(16)]
    for k in range(16):
        P.dma("gpsimd", w_dt[k].t, g["w_l1_dtproj"][k * 128:(k + 1) * 128, :],
              writes=[w_dt[k]])
        P.dma("gpsimd", w_xp[k].t, g["w_l1_xproj"][k * 128:(k + 1) * 128, :],
              writes=[w_xp[k]])
    xsb = [A.alloc(TS2, BF16, name=f"xsb{i}") for i in range(16)]
    xs32 = A.alloc(TS2, name="xs32")
    dl32 = A.alloc(TS2, name="dl32")
    sp1 = A.alloc(TS2, name="sp1")
    sp2 = A.alloc(TS2, name="sp2")
    for s2 in range(T // TS2):
        t0 = s2 * TS2
        for k in range(16):
            P.dma("gpsimd", xsb[k].t, xs_fm[k * 128:(k + 1) * 128, t0:t0 + TS2],
                  writes=[xsb[k]])
        bcw_of = {0: bBl1, 64: bCl1}
        for (mo, rows) in [(0, ap_Bl1), (64, ap_Cl1)]:
            bk = pb[5]
            P.mm_group([(bk.t[0:64, 0:TS2], w_xp[k].t[:, mo:mo + 64], xsb[k].t,
                         [w_xp[k], xsb[k]]) for k in range(16)], writes=[bk])
            P.cp(rows[:, t0:t0 + TS2], bk.t[0:64, 0:TS2], reads=[bk],
                 writes=[bcw_of[mo]])
        for m in range(16):
            bk = pb[2 + (m % 2)]
            P.mm_group([(bk.t[:, 0:TS2], w_dt[k].t[:, m * 128:(m + 1) * 128],
                         xsb[k].t, [w_dt[k], xsb[k]]) for k in range(16)],
                       writes=[bk])
            P.stt(dl32.t, bk.t[:, 0:TS2], 1.0,
                  C["v_l1_dtpb"].t[:, m:m + 1].broadcast_to([128, TS2]),
                  OP.mult, OP.add, reads=[bk, C["v_l1_dtpb"]], writes=[dl32])
            P.act(sp1.t, dl32.t, AF.Abs, reads=[dl32], writes=[sp1])
            P.act(sp1.t, sp1.t, AF.Exp, scale=-1.0, reads=[sp1], writes=[sp1])
            P.act(sp1.t, sp1.t, AF.Ln, bias=1.0, reads=[sp1], writes=[sp1])
            P.op("vector", lambda v, o=sp2.t, a=dl32.t:
                 v.tensor_scalar_max(o, a, 0.0), reads=[dl32], writes=[sp2])
            P.tt(dl32.t, sp1.t, sp2.t, OP.add, reads=[sp1, sp2], writes=[dl32])
            P.dma("scalar", delta_fm[m * 128:(m + 1) * 128, t0:t0 + TS2],
                  dl32.t, reads=[dl32])
            P.dma("sync", xs32.t, xs_fm[m * 128:(m + 1) * 128, t0:t0 + TS2],
                  writes=[xs32])
            sp2bf = sp2.t.bitcast(BF16)[:, 0:TS2]
            P.tt(sp2bf, dl32.t, xs32.t, OP.mult, reads=[dl32, xs32],
                 writes=[sp2])
            P.dma("scalar", dxd_fm[m * 128:(m + 1) * 128, t0:t0 + TS2],
                  sp2bf, reads=[sp2])
    A.pop()
    P.barrier()

    if PHASE_LIMIT <= 6:
        return nc, P, dict(locals())
    TS3 = 512
    A.push()
    bbcb = [A.alloc(TS3, BF16, name=f"bbcb{j}") for j in range(64)]
    y_sb = [A.alloc(TS3, name=f"ysb{d}") for d in range(16)]
    dlc = [A.alloc(TS3, name=f"dlc{j}") for j in range(2)]
    dxc = [A.alloc(TS3, BF16, name=f"dxc{j}") for j in range(2)]
    da2 = [A.alloc(TS3, BF16, name=f"da{j}") for j in range(2)]
    dbx2 = [A.alloc(TS3, BF16, name=f"dbx{j}") for j in range(2)]
    dbxP = [A.alloc(TS3, BF16, name=f"dbxP{j}") for j in range(2)]
    hti2 = [A.alloc(TS3, BF16, name=f"hti{j}") for j in range(2)]
    ytm = [A.alloc(TS3, BF16, name=f"ytm{j}") for j in range(2)]
    ytmP = [A.alloc(TS3, BF16, name=f"ytmP{j}") for j in range(2)]
    gsc = A.alloc(TS3, name="gsc")

    for s3 in range(T // TS3):
        t0 = s3 * TS3
        for nb in range(2):
            for j in range(32):
                n = nb * 32 + j
                bk = pb[j % 2]
                P.mm(bk.t[:, 0:TS3],
                     ap_e64b[:, n:n + 1].broadcast_to([64, 128]),
                     ap_Bl1[:, t0:t0 + TS3], start=True, stop=True,
                     reads=[eye64bf, bBl1], writes=[bk])
                P.cp(bbcb[j].t, bk.t[:, 0:TS3], reads=[bk], writes=[bbcb[j]],
                     eng="scalar")
                bk = pb[j % 2]
                P.mm(bk.t[:, 0:TS3],
                     ap_e64b[:, n:n + 1].broadcast_to([64, 128]),
                     ap_Cl1[:, t0:t0 + TS3], start=True, stop=True,
                     reads=[eye64bf, bCl1], writes=[bk])
                P.cp(bbcb[32 + j].t, bk.t[:, 0:TS3], reads=[bk],
                     writes=[bbcb[32 + j]], eng="scalar")
            for dc in range(16):
                dbuf = dc % 2
                P.dma("sync", dlc[dbuf].t,
                      delta_fm[dc * 128:(dc + 1) * 128, t0:t0 + TS3],
                      writes=[dlc[dbuf]])
                P.dma("gpsimd", dxc[dbuf].t,
                      dxd_fm[dc * 128:(dc + 1) * 128, t0:t0 + TS3],
                      writes=[dxc[dbuf]])
                yb = pb[2 + (dc % 2)]
                def on_pool(j):
                    return False

                def emit_dbx(j):
                    if on_pool(j):
                        dbb = dbxP[j % 2]
                        P.tt(dbb.t, dxc[dbuf].t, bbcb[j].t, OP.mult,
                             reads=[dxc[dbuf], bbcb[j]], writes=[dbb],
                             eng="gpsimd")
                    else:
                        dbb = dbx2[j % 2]
                        P.tt(dbb.t, dxc[dbuf].t, bbcb[j].t, OP.mult,
                             reads=[dxc[dbuf], bbcb[j]], writes=[dbb])
                    return dbb

                def emit_ytmp(j, ph):
                    if on_pool(j):
                        yt = ytmP[j % 2]
                        P.tt(yt.t, ph.t, bbcb[32 + j].t, OP.mult,
                             reads=[ph, bbcb[32 + j]], writes=[yt],
                             eng="gpsimd")
                    else:
                        yt = ytm[j % 2]
                        P.tt(yt.t, ph.t, bbcb[32 + j].t, OP.mult,
                             reads=[ph, bbcb[32 + j]], writes=[yt])
                    return yt

                dpend = {}
                for j in range(32):
                    n = nb * 32 + j
                    dab = da2[j % 2]
                    P.act(dab.t, dlc[dbuf].t, AF.Exp,
                          scale=a_sb.t[:, dc * 64 + n:dc * 64 + n + 1],
                          reads=[dlc[dbuf], a_sb], writes=[dab])
                    dpend[j] = emit_dbx(j)
                    if j > 0:
                        pn = nb * 32 + j - 1
                        pstcol = st_sb.t[:, dc * 64 + pn:dc * 64 + pn + 1]
                        ph = hti2[(j - 1) % 2]
                        P.cp(pstcol, ph.t[:, TS3 - 1:TS3], reads=[ph],
                             writes=[st_sb], eng="scalar")
                        yt = emit_ytmp(j - 1, ph)
                        P.mm(yb.t[:, 0:TS3], ap_e128b, yt.t, start=(j == 1),
                             stop=False, reads=[eye128bf, yt], writes=[yb])
                    stcol = st_sb.t[:, dc * 64 + n:dc * 64 + n + 1]
                    P.scan(hti2[j % 2].t, dab.t, dpend[j].t, stcol,
                           reads=[dab, dpend[j], st_sb], writes=[hti2[j % 2]])
                pn = nb * 32 + 31
                pstcol = st_sb.t[:, dc * 64 + pn:dc * 64 + pn + 1]
                ph = hti2[31 % 2]
                P.cp(pstcol, ph.t[:, TS3 - 1:TS3], reads=[ph], writes=[st_sb],
                     eng="scalar")
                yt = emit_ytmp(31, ph)
                P.mm(yb.t[:, 0:TS3], ap_e128b, yt.t, start=False, stop=True,
                     reads=[eye128bf, yt], writes=[yb])
                if nb == 0:
                    P.cp(y_sb[dc].t, yb.t[:, 0:TS3], reads=[yb],
                         writes=[y_sb[dc]], eng="scalar")
                else:
                    P.tt(y_sb[dc].t, y_sb[dc].t, yb.t[:, 0:TS3], OP.add,
                         reads=[yb, y_sb[dc]], writes=[y_sb[dc]])
        silr = dlc[0]
        ygt_ap = gsc.t.bitcast(BF16)[:, 0:TS3]
        for dc in range(16):
            P.dma("sync", silr.t, res_fm[dc * 128:(dc + 1) * 128, t0:t0 + TS3],
                  writes=[silr])
            P.act(silr.t, silr.t, AF.Silu, reads=[silr], writes=[silr])
            P.tt(ygt_ap, y_sb[dc].t, silr.t, OP.mult,
                 reads=[y_sb[dc], silr], writes=[gsc])
            P.dma("scalar", yg_bf[dc * 128:(dc + 1) * 128, t0:t0 + TS3],
                  ygt_ap, reads=[gsc])
    A.pop()
    A.pop()
    P.barrier()

    if PHASE_LIMIT <= 7:
        return nc, P, dict(locals())
    TS4 = 512
    A.push()
    w_o1 = [A.alloc(1024, BF16, name=f"wo1_{k}") for k in range(16)]
    for k in range(16):
        P.dma("gpsimd", w_o1[k].t, g["w_l1_outproj"][k * 128:(k + 1) * 128, :],
              writes=[w_o1[k]])
    ygc4 = [A.alloc(TS4, BF16, name=f"yg4_{k}") for k in range(16)]
    x2c4 = [A.alloc(TS4, name=f"x2c4_{m}") for m in range(8)]
    for s4 in range(T // TS4):
        t0 = s4 * TS4
        for k in range(16):
            P.dma("sync", ygc4[k].t, yg_bf[k * 128:(k + 1) * 128, t0:t0 + TS4],
                  writes=[ygc4[k]])
        for m in range(8):
            P.dma("sync", x2c4[m].t, x2_fm[m * 128:(m + 1) * 128, t0:t0 + TS4],
                  writes=[x2c4[m]])
            bk = pb[m % 2]
            P.mm_group([(bk.t[:, 0:TS4], w_o1[k].t[:, m * 128:(m + 1) * 128],
                         ygc4[k].t, [w_o1[k], ygc4[k]]) for k in range(16)],
                       writes=[bk])
            P.tt(x2c4[m].t, bk.t[:, 0:TS4], x2c4[m].t, OP.add,
                 reads=[bk, x2c4[m]], writes=[x2c4[m]])
            P.dma("scalar", x3_fm[m * 128:(m + 1) * 128, t0:t0 + TS4],
                  x2c4[m].t, reads=[x2c4[m]])
    A.pop()
    P.barrier()

    if PHASE_LIMIT <= 8:
        return nc, P, dict(locals())
    A.push()
    ld_consts(["v_l1f_b1"])
    weL = A.alloc(T, BF16, name="weL")
    ap_weL = weL.t[0:8, :]
    eye8bf = A.alloc(8, BF16, name="eye8bf")
    ap_e8b = eye8bf.t[0:8, 0:8]
    P.cp(ap_e8b, EYE8.t, reads=[EYE8], writes=[eye8bf])
    A.push()
    mk_ln_rows()
    ld_consts(["v_l1f_rb1", "v_l1f_rb2"])
    rb2B = A.alloc(8, name="rb2B")
    rb2B = Buf(rb2B.t[:, 0:8], "rb2B")
    P.mm(pb[0].t[:, 0:8], ONESR.t, C["v_l1f_rb2"].t, start=True,
         stop=True, reads=[ONESR, C["v_l1f_rb2"]], writes=[pb[0]])
    P.cp(rb2B.t, pb[0].t[:, 0:8], reads=[pb[0]], writes=[rb2B])
    w_r1 = [A.alloc(4096, BF16, name=f"wr1_{k}") for k in range(8)]
    w_r2 = [A.alloc(8, BF16, name=f"wr2_{k}") for k in range(32)]
    for k in range(8):
        P.dma("gpsimd", w_r1[k].t, g["w_l1f_rw1"][k * 128:(k + 1) * 128, :],
              writes=[w_r1[k]])
    for k in range(32):
        P.dma("gpsimd", w_r2[k].t, g["w_l1f_rw2"][k * 128:(k + 1) * 128, :],
              writes=[w_r2[k]])
    TS5 = 512
    x3c = [A.alloc(TS5, name=f"x3c{i}") for i in range(8)]
    xfc5 = [A.alloc(TS5, BF16, name=f"xf5_{i}") for i in range(8)]
    hmid = [A.alloc(TS5, BF16, name=f"hm{m}") for m in range(32)]
    tmp5 = A.alloc(TS5, name="tmp5")
    tp5 = A.alloc(64, name="tp5")
    ap_lg = tp5.t[:, 0:8]
    ap_msk1 = tp5.t[:, 8:16]
    ap_msk2 = tp5.t[:, 16:24]
    ap_d = tp5.t[:, 24:32]
    ap_m1 = tp5.t[:, 32:33]
    ap_m2 = tp5.t[:, 33:34]
    ap_g1 = tp5.t[:, 34:35]
    ap_g2 = tp5.t[:, 35:36]

    def top2_we5(lg_psum_buf, lg_psum_ap, wecols):
        P.stt(ap_lg, lg_psum_ap, 1.0, rb2B.t, OP.mult, OP.add,
              reads=[lg_psum_buf, rb2B], writes=[tp5])
        P.red_max(ap_m1, ap_lg, reads=[tp5], writes=[tp5])
        P.stt(ap_d, ap_lg, 1.0, tp5.t[:, 32:33].broadcast_to([128, 8]),
              OP.mult, OP.subtract, reads=[tp5], writes=[tp5])
        P.act(ap_d, ap_d, AF.Sign, reads=[tp5], writes=[tp5])
        P.tt(ap_msk1, ap_d, ap_d, OP.mult, reads=[tp5], writes=[tp5])
        P.ts(ap_msk1, ap_msk1, -1.0, 1.0, OP.mult, OP.add, reads=[tp5],
             writes=[tp5])
        P.stt(ap_d, ap_msk1, -1e30, ap_lg, OP.mult, OP.add, reads=[tp5],
              writes=[tp5])
        P.red_max(ap_m2, ap_d, reads=[tp5], writes=[tp5])
        P.stt(ap_d, ap_d, 1.0, tp5.t[:, 33:34].broadcast_to([128, 8]),
              OP.mult, OP.subtract, reads=[tp5], writes=[tp5])
        P.act(ap_d, ap_d, AF.Sign, reads=[tp5], writes=[tp5])
        P.tt(ap_msk2, ap_d, ap_d, OP.mult, reads=[tp5], writes=[tp5])
        P.ts(ap_msk2, ap_msk2, -1.0, 1.0, OP.mult, OP.add, reads=[tp5],
             writes=[tp5])
        P.tt(ap_d[:, 0:1], ap_m1, ap_m2, OP.subtract, reads=[tp5], writes=[tp5])
        P.act(ap_g1, ap_d[:, 0:1], AF.Sigmoid, reads=[tp5], writes=[tp5])
        P.ts(ap_g2, ap_g1, -1.0, 1.0, OP.mult, OP.add, reads=[tp5], writes=[tp5])
        P.ts(ap_msk1, ap_msk1, tp5.t[:, 34:35], None, OP.mult, reads=[tp5],
             writes=[tp5])
        P.stt(ap_msk2, ap_msk2, tp5.t[:, 35:36], ap_msk1, OP.mult, OP.add,
              reads=[tp5], writes=[tp5])
        P.transpose(pb[7].t[0:8, 0:128], ap_msk2, EYE.t, reads=[tp5, EYE],
                    writes=[pb[7]])
        P.cp(ap_weL[:, wecols], pb[7].t[0:8, 0:128], reads=[pb[7]],
             writes=[weL])

    for s5 in range(T // TS5):
        t0 = s5 * TS5
        for i in range(8):
            P.dma("sync", x3c[i].t, x3_fm[i * 128:(i + 1) * 128, t0:t0 + TS5],
                  writes=[x3c[i]])

        def outr(i, mB, rB):
            P.tt(tmp5.t, x3c[i].t, mB.t[:, 0:TS5], OP.subtract,
                 reads=[x3c[i], mB], writes=[tmp5])
            P.tt(tmp5.t, tmp5.t, rB.t[:, 0:TS5], OP.mult,
                 reads=[tmp5, rB], writes=[tmp5])
            P.cp(xfc5[i].t, tmp5.t, reads=[tmp5], writes=[xfc5[i]])
            P.dma("scalar", xf_bfs[i * 128:(i + 1) * 128, t0:t0 + TS5],
                  xfc5[i].t, reads=[xfc5[i]])
        ln_feature_major(x3c, [b.t for b in x3c], TS5, 1e-5, tmp5,
                         (pb[0], pb[1]), outr)
        for m in range(32):
            bk = pb[2 + (m % 2)]
            P.mm_group([(bk.t[:, 0:TS5], w_r1[k].t[:, m * 128:(m + 1) * 128],
                         xfc5[k].t, [w_r1[k], xfc5[k]]) for k in range(8)],
                       writes=[bk])
            P.act(hmid[m].t, bk.t[:, 0:TS5], AF.Silu,
                  bias=C["v_l1f_rb1"].t[:, m:m + 1],
                  reads=[bk, C["v_l1f_rb1"]], writes=[hmid[m]])
        for tc in range(4):
            tokap = slice(tc * 128, (tc + 1) * 128)
            P.mm_group([(pb[6].t[:, 0:8], hmid[k].t[:, tokap], w_r2[k].t,
                         [hmid[k], w_r2[k]]) for k in range(32)],
                       writes=[pb[6]])
            top2_we5(pb[6], pb[6].t[:, 0:8],
                     slice(t0 + tc * 128, t0 + (tc + 1) * 128))
    P.dma("scalar", we_sc, ap_weL, reads=[weL])
    A.pop()
    P.barrier()

    if PHASE_LIMIT <= 9:
        return nc, P, dict(locals())
    A.push()
    w1e = [A.alloc(4096, BF16, name=f"w1e{k}") for k in range(8)]
    w2e = [A.alloc(1024, BF16, name=f"w2e{k}") for k in range(32)]
    TSE = 512
    xfe = [A.alloc(TSE, BF16, name=f"xfe{k}") for k in range(8)]
    hstore = [A.alloc(TSE, BF16, name=f"hst{m}") for m in range(32)]
    wcb5 = A.alloc(TSE, BF16, name="wcb5")
    oev = [A.alloc(TSE, BF16, name=f"oev{j}") for j in range(2)]
    for e in range(8):
        for k in range(8):
            P.dma("gpsimd", w1e[k].t, g["w_l1f_w1"][e, k * 128:(k + 1) * 128, :],
                  writes=[w1e[k]])
        for k in range(32):
            P.dma("gpsimd", w2e[k].t, g["w_l1f_w2"][e, k * 128:(k + 1) * 128, :],
                  writes=[w2e[k]])
        for se in range(T // TSE):
            t0 = se * TSE
            for k in range(8):
                P.dma("sync", xfe[k].t,
                      xf_bfs[k * 128:(k + 1) * 128, t0:t0 + TSE],
                      writes=[xfe[k]])
            P.mm(pb[0].t[:, 0:TSE],
                 ap_e8b[:, e:e + 1].broadcast_to([8, 128]),
                 ap_weL[:, t0:t0 + TSE], start=True, stop=True,
                 reads=[eye8bf, weL], writes=[pb[0]])
            P.cp(wcb5.t, pb[0].t[:, 0:TSE], reads=[pb[0]], writes=[wcb5])

            def h_mm(m):
                bk = pb[2 + (m % 2)]
                for k in range(8):
                    last = (k == 7)
                    P.mm(bk.t[:, 0:TSE], w1e[k].t[:, m * 128:(m + 1) * 128],
                         xfe[k].t, start=(k == 0), stop=last,
                         reads=(w1e + xfe) if last else [w1e[k], xfe[k]],
                         writes=[bk], track=last)
            h_mm(0)
            for m in range(32):
                bk = pb[2 + (m % 2)]
                P.act(hstore[m].t, bk.t[:, 0:TSE], AF.Silu,
                      bias=g_b1_col(C, m, e),
                      reads=[bk, C["v_l1f_b1"]], writes=[hstore[m]])
                if m + 1 < 32:
                    h_mm(m + 1)
            for half in range(2):
                for m in range(32):
                    for mp4 in range(4):
                        mp = half * 4 + mp4
                        ob = pb[4 + mp4]
                        last = (m == 31)
                        full = last and (mp4 == 3)
                        P.mm(ob.t[:, 0:TSE],
                             w2e[m].t[:, mp * 128:(mp + 1) * 128],
                             hstore[m].t, start=(m == 0), stop=last,
                             reads=(w2e + hstore) if full
                             else [w2e[m], hstore[m]],
                             writes=[ob], track=last)
                for mp4 in range(4):
                    mp = half * 4 + mp4
                    ob = pb[4 + mp4]
                    ov = oev[mp4 % 2]
                    P.tt(ov.t, ob.t[:, 0:TSE], wcb5.t, OP.mult,
                         reads=[ob, wcb5], writes=[ov])
                    P.dma("scalar",
                          oparts[e, mp * 128:(mp + 1) * 128, t0:t0 + TSE],
                          ov.t, reads=[ov])
    A.pop()
    A.pop()
    P.barrier()

    if PHASE_LIMIT <= 10:
        return nc, P, dict(locals())
    A.push()
    TSR = 512
    accp = [A.alloc(TSR, name=f"accp{j}") for j in range(8)]
    x3r = A.alloc(TSR, name="x3r")
    b2l = A.alloc(1024, name="b2l")
    ap_b2 = b2l.t[0:8, 0:1024]
    P.dma("sync", ap_b2, g["w_l1f_b2"], writes=[b2l])
    wer = A.alloc(TSR, name="wer")
    ap_wer = wer.t[0:8, :]
    for sr in range(T // TSR):
        t0 = sr * TSR
        P.dma("gpsimd", ap_wer, we_sc[:, t0:t0 + TSR], writes=[wer])
        for mp in range(8):
            bk = pb[mp % 2]
            P.mm(bk.t[:, 0:TSR], ap_b2[:, mp * 128:(mp + 1) * 128], ap_wer,
                 start=True, stop=True, reads=[b2l, wer], writes=[bk])
            P.dma("sync", x3r.t, x3_fm[mp * 128:(mp + 1) * 128, t0:t0 + TSR],
                  writes=[x3r])
            for e in range(8):
                P.dma("gpsimd", accp[e].t,
                      oparts[e, mp * 128:(mp + 1) * 128, t0:t0 + TSR],
                      writes=[accp[e]])
            for e in range(1, 8):
                P.tt(accp[0].t, accp[0].t, accp[e].t, OP.add,
                     reads=[accp[0], accp[e]], writes=[accp[0]])
            P.tt(x3r.t, x3r.t, accp[0].t, OP.add, reads=[x3r, accp[0]],
                 writes=[x3r])
            P.tt(x3r.t, x3r.t, bk.t[:, 0:TSR], OP.add, reads=[x3r, bk],
                 writes=[x3r])
            P.dma("scalar", out_fm[mp * 128:(mp + 1) * 128, t0:t0 + TSR],
                  x3r.t, reads=[x3r])
    A.pop()
    P.barrier()
    return nc, P, dict(locals())


def g_b1_col(C, m, e):
    return C["v_l1f_b1"].t[:, m * 8 + e:m * 8 + e + 1]


def finish_program(nc, P, env):
    with nc.Block() as block:
        P.finalize(block)
    return nc


_CACHE = {}


def kernel(**inputs):
    core_map = prep_inputs(inputs)
    key = "prog"
    if key not in _CACHE:
        nc, P, env = build_program()
        finish_program(nc, P, env)
        _CACHE[key] = (nc, env)
    nc, env = _CACHE[key]
    in_maps = [core_map(b) for b in range(NCORES)]
    res = run_bass_kernel_spmd(nc, in_maps, core_ids=list(range(NCORES)))
    out = np.stack([np.ascontiguousarray(np.asarray(r["out_fm"]).T)
                    for r in res.results])
    return out.astype(np.float32)
